# version 03
# baseline (speedup 1.0000x reference)
import sys
sys.path.insert(0, '/opt/trn_rl_repo')
import contextlib
import numpy as np
import ml_dtypes
import concourse.bass as bass
import concourse.mybir as mybir
from concourse.bass_utils import run_bass_kernel_spmd

F32 = mybir.dt.float32
BF16 = mybir.dt.bfloat16
AF = mybir.ActivationFunctionType
ALU = mybir.AluOpType

B, S, D, H, DH, I = 4, 4096, 1024, 16, 64, 2816
NC = 8
TH = S // 2
CH = 128
NCH = S // CH
STW = 512
NST = S // STW
NITI = (I + 511) // 512
EPS_ATT = 1e-8
EPS_LN = 1e-5
THR = [(0.25, 0.5), (0.75, 0.5), (1.25, 0.5), (1.75, 0.5),
       (2.5, 1.0), (3.5, 1.0), (5.0, 2.0)]


class Plan:

    def __init__(self, barrier=False):
        self.ops = []
        self.lands = {}
        self.barrier = barrier

    def op(self, eng, fn=None, incs=(), waits=(), defs=()):
        if self.barrier:
            waits = tuple(waits) + tuple(
                ("ALL", s) for s in ("DM", "TP", "V", "SC", "G", "T"))
        self.ops.append((eng, fn, tuple(incs), tuple(waits), tuple(defs)))

    def resolve(self, sems):
        counts = {k: 0 for k in sems}
        pre = []
        for eng, fn, incs, waits, defs in self.ops:
            pre.append(dict(counts))
            for sname, n in incs:
                counts[sname] += n
            for dname in defs:
                assert incs, f"landmark {dname} on op with no incs"
                self.lands[dname] = (incs[-1][0], counts[incs[-1][0]])
        self.resolved = []
        for i, (eng, fn, incs, waits, defs) in enumerate(self.ops):
            out = []
            for w in waits:
                if isinstance(w, tuple) and w[0] == "ALL":
                    if pre[i][w[1]] > 0:
                        out.append((w[1], pre[i][w[1]]))
                else:
                    assert w in self.lands, f"wait on undefined landmark {w}"
                    out.append(self.lands[w])
            self.resolved.append(out)

    def emit(self, eng_name, handle, sems, flush=None):
        for i, (eng, fn, incs, waits, defs) in enumerate(self.ops):
            if eng != eng_name:
                continue
            byname = {}
            for sname, val in self.resolved[i]:
                byname[sname] = max(byname.get(sname, 0), val)
            for sname, val in byname.items():
                handle.wait_ge(sems[sname], val)
            inst = fn(handle) if fn is not None else None
            if incs:
                if flush is not None and fn is not None:
                    inst = flush(handle)
                elif inst is None:
                    inst = handle.engine_nop()
                for sname, n in incs:
                    inst = inst.then_inc(sems[sname], n)


def ceil_div(a, b):
    return (a + b - 1) // b


def _emit_all(pl, block, sem, nc, dmy):
    ALU_ = mybir.AluOpType

    @block.sync
    def _(sync):
        pl.emit("sync", sync, sem)

    @block.vector
    def _(vector):
        pl.emit("vector", vector, sem,
                flush=lambda e: e.tensor_scalar(
                    dmy[:, 224:256], dmy[:, 128:160], 1.0, None, ALU_.mult))

    @block.scalar
    def _(scalar):
        pl.emit("scalar", scalar, sem,
                flush=lambda e: e.copy(dmy[:, 160:176], dmy[:, 128:144]))

    @block.gpsimd
    def _(gpsimd):
        pl.emit("gpsimd", gpsimd, sem,
                flush=lambda e: e.tensor_copy(dmy[:, 192:208], dmy[:, 144:160]))

    @block.tensor
    def _(tensor):
        pl.emit("tensor", tensor, sem, flush=(pl.tensor_flush if pl.use_tflush else None))
    return nc


def build_program(phases='ABCD', dsub=9, dbg=False, csub=9):
    nc = bass.Bass("TRN2", target_bir_lowering=False,
                   detect_race_conditions=False)

    xs_tm = nc.dram_tensor("xs_tm", [S, D], F32, kind="ExternalInput").ap()
    xs_fmh = nc.dram_tensor("xs_fmh", [D, TH], F32, kind="ExternalInput").ap()
    kmask = nc.dram_tensor("kmask", [CH, NCH], F32, kind="ExternalInput").ap()
    wqt = nc.dram_tensor("wqt", [D, D], F32, kind="ExternalInput").ap()
    wkt = nc.dram_tensor("wkt", [D, D], F32, kind="ExternalInput").ap()
    wvt = nc.dram_tensor("wvt", [D, D], F32, kind="ExternalInput").ap()
    wot = nc.dram_tensor("wot", [D, D], F32, kind="ExternalInput").ap()
    wg = nc.dram_tensor("wg", [I, D], F32, kind="ExternalInput").ap()
    wu = nc.dram_tensor("wu", [I, D], F32, kind="ExternalInput").ap()
    wd = nc.dram_tensor("wd", [D, I], F32, kind="ExternalInput").ap()
    out_fm = nc.dram_tensor("out_fm", [D, TH], F32, kind="ExternalOutput").ap()
    sc_xfm = nc.dram_tensor("sc_xfm", [D, S], BF16, kind="Internal").ap()
    sc_wgt = nc.dram_tensor("sc_wgt", [D, I], BF16, kind="Internal").ap()
    sc_wut = nc.dram_tensor("sc_wut", [D, I], BF16, kind="Internal").ap()
    sc_wdt = nc.dram_tensor("sc_wdt", [I, D], BF16, kind="Internal").ap()
    sc_axr = nc.dram_tensor("sc_axr", [D, TH], F32, kind="Internal").ap()
    sc_hqf = nc.dram_tensor("sc_hqf", [I, TH], BF16, kind="Internal").ap()
    dbg_out = {}
    if dbg:
        dbg_out["d_xfm"] = nc.dram_tensor("d_xfm", [D, S], BF16, kind="ExternalOutput").ap()
        dbg_out["d_axr"] = nc.dram_tensor("d_axr", [D, TH], F32, kind="ExternalOutput").ap()
        dbg_out["d_wgt"] = nc.dram_tensor("d_wgt", [D, I], BF16, kind="ExternalOutput").ap()
        dbg_out["d_hqf"] = nc.dram_tensor("d_hqf", [I, TH], BF16, kind="ExternalOutput").ap()
        dbg_out["d_xq"] = nc.dram_tensor("d_xq", [128, 8 * TH], BF16, kind="ExternalOutput").ap()
        dbg_out["d_diag"] = nc.dram_tensor("d_diag", [128, 512], F32, kind="ExternalOutput").ap()
        for _i in range(8):
            dbg_out[f"d_s{_i}"] = nc.dram_tensor(f"d_s{_i}", [128, 512], F32, kind="ExternalOutput").ap()
        dbg_out["d_wut"] = nc.dram_tensor("d_wut", [D, I], BF16, kind="ExternalOutput").ap()
        dbg_out["d_wdt"] = nc.dram_tensor("d_wdt", [I, D], BF16, kind="ExternalOutput").ap()
        dbg_out["d_hq2"] = nc.dram_tensor("d_hq2", [128, 22 * TH], BF16, kind="ExternalOutput").ap()
        dbg_out["d_sil"] = nc.dram_tensor("d_sil", [128, 512], F32, kind="ExternalOutput").ap()
        dbg_out["d_hbb"] = nc.dram_tensor("d_hbb", [128, 512], F32, kind="ExternalOutput").ap()
        dbg_out["d_hqt"] = nc.dram_tensor("d_hqt", [128, 512], BF16, kind="ExternalOutput").ap()
        dbg_out["d_wgs"] = nc.dram_tensor("d_wgs", [128, 8 * 512], BF16, kind="ExternalOutput").ap()

    import os as _os
    pl = Plan(barrier=_os.environ.get("NO_BARRIER", "") == "")
    SEMS = ["DM", "TP", "V", "SC", "G", "T"]
    uid = [0]

    def land(pfx):
        uid[0] += 1
        return f"{pfx}{uid[0]}"

    with contextlib.ExitStack() as stk:
        E = stk.enter_context
        sem = {name: E(nc.semaphore(name)) for name in SEMS}

        xqfm = E(nc.sbuf_tensor("xqfm", [128, 8 * TH], BF16,
                                is_transpose_dest=True))
        QW = 1024
        q_u = E(nc.sbuf_tensor("q_u", [CH, QW], F32))
        q_zs = E(nc.sbuf_tensor("q_zs", [CH, QW], BF16))
        q_acc = E(nc.sbuf_tensor("q_acc", [CH, QW], BF16))
        q_t1 = E(nc.sbuf_tensor("q_t1", [CH, QW], BF16))
        q_t2 = E(nc.sbuf_tensor("q_t2", [CH, QW], BF16))
        q_out = E(nc.sbuf_tensor("q_out", [CH, QW], BF16))
        q_bm = E(nc.sbuf_tensor("q_bm", [CH, QW // 16], F32))
        q_s = E(nc.sbuf_tensor("q_s", [CH, QW // 16], F32))
        q_rs = E(nc.sbuf_tensor("q_rs", [CH, QW // 16], F32))
        sumc = E(nc.sbuf_tensor("sumc", [CH, 12], F32))
        mrc = E(nc.sbuf_tensor("mrc", [CH, 8], F32))
        kmv = E(nc.sbuf_tensor("kmv", [CH, NCH], F32))
        onescol = E(nc.sbuf_tensor("onescol", [128, 1], BF16))
        iocol = E(nc.sbuf_tensor("iocol", [128, 1], F32))
        iof = E(nc.sbuf_tensor("iof", [128, 128], F32))
        ltm4 = E(nc.sbuf_tensor("ltm4", [128, 512], BF16))
        dencol = E(nc.sbuf_tensor("dencol", [128, 16], F32))
        dmy = E(nc.sbuf_tensor("dmy", [128, 256], F32))
        ksum_acc = E(nc.sbuf_tensor("ksum_acc", [128, 8], F32))
        denr = E(nc.sbuf_tensor("denr", [128, 16], F32))
        g_f32 = E(nc.sbuf_tensor("g_f32", [128, 7104], F32))
        g_bf = E(nc.sbuf_tensor("g_bf", [128, 66624], BF16,
                                is_transpose_dest=True))

        class Carver:
            def __init__(self, base_ap):
                self.ap = base_ap
                self.off = 0

            def take(self, cols):
                cols_al = (cols + 63) & ~63
                a = self.ap[:, self.off:self.off + cols]
                self.off += cols_al
                assert self.off <= self.ap.shape[1], "arena overflow"
                return a

        cf = Carver(g_f32[:])
        cb = Carver(g_bf[:])
        xbuf = [cf.take(D) for _ in range(2)]
        trash = cf.take(D)
        xn_tm = cf.take(D)
        wbuf = [cf.take(1024) for _ in range(2)]
        xt_tm = cb.take(D)
        xt_fm = cb.take(D)
        wq_tm = cb.take(1024)
        wq_fm = cb.take(1024)
        cf = Carver(g_f32[:])
        cb = Carver(g_bf[:])
        sil = cf.take(512)
        hbb = cf.take(512)
        axr = cf.take(512)
        aout = cf.take(512)
        dbg_p = cf.take(512)
        hqp = cb.take(4 * 128)
        wgs = cb.take(8 * 512)
        wus = cb.take(8 * 512)
        hq_tm = cb.take(512)
        hqfm = cb.take(22 * TH)
        wds = cb.take(22 * 128)
        cf = Carver(g_f32[:])
        cb = Carver(g_bf[:])
        wfb = [cf.take(D) for _ in range(2)]
        elu1 = cf.take(512)
        elu2 = cf.take(512)
        axr_d = cf.take(512)
        aout_d = cf.take(512)
        kfm = cb.take(8 * STW)
        omfm = cb.take(8 * STW)
        xfmst = [cb.take(8 * STW) for _ in range(2)]
        wab = {w: cb.take(8 * D) for w in ("q", "k", "v", "o")}
        kmtm = cb.take(4 * D)
        vtm = cb.take(4 * D)
        qfm = cb.take(8 * STW)
        saug = cb.take(8 * 65)
        atm = cb.take(512)
        omtm = cb.take(4 * D)
        p_S = E(nc.psum_tensor("p_S", [128, 512], F32))
        p_misc = E(nc.psum_tensor("p_misc", [128, 512], F32))
        p_ks = p_misc[:, 0:8]
        p_dn = p_misc[:, 16:32]
        p_fl = p_misc[0:1, 256:257]
        p_at = E(nc.psum_tensor("p_at", [128, 512], F32))
        p_n0 = E(nc.psum_tensor("p_n0", [128, 512], F32))
        p_n1 = E(nc.psum_tensor("p_n1", [128, 512], F32))
        p_w0 = E(nc.psum_tensor("p_w0", [128, 512], F32))
        p_w1 = E(nc.psum_tensor("p_w1", [128, 512], F32))

        block = E(nc.Block())

        dmy_i = [0]

        def vdum(e):
            i = dmy_i[0] % 4
            dmy_i[0] += 1
            e.tensor_scalar(dmy[:, 32 * i:32 * i + 32], dmy[:, 128:160],
                            1.0, None, ALU.mult)

        def _tflush(e):
            return e.matmul(p_fl, onescol[:], onescol[:], start=True,
                            stop=True, skip_group_check=True)
        pl.tensor_flush = _tflush
        import os as _os
        pl.use_tflush = _os.environ.get("NO_TFLUSH", "") == ""

        def dma(src_ap, dst_ap, waits=(), defs=()):
            pl.op("sync", lambda e, s=src_ap, d=dst_ap: e.dma_start(out=d, in_=s),
                  incs=[("DM", 16)], waits=tuple(waits) + (("ALL", "TP"),),
                  defs=defs)

        def dmatp(src_ap, dst_ap, waits=(), defs=()):
            pl.op("sync", lambda e, s=src_ap, d=dst_ap: e.dma_start_transpose(d, s),
                  incs=[("TP", 16)],
                  waits=tuple(waits) + (("ALL", "DM"), ("ALL", "TP")), defs=defs)

        def vop(fn, waits=(), defs=()):
            pl.op("vector", fn, incs=[("V", 1)], waits=waits, defs=defs)

        def sop(fn, waits=(), defs=()):
            if not first_eng["scalar"]:
                first_eng["scalar"] = True
                waits = tuple(waits) + (l_dmy,)
            pl.op("scalar", fn, incs=[("SC", 1)], waits=waits, defs=defs)

        def gop(fn, waits=(), defs=()):
            if not first_eng["gpsimd"]:
                first_eng["gpsimd"] = True
                waits = tuple(waits) + (l_dmy,)
            pl.op("gpsimd", fn, incs=[("G", 1)], waits=waits, defs=defs)

        def top(fn, waits=(), defs=()):
            pl.op("tensor", fn, incs=[("T", 1)], waits=waits, defs=defs)

        te_state = {"base": None}

        def tmms(mmlist, waits=(), defs=()):
            def go(e):
                last = None
                for item in mmlist:
                    if isinstance(item, tuple):
                        base, f = item
                        if te_state["base"] is not None and \
                                te_state["base"] != base:
                            e.ldweights(ltm4[:, 0:128])
                        te_state["base"] = base
                        last = f(e)
                    else:
                        te_state["base"] = None
                        last = item(e)
                return last
            pl.op("tensor", go, incs=[("T", 1)], waits=waits, defs=defs)

        q_state = {"zs": None, "dst": None}

        def emit_quant2(src_fn, w, out_dst_fn, waits=(), dst_free=None):
            nb = w // 16
            src = src_fn()
            outd = out_dst_fn()
            def mk_scales(e):
                e.tensor_reduce(
                    q_bm[:, 0:nb], src.rearrange("p (b k) -> p b k", k=16),
                    axis=mybir.AxisListType.X, op=ALU.max,
                    apply_absolute_value=True)
                vdum(e)
                e.tensor_scalar(q_s[:, 0:nb], q_bm[:, 0:nb],
                                6e-12, 1.0 / 6.0, ALU.max, ALU.mult)
                vdum(e)
                e.reciprocal(q_rs[:, 0:nb], q_s[:, 0:nb])
                vdum(e)
                return e.tensor_scalar(dmy[:, 0:32], dmy[:, 128:160], 1.0,
                                       None, ALU.mult)
            vop(mk_scales, waits=waits)
            rs_b = q_rs[:, 0:nb].unsqueeze(2).broadcast_to([128, nb, 16])
            s_b = q_s[:, 0:nb].unsqueeze(2).broadcast_to([128, nb, 16])
            l_u = land("qu")
            uw = [q_state["zs"]] if q_state["zs"] else []
            vop(lambda e: e.tensor_tensor(q_u[:, 0:w], src, rs_b, ALU.mult),
                waits=uw, defs=[l_u])
            l_zs = land("qzs")
            sop(lambda e: e.activation(q_zs[:, 0:w], q_u[:, 0:w], AF.Sign),
                waits=[l_u])
            sop(lambda e: e.activation(q_u[:, 0:w], q_u[:, 0:w], AF.Abs),
                defs=[l_zs])
            q_state["zs"] = l_zs
            t0, w0 = THR[0]
            t1x, w1x = THR[1]
            vop(lambda e: e.tensor_scalar(
                q_t1[:, 0:w], q_u[:, 0:w], t0, w0, ALU.is_gt, ALU.mult),
                waits=[l_zs])
            vop(lambda e: e.tensor_scalar(
                q_t2[:, 0:w], q_u[:, 0:w], t1x, w1x, ALU.is_gt, ALU.mult))
            vop(lambda e: e.tensor_tensor(q_acc[:, 0:w], q_t1[:, 0:w],
                                          q_t2[:, 0:w], ALU.add))
            for tj, wj in THR[2:]:
                vop(lambda e, tj=tj, wj=wj: e.tensor_scalar(
                    q_t1[:, 0:w], q_u[:, 0:w], tj, wj, ALU.is_gt, ALU.mult))
                vop(lambda e: e.tensor_tensor(q_acc[:, 0:w], q_acc[:, 0:w],
                                              q_t1[:, 0:w], ALU.add))
            vop(lambda e: e.tensor_tensor(q_t2[:, 0:w], q_acc[:, 0:w],
                                          q_zs[:, 0:w], ALU.mult))
            l_out = land("qo")
            ow = [dst_free] if dst_free else []
            vop(lambda e: e.tensor_tensor(outd, q_t2[:, 0:w], s_b, ALU.mult),
                waits=ow, defs=[l_out])
            return l_out

        l_dmy = land("dmy0")
        vop(lambda e: e.memset(dmy[:], 0.0), defs=[l_dmy])
        first_eng = {"scalar": False, "gpsimd": False}
        l_km = land("km")
        dma(kmask, kmv[:, 0:NCH],
            defs=[l_km])
        l_iota = land("iota")
        gop(lambda e: e.iota(iocol[:], pattern=[[0, 1]], base=0,
                             channel_multiplier=1,
                             allow_small_or_imprecise_dtypes=True))
        gop(lambda e: e.iota(iof[:], pattern=[[1, 128]], base=0,
                             channel_multiplier=0,
                             allow_small_or_imprecise_dtypes=True),
            defs=[l_iota])
        l_ones = land("ones")
        vop(lambda e: e.memset(onescol[:], 1.0), defs=[l_ones])
        l_ltm = land("ltm")
        def mk_ltm(e):
            last = None
            for r in range(4):
                last = e.tensor_scalar(ltm4[:, r * 128:(r + 1) * 128], iof[:],
                                       iocol[:], None, ALU.is_ge)
            return last
        vop(mk_ltm, waits=[l_iota], defs=[l_ltm])

        x_done_lm = [None] * NCH
        last_mr = [None]
        last_xtp = [None]
        last_xq_tp = [None]
        for c in range(NCH):
            buf = xbuf[c % 2]
            wl = [] if c < 2 else [x_done_lm[c - 2]]
            l_ld = land("xa")
            dma(xs_tm[c * CH:(c + 1) * CH, :], buf[:], waits=wl, defs=[l_ld])
            def mk_mr(e, b=buf):
                e.bn_stats(sumc[:, 0:6], b[:, 0:512])
                e.bn_stats(sumc[:, 6:12], b[:, 512:1024])
                e.bn_aggr(mrc[:, 4:6], sumc[:, 0:6])
                e.bn_aggr(mrc[:, 6:8], sumc[:, 6:12])
                vdum(e)
                e.tensor_tensor(mrc[:, 0:1], mrc[:, 4:5], mrc[:, 6:7], ALU.add)
                e.tensor_tensor(mrc[:, 3:4], mrc[:, 4:5], mrc[:, 6:7], ALU.subtract)
                e.tensor_tensor(mrc[:, 1:2], mrc[:, 5:6], mrc[:, 7:8], ALU.add)
                e.tensor_scalar(mrc[:, 0:1], mrc[:, 0:1], -0.5, None, ALU.mult)
                e.tensor_tensor(mrc[:, 3:4], mrc[:, 3:4], mrc[:, 3:4], ALU.mult)
                e.tensor_scalar(mrc[:, 1:2], mrc[:, 1:2], 0.5, None, ALU.mult)
                e.tensor_scalar(mrc[:, 3:4], mrc[:, 3:4], 0.25, EPS_LN,
                                ALU.mult, ALU.add)
                vdum(e)
                e.tensor_tensor(mrc[:, 3:4], mrc[:, 1:2], mrc[:, 3:4], ALU.add)
                vdum(e)
                return e.reciprocal(mrc[:, 1:2], mrc[:, 3:4])
            l_mr = land("lm")
            vop(mk_mr, waits=[l_ld] + ([last_mr[0]] if last_mr[0] else []),
                defs=[l_mr])
            last_mr[0] = l_mr
            l_r = land("r")
            sop(lambda e: e.activation(mrc[:, 1:2], mrc[:, 1:2], AF.Sqrt),
                waits=[l_mr], defs=[l_r])
            l_xt = land("xt")

            def mk_xt(e, b=buf):
                e.tensor_tensor(mrc[:, 2:3], mrc[:, 0:1], mrc[:, 1:2], ALU.mult)
                vdum(e)
                return e.tensor_scalar(xt_tm[:], b[:], mrc[:, 1:2], mrc[:, 2:3],
                                       ALU.mult, ALU.add)
            xtw = [l_r] + ([last_xtp[0]] if last_xtp[0] else [])
            vop(mk_xt, waits=xtw, defs=[l_xt])
            l_tp = land("tp")
            dmatp(xt_tm[:], xt_fm[:].rearrange("p (dc t) -> p dc t", t=CH),
                  waits=[l_xt], defs=[l_tp])
            last_xtp[0] = l_tp
            l_xf = land("xf")
            dma(xt_fm[:].rearrange("p (dc t) -> p dc t", t=CH),
                sc_xfm.rearrange("(dc p) s -> p dc s", p=128)
                [:, :, c * CH:(c + 1) * CH],
                waits=[l_tp], defs=[l_xf])
            if c >= NCH // 2:
                tc = c - NCH // 2
                l_xn = land("xn")

                def mk_xn(e, b=buf):
                    return e.tensor_scalar(xn_tm[:], b[:], mrc[:, 1:2],
                                           mrc[:, 2:3], ALU.mult, ALU.add)
                vop(mk_xn, waits=[l_r], defs=[l_xn])
                l_q = emit_quant2(lambda: xn_tm[:], D,
                                  lambda: q_out[:, 0:D], waits=[l_xn],
                                  dst_free=last_xq_tp[0])
                l_tq = land("tq")
                dmatp(q_out[:, 0:D],
                      xqfm[:].rearrange("p (dc t) -> p dc t", t=TH)
                      [:, :, tc * CH:(tc + 1) * CH],
                      waits=[l_q], defs=[l_tq])
                last_xq_tp[0] = l_tq
                x_done_lm[c] = l_tq
            else:
                x_done_lm[c] = l_xf

        if 'B' not in phases:
            if dbg:
                dma(sc_xfm, dbg_out["d_xfm"], waits=[("ALL", "DM")])
            pl.op("sync", None, incs=[],
                  waits=[("ALL", "DM"), ("ALL", "TP")])
            pl.resolve({k: None for k in SEMS})
            return _emit_all(pl, block, sem, nc, dmy)
        wstate = {"tp": None, "prev": [None, None], "i": 0}

        def wprep(w_in, row0, col0, width, sc_out):
            i = wstate["i"]
            wstate["i"] += 1
            buf = wbuf[i % 2]
            wl = [] if wstate["prev"][i % 2] is None else [wstate["prev"][i % 2]]
            l_ld = land("wl")
            dma(w_in[row0:row0 + CH, col0:col0 + width], buf[:, 0:width],
                waits=wl, defs=[l_ld])
            l_q = emit_quant2(lambda b=buf: b[:, 0:width], width,
                              lambda: wq_tm[:, 0:width], waits=[l_ld],
                              dst_free=wstate["tp"])
            l_tp = land("wt")
            ncc = width // 128
            dmatp(wq_tm[:, 0:width],
                  wq_fm[:].rearrange("p (cc t) -> p cc t", t=CH)[:, 0:ncc, :],
                  waits=[l_q], defs=[l_tp])
            wstate["tp"] = l_tp
            l_st = land("ws")
            dma(wq_fm[:].rearrange("p (cc t) -> p cc t", t=CH)[:, 0:ncc, :],
                sc_out.rearrange("(cc p) n -> p cc n", p=128)
                [:, col0 // 128:col0 // 128 + ncc, row0:row0 + CH],
                waits=[l_tp], defs=[l_st])
            wstate["prev"][i % 2] = l_st
            return l_st

        for c in range(I // CH):
            wprep(wg, c * CH, 0, D, sc_wgt)
        for c in range(I // CH):
            wprep(wu, c * CH, 0, D, sc_wut)
        for c in range(D // CH):
            wprep(wd, c * CH, 0, 1024, sc_wdt)
            wprep(wd, c * CH, 1024, 1024, sc_wdt)
            l_wdone = wprep(wd, c * CH, 2048, 768, sc_wdt)

        if 'C' not in phases:
            if dbg:
                dma(sc_xfm, dbg_out["d_xfm"], waits=[("ALL", "DM")])
                dma(sc_wgt, dbg_out["d_wgt"])
            pl.op("sync", None, incs=[],
                  waits=[("ALL", "DM"), ("ALL", "TP")])
            pl.resolve({k: None for k in SEMS})
            return _emit_all(pl, block, sem, nc, dmy)
        hq_done = []
        hq_cp = [None]
        hq_st = [None]
        last_wg_lm = None
        _ctiles = int(_os.environ.get('CTILES', '999'))
        if dbg:
            l_dg0 = land("dg0")
            dma(sc_wgt.rearrange("(dc p) n -> p dc n", p=128)[:, :, 0:512],
                wgs[:].rearrange("p (dc n) -> p dc n", n=512),
                waits=[l_wdone], defs=[l_dg0])
            l_dgm = land("dgm")
            xsl0 = xqfm[:].rearrange("p (dc t) -> p dc t", t=TH)
            import os as _os
            _dsel = _os.environ.get('DIAGSEL', '1')
            _dml = []
            if _dsel == '1':
                _dml.append(lambda e: e.matmul(
                    p_w0[:, 0:512], xsl0[:, 0, 0:128],
                    wgs[:].rearrange("p (dc n) -> p dc n", n=512)[:, 0, :],
                    start=True, stop=True))
            else:
                for dc in range(8):
                    _dml.append(lambda e, dc=dc: e.matmul(
                        p_w0[:, 0:512], xsl0[:, dc, 0:128],
                        wgs[:].rearrange("p (dc n) -> p dc n", n=512)[:, dc, :],
                        start=(dc == 0), stop=(dc == 7)))
                if _dsel == '16':
                    for dc in range(8):
                        _dml.append(lambda e, dc=dc: e.matmul(
                            p_w1[:, 0:512], xsl0[:, dc, 0:128],
                            wus[:].rearrange("p (dc n) -> p dc n", n=512)[:, dc, :],
                            start=(dc == 0), stop=(dc == 7)))
            tmms(_dml, waits=[l_dg0], defs=[l_dgm])
            l_dgc = land("dgc")
            sop(lambda e: e.activation(sil[:, 0:512], p_w0[:, 0:512], AF.Copy),
                waits=[l_dgm], defs=[l_dgc])
            dma(sil[:, 0:512], dbg_out["d_diag"], waits=[l_dgc])
        for it in range(NITI):
            isz = min(512, I - it * 512)
            l_wg = land("fg")
            wl = [l_wdone] if it == 0 else [hq_done[-1]]
            dma(sc_wgt.rearrange("(dc p) n -> p dc n", p=128)
                [:, :, it * 512:it * 512 + isz],
                wgs[:].rearrange("p (dc n) -> p dc n", n=512)[:, :, 0:isz],
                waits=wl, defs=[l_wg])
            l_wu = land("fu")
            dma(sc_wut.rearrange("(dc p) n -> p dc n", p=128)
                [:, :, it * 512:it * 512 + isz],
                wus[:].rearrange("p (dc n) -> p dc n", n=512)[:, :, 0:isz],
                defs=[l_wu])
            last_wg_lm = l_wu
            for tc in range(TH // CH):
                if it * 16 + tc >= _ctiles:
                    continue
                xsl = xqfm[:].rearrange("p (dc t) -> p dc t", t=TH)
                l_mm = land("fm")
                gml = []
                for dc in range(8):
                    gml.append(lambda e, dc=dc, tc=tc, isz=isz, xsl=xsl: e.matmul(
                        p_w0[:, 0:isz],
                        xsl[:, dc, tc * CH:(tc + 1) * CH],
                        wgs[:].rearrange("p (dc n) -> p dc n", n=512)[:, dc, 0:isz],
                        start=(dc == 0), stop=(dc == 7)))
                for dc in range(8):
                    gml.append(lambda e, dc=dc, tc=tc, isz=isz, xsl=xsl: e.matmul(
                        p_w1[:, 0:isz],
                        xsl[:, dc, tc * CH:(tc + 1) * CH],
                        wus[:].rearrange("p (dc n) -> p dc n", n=512)[:, dc, 0:isz],
                        start=(dc == 0), stop=(dc == 7)))
                mmw = [l_wg, l_wu] if tc == 0 else [l_wg, l_wu, hq_done[-1]]
                tmms(gml, waits=mmw, defs=[l_mm])
                l_sil = land("fs")
                sop(lambda e, isz=isz: e.activation(sil[:, 0:isz], p_w0[:, 0:isz],
                                                    AF.Silu),
                    waits=[l_mm], defs=[l_sil])
                if csub < 2:
                    hq_done.append(l_sil)
                    continue
                l_h = land("fh")
                vop(lambda e, isz=isz: e.tensor_tensor(
                    hbb[:, 0:isz], sil[:, 0:isz], p_w1[:, 0:isz], ALU.mult),
                    waits=[l_sil], defs=[l_h])
                l_q = emit_quant2(lambda isz=isz: hbb[:, 0:isz], isz,
                                  lambda isz=isz: hq_tm[:, 0:isz], waits=[l_h],
                                  dst_free=hq_done[-1] if hq_done else None)
                _slots = {(0, 0): 0, (0, 15): 2, (1, 0): 3,
                          (2, 0): 4, (3, 0): 5, (4, 1): 6, (5, 0): 7}
                if dbg and (it, tc) in _slots:
                    dma(sil[:], dbg_out[f"d_s{_slots[(it, tc)]}"],
                        waits=[l_sil])
                if dbg and (it, tc) == (0, 0):
                    l_pd = land("pd")
                    vop(lambda e: e.tensor_copy(aout[:], p_w0[:, 0:512]),
                        waits=[l_sil], defs=[l_pd])
                    dma(aout[:], dbg_out["d_s1"], waits=[l_pd])
                if dbg and it == 4 and tc == 0:
                    dma(sil[:], dbg_out["d_sil"], waits=[l_sil])
                    dma(hbb[:], dbg_out["d_hbb"], waits=[l_h])
                    dma(hq_tm[:], dbg_out["d_hqt"], waits=[l_q])
                    dma(wgs[:], dbg_out["d_wgs"], waits=[l_mm])
                l_tp2 = land("ftp")
                ncc = isz // 128
                tpw = [l_q] + ([hq_cp[0]] if hq_cp[0] else [])
                dmatp(hq_tm[:, 0:isz],
                      hqp.rearrange("p (cc t) -> p cc t", t=CH)[:, 0:ncc, :],
                      waits=tpw, defs=[l_tp2])
                l_tq = land("ft")
                vop(lambda e, it=it, tc=tc, ncc=ncc: e.tensor_copy(
                    hqfm.rearrange("p (cc t) -> p cc t", t=TH)
                    [:, it * 4:it * 4 + ncc, tc * CH:(tc + 1) * CH],
                    hqp.rearrange("p (cc t) -> p cc t", t=CH)[:, 0:ncc, :]),
                    waits=[l_tp2], defs=[l_tq])
                hq_cp[0] = l_tq
                hq_done.append(l_tq)
        if csub < 2 or _ctiles < 96:
            if dbg:
                dma(xqfm[:], dbg_out["d_xq"], waits=[("ALL", "DM"), ("ALL", "TP")])
            pl.op("sync", None, incs=[], waits=[("ALL", "DM"), ("ALL", "TP")])
            pl.resolve({k: None for k in SEMS})
            return _emit_all(pl, block, sem, nc, dmy)
        axr_done = []
        ffn_fe = [None]
        l_hq_all = hq_done[-1]
        last_wd = [None]
        for oc in range(8):
            l_wdl = land("dwl")
            wlw = [last_wd[0]] if last_wd[0] else [l_hq_all]
            dma(sc_wdt.rearrange("(ic p) o -> p ic o", p=128)
                [:, :, oc * 128:(oc + 1) * 128],
                wds.rearrange("p (ic o) -> p ic o", o=128),
                waits=wlw, defs=[l_wdl])
            for ts4 in range(TH // 512):
                l_mm = land("dm")
                mml = []
                for ic in range(22):
                    mml.append(lambda e, ic=ic, oc=oc, ts4=ts4: e.matmul(
                        p_w0[:, 0:512],
                        wds.rearrange("p (ic o) -> p ic o", o=128)[:, ic, :],
                        hqfm.rearrange("p (cc t) -> p cc t", t=TH)
                        [:, ic, ts4 * 512:(ts4 + 1) * 512],
                        start=(ic == 0), stop=(ic == 21)))
                mmw = [l_wdl]
                if axr_done:
                    mmw.append(axr_done[-1])
                tmms(mml, waits=mmw, defs=[l_mm])
                l_x = land("dx")
                axw = [ffn_fe[0]] if ffn_fe[0] else []
                dma(xs_fmh[oc * 128:(oc + 1) * 128, ts4 * 512:(ts4 + 1) * 512],
                    axr[:], waits=axw, defs=[l_x])
                l_e = land("de")
                few = [l_mm, l_x] + ([axr_done[-1]] if axr_done else [])
                vop(lambda e: e.tensor_tensor(aout[:], p_w0[:, 0:512], axr[:],
                                              ALU.add),
                    waits=few, defs=[l_e])
                ffn_fe[0] = l_e
                l_s = land("ds")
                dma(aout[:], sc_axr[oc * 128:(oc + 1) * 128,
                                    ts4 * 512:(ts4 + 1) * 512],
                    waits=[l_e], defs=[l_s])
                axr_done.append(l_s)
            last_wd[0] = l_mm
        l_axr_all = axr_done[-1]

        if 'D' not in phases:
            if dbg:
                dma(sc_xfm, dbg_out["d_xfm"], waits=[("ALL", "DM")])
                dma(sc_axr, dbg_out["d_axr"])
                dma(sc_wgt, dbg_out["d_wgt"])
                dma(xqfm[:], dbg_out["d_xq"])
                dma(hqfm, dbg_out["d_hq2"])
            pl.op("sync", None, incs=[],
                  waits=[("ALL", "DM"), ("ALL", "TP")])
            pl.resolve({k: None for k in SEMS})
            return _emit_all(pl, block, sem, nc, dmy)
        wsrc = {"q": wqt, "k": wkt, "v": wvt, "o": wot}
        l_wcast = {}
        prev = [l_axr_all, l_axr_all]
        for wi, wname in enumerate(("q", "k", "v", "o")):
            for dc in range(8):
                idx = (wi * 8 + dc) % 2
                buf = wfb[idx]
                wl = [prev[idx]] if prev[idx] else []
                l_ld = land("awl")
                dma(wsrc[wname][dc * 128:(dc + 1) * 128, :], buf[:],
                    waits=wl, defs=[l_ld])
                l_c = land("awc")
                vop(lambda e, b=buf, wname=wname, dc=dc: e.tensor_copy(
                    wab[wname][:, dc * D:(dc + 1) * D], b[:]),
                    waits=[l_ld], defs=[l_c])
                prev[idx] = l_c
            l_wcast[wname] = prev[(wi * 8 + 7) % 2]

        if dsub < 1.5:
            pl.op("sync", None, incs=[], waits=[("ALL", "DM"), ("ALL", "TP")])
            pl.resolve({k: None for k in SEMS})
            return _emit_all(pl, block, sem, nc, dmy)
        l_ksa0 = land("ks0")
        vop(lambda e: e.memset(ksum_acc[:], 0.0), defs=[l_ksa0])
        last_ksa = [l_ksa0]
        st_lm = [None, None]
        out_done = []
        last_supd = [None]
        prev_om = [None]
        prev_kf = [None]
        prev_wo = [None]
        last_fe = [None]
        for st in range(NST):
            full = st >= NST // 2
            xb = xfmst[st % 2]
            wl = [st_lm[st % 2]] if st_lm[st % 2] else [l_axr_all]
            l_x = land("sx")
            dma(sc_xfm.rearrange("(dc p) s -> p dc s", p=128)
                [:, :, st * STW:(st + 1) * STW],
                xb[:].rearrange("p (dc t) -> p dc t", t=STW),
                waits=wl, defs=[l_x])
            xsl = xb[:].rearrange("p (dc t) -> p dc t", t=STW)
            kv_lms = []
            for tc in range(4):
                for half in range(2):
                    l_mm = land("kp")
                    mml = []
                    for dc in range(8):
                        mml.append(lambda e, dc=dc, tc=tc, half=half,
                                    xsl=xsl: e.matmul(
                            p_w0[:, 0:512],
                            xsl[:, dc, tc * CH:(tc + 1) * CH],
                            wab["k"][:, dc * D + half * 512: dc * D + (half + 1) * 512],
                            start=(dc == 0), stop=(dc == 7)))
                    for dc in range(8):
                        mml.append(lambda e, dc=dc, tc=tc, half=half,
                                    xsl=xsl: e.matmul(
                            p_w1[:, 0:512],
                            xsl[:, dc, tc * CH:(tc + 1) * CH],
                            wab["v"][:, dc * D + half * 512: dc * D + (half + 1) * 512],
                            start=(dc == 0), stop=(dc == 7)))
                    mmw = [l_x, l_wcast["k"], l_wcast["v"], l_km]
                    if kv_lms:
                        mmw += [kv_lms[-1][0], kv_lms[-1][1]]
                    else:
                        if prev_kf[0]:
                            mmw.append(prev_kf[0])
                        if last_supd[0]:
                            mmw.append(last_supd[0])
                    tmms(mml, waits=mmw, defs=[l_mm])
                    l_e0 = land("ke0")
                    l_e1 = land("ke")
                    vop(lambda e: e.tensor_scalar(elu1[:, 0:512], p_w0[:, 0:512],
                                                  0.0, None, ALU.min),
                        waits=[l_mm], defs=[l_e0])
                    sop(lambda e: e.activation(elu1[:, 0:512], elu1[:, 0:512],
                                               AF.Exp), waits=[l_e0])
                    sop(lambda e: e.activation(elu2[:, 0:512], p_w0[:, 0:512],
                                               AF.Relu), defs=[l_e1])
                    l_k = land("kk")

                    def mk_k(e, tc=tc, half=half, st=st):
                        e.tensor_tensor(elu1[:, 0:512], elu1[:, 0:512],
                                        elu2[:, 0:512], ALU.add)
                        return e.tensor_scalar(
                            kmtm[:, tc * D + half * 512: tc * D + (half + 1) * 512],
                            elu1[:, 0:512],
                            kmv[:, st * 4 + tc: st * 4 + tc + 1], None, ALU.mult)
                    vop(mk_k, waits=[l_e1], defs=[l_k])
                    l_v = land("kv")
                    sop(lambda e, tc=tc, half=half: e.activation(
                        vtm[:, tc * D + half * 512: tc * D + (half + 1) * 512],
                        p_w1[:, 0:512], AF.Copy),
                        waits=[l_mm], defs=[l_v])
                    kv_lms.append((l_v, l_k))
            q_lms = []
            kf_lms = []
            if full and dsub >= 3:
                for oc in range(8):
                    l_mm = land("qp")
                    mml = []
                    for dc in range(8):
                        mml.append(lambda e, dc=dc, oc=oc, xsl=xsl: e.matmul(
                            p_w0[:, 0:512] if oc % 2 == 0 else p_w1[:, 0:512],
                            wab["q"][:, dc * D + oc * 128: dc * D + (oc + 1) * 128],
                            xsl[:, dc, :],
                            start=(dc == 0), stop=(dc == 7)))
                    mmw = [l_x, l_wcast["q"]]
                    if q_lms:
                        mmw.append(q_lms[-1])
                    else:
                        mmw += [kv_lms[-1][0], kv_lms[-1][1]]
                    tmms(mml, waits=mmw, defs=[l_mm])
                    l_e0 = land("qe0")
                    l_e1 = land("qe")
                    psrc = p_w0 if oc % 2 == 0 else p_w1
                    vop(lambda e, p=psrc: e.tensor_scalar(
                        elu1[:, 0:512], p[:, 0:512], 0.0, None, ALU.min),
                        waits=[l_mm], defs=[l_e0])
                    sop(lambda e: e.activation(elu1[:, 0:512], elu1[:, 0:512],
                                               AF.Exp), waits=[l_e0])
                    sop(lambda e, p=psrc: e.activation(
                        elu2[:, 0:512], p[:, 0:512], AF.Relu), defs=[l_e1])
                    l_q = land("qq")
                    vop(lambda e, oc=oc: e.tensor_tensor(
                        qfm[:, oc * STW:(oc + 1) * STW], elu1[:, 0:512],
                        elu2[:, 0:512], ALU.add),
                        waits=[l_e1], defs=[l_q])
                    q_lms.append(l_q)
                for tc in range(4):
                    l_t = land("kt")
                    dmatp(kmtm[:, tc * D:(tc + 1) * D],
                          kfm[:].rearrange("p (oc t) -> p oc t", t=STW)
                          [:, :, tc * CH:(tc + 1) * CH],
                          waits=[kv_lms[tc * 2 + 1][1]], defs=[l_t])
                    kf_lms.append(l_t)
                prev_kf[0] = kf_lms[-1]
            om_lms = []
            for tc in range(4):
                c_glob = st * 4 + tc
                l_sa = land("sa")

                def mk_sa(e):
                    for pr in range(8):
                        e.copy(
                            saug[:].rearrange("p (pr x) -> p pr x", x=65)
                            [:, pr, 0:64],
                            p_S[:, pr * 64:(pr + 1) * 64])
                    return e.copy(
                        saug[:].rearrange("p (pr x) -> p pr x", x=65)
                        [:, :, 64:65], ksum_acc[:].unsqueeze(2))
                sa_w = ([last_supd[0]] if last_supd[0] else []) + [last_ksa[0]]
                if c_glob == 0:
                    sa_w = sa_w + [l_ones]
                sop(mk_sa, waits=sa_w, defs=[l_sa])
                if full and dsub >= 4:
                    omtm_lms = []
                    for hgrp in range(4):
                        l_at = land("at")
                        mml = []
                        for hh in range(4):
                            h = hgrp * 4 + hh
                            pb = 64 * (h % 2)
                            oc = h // 2
                            mml.append((pb, lambda e, h=h, pb=pb, oc=oc,
                                        hh=hh, tc=tc: e.matmul(
                                p_at[:, hh * 128:(hh + 1) * 128],
                                kfm[pb:pb + 64,
                                    oc * STW + tc * CH: oc * STW + (tc + 1) * CH],
                                qfm[pb:pb + 64,
                                    oc * STW + tc * CH: oc * STW + (tc + 1) * CH],
                                start=True, stop=True, tile_position=(pb, 0))))
                        atw = [q_lms[-1], kf_lms[tc], l_ltm]
                        if omtm_lms:
                            atw.append(omtm_lms[-1])
                        else:
                            if prev_om[0]:
                                atw.append(prev_om[0])
                            if prev_wo[0]:
                                atw.append(prev_wo[0])
                        tmms(mml, waits=atw, defs=[l_at])
                        l_am = land("am")
                        if dsub < 4.08:
                            vop(lambda e: e.tensor_copy(atm[:], p_at[:, 0:512]),
                                waits=[l_at], defs=[l_am])
                        else:
                            vop(lambda e: e.tensor_tensor(atm[:], p_at[:, 0:512],
                                                          ltm4[:], ALU.mult),
                                waits=[l_at], defs=[l_am])
                        if dsub < 4.2:
                            omtm_lms.append(l_am)
                            continue
                        l_nx = land("nx")
                        mml2 = []
                        for hh in range(4):
                            h = hgrp * 4 + hh
                            pb = 64 * (h % 2)
                            oc = h // 2
                            pn = p_n0 if h < 8 else p_n1
                            hs = (h % 8)
                            mml2.append(lambda e, h=h, hh=hh, tc=tc, pn=pn, hs=hs:
                                        e.matmul(
                                pn[:, hs * 64:(hs + 1) * 64],
                                atm[:, hh * 128:(hh + 1) * 128],
                                vtm[:, tc * D + h * 64: tc * D + (h + 1) * 64],
                                start=True, stop=False))
                            mml2.append((pb, lambda e, h=h, pb=pb, oc=oc,
                                         tc=tc, pn=pn, hs=hs: e.matmul(
                                pn[:, hs * 64:(hs + 1) * 64],
                                qfm[pb:pb + 64,
                                    oc * STW + tc * CH: oc * STW + (tc + 1) * CH],
                                saug[pb:pb + 64,
                                     (h // 2) * 65:(h // 2) * 65 + 64],
                                start=False, stop=True, tile_position=(pb, 0))))
                            if dsub < 4.5:
                                continue
                            mml2.append(lambda e, h=h, hh=hh: e.matmul(
                                p_dn[:, h:h + 1],
                                atm[:, hh * 128:(hh + 1) * 128],
                                onescol[:],
                                start=True, stop=False,
                                skip_group_check=True))
                            mml2.append((pb, lambda e, h=h, pb=pb, oc=oc,
                                         tc=tc: e.matmul(
                                p_dn[:, h:h + 1],
                                qfm[pb:pb + 64,
                                    oc * STW + tc * CH: oc * STW + (tc + 1) * CH],
                                saug[pb:pb + 64,
                                     (h // 2) * 65 + 64:(h // 2) * 65 + 65],
                                start=False, stop=True, skip_group_check=True,
                                tile_position=(pb, 0))))
                        tmms(mml2, waits=[l_am, l_sa, l_ones], defs=[l_nx])
                        omtm_lms.append(l_nx)
                    if dsub < 4.5:
                        prev_om[0] = omtm_lms[-1]
                        last_supd_skip = True
                    l_dr = land("dr")

                    def mk_dr(e):
                        e.tensor_copy(dencol[:], p_dn[:, 0:16])
                        vdum(e)
                        e.tensor_scalar(dencol[:], dencol[:], EPS_ATT, None,
                                        ALU.add)
                        vdum(e)
                        e.reciprocal(denr[:], dencol[:])
                        vdum(e)
                        return e.tensor_scalar(dmy[:, 0:32], dmy[:, 128:160],
                                               1.0, None, ALU.mult)
                    if dsub < 4.5:
                        l_dr = None
                    else:
                        vop(mk_dr, waits=[omtm_lms[-1]], defs=[l_dr])
                    l_om = land("om")

                    def mk_om(e, tc=tc):
                        pass

                    def mk_om(e, tc=tc):
                        last = None
                        for h in range(16):
                            pn = p_n0 if h < 8 else p_n1
                            hs = h % 8
                            last = e.tensor_scalar(
                                omtm[:, tc * D + h * 64: tc * D + (h + 1) * 64],
                                pn[:, hs * 64:(hs + 1) * 64],
                                denr[:, h:h + 1], None, ALU.mult)
                        return last
                    if dsub >= 4.5:
                        vop(mk_om, waits=[l_dr], defs=[l_om])
                        prev_om[0] = l_om
                        om_lms.append((tc, l_om))
                l_su = land("su")
                mml3 = []
                for h in range(16):
                    pb = 64 * (h % 2)
                    mml3.append(lambda e, h=h, pb=pb, tc=tc, c_glob=c_glob:
                                e.matmul(
                        p_S[pb:pb + 64, (h // 2) * 64:(h // 2) * 64 + 64],
                        kmtm[:, tc * D + h * 64: tc * D + (h + 1) * 64],
                        vtm[:, tc * D + h * 64: tc * D + (h + 1) * 64],
                        start=(c_glob == 0), stop=(c_glob == NCH - 1),
                        tile_position=(0, pb)))
                    mml3.append(lambda e, h=h, pb=pb, tc=tc, c_glob=c_glob:
                                e.matmul(
                        p_ks[pb:pb + 64, (h // 2):(h // 2) + 1],
                        kmtm[:, tc * D + h * 64: tc * D + (h + 1) * 64],
                        onescol[:],
                        start=True, stop=True,
                        skip_group_check=True, tile_position=(0, pb)))
                suw = [l_sa, kv_lms[tc * 2 + 1][1], kv_lms[tc * 2 + 1][0],
                       l_ones]
                tmms(mml3, waits=suw, defs=[l_su])
                last_supd[0] = l_su
                l_ksa = land("ksa")
                vop(lambda e: e.tensor_tensor(ksum_acc[:], ksum_acc[:], p_ks,
                                              ALU.add),
                    waits=[l_su], defs=[l_ksa])
                last_ksa[0] = l_ksa
            if full and dsub >= 5:
                ot_lms = []
                for tci, l_omx in om_lms:
                    l_ot = land("ot")
                    otw = [l_omx]
                    if prev_wo[0] and not ot_lms:
                        otw.append(prev_wo[0])
                    dmatp(omtm[:, tci * D:(tci + 1) * D],
                          omfm[:].rearrange("p (oc t) -> p oc t", t=STW)
                          [:, :, tci * CH:(tci + 1) * CH],
                          waits=otw, defs=[l_ot])
                    ot_lms.append(l_ot)
                to = st - NST // 2
                for oc in range(8):
                    l_wo = land("wo")
                    mml4 = []
                    pw = p_w0 if oc % 2 == 0 else p_w1
                    for mc in range(8):
                        mml4.append(lambda e, mc=mc, oc=oc, pw=pw: e.matmul(
                            pw[:, 0:512],
                            wab["o"][:, mc * D + oc * 128: mc * D + (oc + 1) * 128],
                            omfm[:].rearrange("p (k t) -> p k t", t=STW)[:, mc, :],
                            start=(mc == 0), stop=(mc == 7)))
                    wow = ot_lms + [l_wcast["o"], l_axr_all]
                    if out_done:
                        wow.append(out_done[-1])
                    tmms(mml4, waits=wow, defs=[l_wo])
                    prev_wo[0] = l_wo
                    l_ax = land("ax")
                    axw = [last_fe[0]] if last_fe[0] else [l_axr_all]
                    dma(sc_axr[oc * 128:(oc + 1) * 128, to * STW:(to + 1) * STW],
                        axr_d[:], waits=axw, defs=[l_ax])
                    l_fe = land("fe")
                    few = [l_wo, l_ax] + ([out_done[-1]] if out_done else [])
                    vop(lambda e, pw=pw: e.tensor_tensor(
                        aout_d[:], pw[:, 0:512], axr_d[:], ALU.add),
                        waits=few, defs=[l_fe])
                    last_fe[0] = l_fe
                    l_fo = land("fo")
                    dma(aout_d[:], out_fm[oc * 128:(oc + 1) * 128,
                                          to * STW:(to + 1) * STW],
                        waits=[l_fe], defs=[l_fo])
                    out_done.append(l_fo)
            st_lm[st % 2] = last_supd[0]

        if dbg:
            dma(sc_xfm, dbg_out["d_xfm"], waits=[("ALL", "DM")])
            dma(sc_axr, dbg_out["d_axr"])
            dma(sc_wgt, dbg_out["d_wgt"])
            pass
            dma(xqfm[:], dbg_out["d_xq"])
            dma(hqfm, dbg_out["d_hq2"])
            dma(sc_wut, dbg_out["d_wut"])
            dma(sc_wdt, dbg_out["d_wdt"])

        fw = ([out_done[-1]] if out_done else []) + [("ALL", "DM"), ("ALL", "TP")]
        pl.op("sync", None, incs=[], waits=fw)

        pl.resolve({k: None for k in SEMS})
        return _emit_all(pl, block, sem, nc, dmy)


_NC_CACHE = None


def kernel(L0_attn_x, L0_attn_Wq, L0_attn_Wk, L0_attn_Wv, L0_attn_Wo,
           L0_ffn_w_gate, L0_ffn_w_up, L0_ffn_w_down):
    global _NC_CACHE
    x = np.ascontiguousarray(np.asarray(L0_attn_x, dtype=np.float32))
    wq = np.ascontiguousarray(np.asarray(L0_attn_Wq, dtype=np.float32).T)
    wk = np.ascontiguousarray(np.asarray(L0_attn_Wk, dtype=np.float32).T)
    wv = np.ascontiguousarray(np.asarray(L0_attn_Wv, dtype=np.float32).T)
    wo = np.ascontiguousarray(np.asarray(L0_attn_Wo, dtype=np.float32).T)
    wg = np.ascontiguousarray(np.asarray(L0_ffn_w_gate, dtype=np.float32))
    wu = np.ascontiguousarray(np.asarray(L0_ffn_w_up, dtype=np.float32))
    wd = np.ascontiguousarray(np.asarray(L0_ffn_w_down, dtype=np.float32))

    in_maps = []
    for c in range(NC):
        b, hf = c // 2, c % 2
        if hf == 0:
            prefix = np.zeros((TH, D), dtype=np.float32)
            km = np.concatenate([np.zeros(TH, np.float32),
                                 np.ones(TH, np.float32)])
        else:
            prefix = x[b, 0:TH]
            km = np.ones(S, np.float32)
        km = np.ascontiguousarray(km.reshape(NCH, CH).T)
        own = x[b, hf * TH:(hf + 1) * TH]
        xs = np.ascontiguousarray(np.concatenate([prefix, own], axis=0))
        xfh = np.ascontiguousarray(own.T)
        in_maps.append({
            "xs_tm": xs, "xs_fmh": xfh, "kmask": km,
            "wqt": wq, "wkt": wk, "wvt": wv, "wot": wo,
            "wg": wg, "wu": wu, "wd": wd,
        })

    if _NC_CACHE is None:
        _NC_CACHE = build_program()
    res = run_bass_kernel_spmd(_NC_CACHE, in_maps, core_ids=list(range(NC)))
    out = np.empty((B, S, D), dtype=np.float32)
    for c in range(NC):
        b, hf = c // 2, c % 2
        out[b, hf * TH:(hf + 1) * TH] = res.results[c]["out_fm"].T
    return out


# version 04
# speedup vs baseline: 1.4753x; 1.4753x over previous
import sys
sys.path.insert(0, '/opt/trn_rl_repo')
import contextlib
import numpy as np
import ml_dtypes
import concourse.bass as bass
import concourse.mybir as mybir
from concourse.bass_utils import run_bass_kernel_spmd

F32 = mybir.dt.float32
BF16 = mybir.dt.bfloat16
AF = mybir.ActivationFunctionType
ALU = mybir.AluOpType

B, S, D, H, DH, I = 4, 4096, 1024, 16, 64, 2816
NC = 8
TH = S // 2
CH = 128
NCH = S // CH
STW = 512
NST = S // STW
NITI = (I + 511) // 512
EPS_ATT = 1e-8
EPS_LN = 1e-5
THR = [(0.25, 0.5), (0.75, 0.5), (1.25, 0.5), (1.75, 0.5),
       (2.5, 1.0), (3.5, 1.0), (5.0, 2.0)]


class Plan:

    def __init__(self, barrier=False):
        self.ops = []
        self.lands = {}
        self.barrier = barrier

    def op(self, eng, fn=None, incs=(), waits=(), defs=()):
        if self.barrier:
            waits = tuple(waits) + tuple(
                ("ALL", s) for s in ("DM", "TP", "V", "SC", "G", "T"))
        self.ops.append((eng, fn, tuple(incs), tuple(waits), tuple(defs)))

    def resolve(self, sems):
        counts = {k: 0 for k in sems}
        pre = []
        for eng, fn, incs, waits, defs in self.ops:
            pre.append(dict(counts))
            for sname, n in incs:
                counts[sname] += n
            for dname in defs:
                assert incs, f"landmark {dname} on op with no incs"
                self.lands[dname] = (incs[-1][0], counts[incs[-1][0]])
        self.resolved = []
        for i, (eng, fn, incs, waits, defs) in enumerate(self.ops):
            out = []
            for w in waits:
                if isinstance(w, tuple) and w[0] == "ALL":
                    if pre[i][w[1]] > 0:
                        out.append((w[1], pre[i][w[1]]))
                else:
                    assert w in self.lands, f"wait on undefined landmark {w}"
                    out.append(self.lands[w])
            self.resolved.append(out)

    def emit(self, eng_name, handle, sems, flush=None):
        for i, (eng, fn, incs, waits, defs) in enumerate(self.ops):
            if eng != eng_name:
                continue
            byname = {}
            for sname, val in self.resolved[i]:
                byname[sname] = max(byname.get(sname, 0), val)
            for sname, val in byname.items():
                handle.wait_ge(sems[sname], val)
            inst = fn(handle) if fn is not None else None
            if incs:
                if flush is not None and fn is not None:
                    inst = flush(handle)
                elif inst is None:
                    inst = handle.engine_nop()
                for sname, n in incs:
                    inst = inst.then_inc(sems[sname], n)


def ceil_div(a, b):
    return (a + b - 1) // b


def _emit_all(pl, block, sem, nc, dmy):
    ALU_ = mybir.AluOpType

    @block.sync
    def _(sync):
        pl.emit("sync", sync, sem)

    @block.vector
    def _(vector):
        pl.emit("vector", vector, sem,
                flush=lambda e: e.tensor_scalar(
                    dmy[:, 224:256], dmy[:, 128:160], 1.0, None, ALU_.mult))

    @block.scalar
    def _(scalar):
        pl.emit("scalar", scalar, sem,
                flush=lambda e: e.copy(dmy[:, 160:176], dmy[:, 128:144]))

    @block.gpsimd
    def _(gpsimd):
        pl.emit("gpsimd", gpsimd, sem,
                flush=lambda e: e.tensor_copy(dmy[:, 192:208], dmy[:, 144:160]))

    @block.tensor
    def _(tensor):
        pl.emit("tensor", tensor, sem, flush=(pl.tensor_flush if pl.use_tflush else None))
    return nc


def build_program(phases='ABCD', dsub=9, dbg=False, csub=9):
    nc = bass.Bass("TRN2", target_bir_lowering=False,
                   detect_race_conditions=False)

    xs_tm = nc.dram_tensor("xs_tm", [S, D], F32, kind="ExternalInput").ap()
    xs_fmh = nc.dram_tensor("xs_fmh", [D, TH], F32, kind="ExternalInput").ap()
    kmask = nc.dram_tensor("kmask", [CH, NCH], F32, kind="ExternalInput").ap()
    wqt = nc.dram_tensor("wqt", [D, D], F32, kind="ExternalInput").ap()
    wkt = nc.dram_tensor("wkt", [D, D], F32, kind="ExternalInput").ap()
    wvt = nc.dram_tensor("wvt", [D, D], F32, kind="ExternalInput").ap()
    wot = nc.dram_tensor("wot", [D, D], F32, kind="ExternalInput").ap()
    wg = nc.dram_tensor("wg", [I, D], F32, kind="ExternalInput").ap()
    wu = nc.dram_tensor("wu", [I, D], F32, kind="ExternalInput").ap()
    wd = nc.dram_tensor("wd", [D, I], F32, kind="ExternalInput").ap()
    out_fm = nc.dram_tensor("out_fm", [D, TH], F32, kind="ExternalOutput").ap()
    sc_xfm = nc.dram_tensor("sc_xfm", [D, S], BF16, kind="Internal").ap()
    sc_wgt = nc.dram_tensor("sc_wgt", [D, I], BF16, kind="Internal").ap()
    sc_wut = nc.dram_tensor("sc_wut", [D, I], BF16, kind="Internal").ap()
    sc_wdt = nc.dram_tensor("sc_wdt", [I, D], BF16, kind="Internal").ap()
    sc_axr = nc.dram_tensor("sc_axr", [D, TH], F32, kind="Internal").ap()
    sc_hqf = nc.dram_tensor("sc_hqf", [I, TH], BF16, kind="Internal").ap()
    dbg_out = {}
    if dbg:
        dbg_out["d_xfm"] = nc.dram_tensor("d_xfm", [D, S], BF16, kind="ExternalOutput").ap()
        dbg_out["d_axr"] = nc.dram_tensor("d_axr", [D, TH], F32, kind="ExternalOutput").ap()
        dbg_out["d_wgt"] = nc.dram_tensor("d_wgt", [D, I], BF16, kind="ExternalOutput").ap()
        dbg_out["d_hqf"] = nc.dram_tensor("d_hqf", [I, TH], BF16, kind="ExternalOutput").ap()
        dbg_out["d_xq"] = nc.dram_tensor("d_xq", [128, 8 * TH], BF16, kind="ExternalOutput").ap()
        dbg_out["d_diag"] = nc.dram_tensor("d_diag", [128, 512], F32, kind="ExternalOutput").ap()
        for _i in range(8):
            dbg_out[f"d_s{_i}"] = nc.dram_tensor(f"d_s{_i}", [128, 512], F32, kind="ExternalOutput").ap()
        dbg_out["d_wut"] = nc.dram_tensor("d_wut", [D, I], BF16, kind="ExternalOutput").ap()
        dbg_out["d_wdt"] = nc.dram_tensor("d_wdt", [I, D], BF16, kind="ExternalOutput").ap()
        dbg_out["d_hq2"] = nc.dram_tensor("d_hq2", [128, 22 * TH], BF16, kind="ExternalOutput").ap()
        dbg_out["d_sil"] = nc.dram_tensor("d_sil", [128, 512], F32, kind="ExternalOutput").ap()
        dbg_out["d_hbb"] = nc.dram_tensor("d_hbb", [128, 512], F32, kind="ExternalOutput").ap()
        dbg_out["d_hqt"] = nc.dram_tensor("d_hqt", [128, 512], BF16, kind="ExternalOutput").ap()
        dbg_out["d_wgs"] = nc.dram_tensor("d_wgs", [128, 8 * 512], BF16, kind="ExternalOutput").ap()

    import os as _os
    pl = Plan(barrier=_os.environ.get("BARRIER", "") != "")
    SEMS = ["DM", "TP", "V", "SC", "G", "T"]
    uid = [0]

    def land(pfx):
        uid[0] += 1
        return f"{pfx}{uid[0]}"

    with contextlib.ExitStack() as stk:
        E = stk.enter_context
        sem = {name: E(nc.semaphore(name)) for name in SEMS}

        xqfm = E(nc.sbuf_tensor("xqfm", [128, 8 * TH], BF16,
                                is_transpose_dest=True))
        QW = 1024
        q_u = E(nc.sbuf_tensor("q_u", [CH, QW], F32))
        q_zs = E(nc.sbuf_tensor("q_zs", [CH, QW], BF16))
        q_acc = E(nc.sbuf_tensor("q_acc", [CH, QW], BF16))
        q_t1 = E(nc.sbuf_tensor("q_t1", [CH, QW], BF16))
        q_t2 = E(nc.sbuf_tensor("q_t2", [CH, QW], BF16))
        q_out = E(nc.sbuf_tensor("q_out", [CH, QW], BF16))
        q_bm = E(nc.sbuf_tensor("q_bm", [CH, QW // 16], F32))
        q_s = E(nc.sbuf_tensor("q_s", [CH, QW // 16], F32))
        q_rs = E(nc.sbuf_tensor("q_rs", [CH, QW // 16], F32))
        sumc = E(nc.sbuf_tensor("sumc", [CH, 12], F32))
        mrc = E(nc.sbuf_tensor("mrc", [CH, 8], F32))
        kmv = E(nc.sbuf_tensor("kmv", [CH, NCH], F32))
        onescol = E(nc.sbuf_tensor("onescol", [128, 1], BF16))
        iocol = E(nc.sbuf_tensor("iocol", [128, 1], F32))
        iof = E(nc.sbuf_tensor("iof", [128, 128], F32))
        ltm4 = E(nc.sbuf_tensor("ltm4", [128, 512], BF16))
        dencol = E(nc.sbuf_tensor("dencol", [128, 16], F32))
        dmy = E(nc.sbuf_tensor("dmy", [128, 256], F32))
        ksum_acc = E(nc.sbuf_tensor("ksum_acc", [128, 8], F32))
        denr = E(nc.sbuf_tensor("denr", [128, 16], F32))
        g_f32 = E(nc.sbuf_tensor("g_f32", [128, 7104], F32))
        g_bf = E(nc.sbuf_tensor("g_bf", [128, 66624], BF16,
                                is_transpose_dest=True))

        class Carver:
            def __init__(self, base_ap):
                self.ap = base_ap
                self.off = 0

            def take(self, cols):
                cols_al = (cols + 63) & ~63
                a = self.ap[:, self.off:self.off + cols]
                self.off += cols_al
                assert self.off <= self.ap.shape[1], "arena overflow"
                return a

        cf = Carver(g_f32[:])
        cb = Carver(g_bf[:])
        xbuf = [cf.take(D) for _ in range(2)]
        trash = cf.take(D)
        xn_tm = cf.take(D)
        wbuf = [cf.take(1024) for _ in range(2)]
        xt_tm = cb.take(D)
        xt_fm = cb.take(D)
        wq_tm = cb.take(1024)
        wq_fm = cb.take(1024)
        cf = Carver(g_f32[:])
        cb = Carver(g_bf[:])
        sil = cf.take(512)
        hbb = cf.take(512)
        axr = cf.take(512)
        aout = cf.take(512)
        dbg_p = cf.take(512)
        hqp = cb.take(4 * 128)
        wgs = cb.take(8 * 512)
        wus = cb.take(8 * 512)
        hq_tm = cb.take(512)
        hqfm = cb.take(22 * TH)
        wds = cb.take(22 * 128)
        cf = Carver(g_f32[:])
        cb = Carver(g_bf[:])
        wfb = [cf.take(D) for _ in range(2)]
        elu1 = cf.take(512)
        elu2 = cf.take(512)
        axr_d = cf.take(512)
        aout_d = cf.take(512)
        kfm = cb.take(8 * STW)
        omfm = cb.take(8 * STW)
        xfmst = [cb.take(8 * STW) for _ in range(2)]
        wab = {w: cb.take(8 * D) for w in ("q", "k", "v", "o")}
        kmtm = cb.take(4 * D)
        vtm = cb.take(4 * D)
        qfm = cb.take(8 * STW)
        saug = cb.take(8 * 65)
        atm = cb.take(512)
        omtm = cb.take(4 * D)
        p_S = E(nc.psum_tensor("p_S", [128, 512], F32))
        p_misc = E(nc.psum_tensor("p_misc", [128, 512], F32))
        p_ks = p_misc[:, 0:8]
        p_dn = p_misc[:, 16:32]
        p_fl = p_misc[0:1, 256:257]
        p_at = E(nc.psum_tensor("p_at", [128, 512], F32))
        p_n0 = E(nc.psum_tensor("p_n0", [128, 512], F32))
        p_n1 = E(nc.psum_tensor("p_n1", [128, 512], F32))
        p_w0 = E(nc.psum_tensor("p_w0", [128, 512], F32))
        p_w1 = E(nc.psum_tensor("p_w1", [128, 512], F32))

        block = E(nc.Block())

        dmy_i = [0]

        def vdum(e):
            i = dmy_i[0] % 4
            dmy_i[0] += 1
            e.tensor_scalar(dmy[:, 32 * i:32 * i + 32], dmy[:, 128:160],
                            1.0, None, ALU.mult)

        def _tflush(e):
            return e.matmul(p_fl, onescol[:], onescol[:], start=True,
                            stop=True, skip_group_check=True)
        pl.tensor_flush = _tflush
        import os as _os
        pl.use_tflush = _os.environ.get("NO_TFLUSH", "") == ""

        def dma(src_ap, dst_ap, waits=(), defs=()):
            pl.op("sync", lambda e, s=src_ap, d=dst_ap: e.dma_start(out=d, in_=s),
                  incs=[("DM", 16)], waits=tuple(waits) + (("ALL", "TP"),),
                  defs=defs)

        def dmatp(src_ap, dst_ap, waits=(), defs=()):
            pl.op("sync", lambda e, s=src_ap, d=dst_ap: e.dma_start_transpose(d, s),
                  incs=[("TP", 16)],
                  waits=tuple(waits) + (("ALL", "DM"), ("ALL", "TP")), defs=defs)

        def vop(fn, waits=(), defs=()):
            pl.op("vector", fn, incs=[("V", 1)], waits=waits, defs=defs)

        def sop(fn, waits=(), defs=()):
            if not first_eng["scalar"]:
                first_eng["scalar"] = True
                waits = tuple(waits) + (l_dmy,)
            pl.op("scalar", fn, incs=[("SC", 1)], waits=waits, defs=defs)

        def gop(fn, waits=(), defs=()):
            if not first_eng["gpsimd"]:
                first_eng["gpsimd"] = True
                waits = tuple(waits) + (l_dmy,)
            pl.op("gpsimd", fn, incs=[("G", 1)], waits=waits, defs=defs)

        def top(fn, waits=(), defs=()):
            pl.op("tensor", fn, incs=[("T", 1)], waits=waits, defs=defs)

        te_state = {"base": None}

        def tmms(mmlist, waits=(), defs=()):
            def go(e):
                last = None
                for item in mmlist:
                    if isinstance(item, tuple):
                        base, f = item
                        if te_state["base"] is not None and \
                                te_state["base"] != base:
                            e.ldweights(ltm4[:, 0:128])
                        te_state["base"] = base
                        last = f(e)
                    else:
                        te_state["base"] = None
                        last = item(e)
                return last
            pl.op("tensor", go, incs=[("T", 1)], waits=waits, defs=defs)

        q_state = {"zs": None, "dst": None}

        def emit_quant2(src_fn, w, out_dst_fn, waits=(), dst_free=None):
            nb = w // 16
            src = src_fn()
            outd = out_dst_fn()
            def mk_scales(e):
                e.tensor_reduce(
                    q_bm[:, 0:nb], src.rearrange("p (b k) -> p b k", k=16),
                    axis=mybir.AxisListType.X, op=ALU.max,
                    apply_absolute_value=True)
                vdum(e)
                e.tensor_scalar(q_s[:, 0:nb], q_bm[:, 0:nb],
                                6e-12, 1.0 / 6.0, ALU.max, ALU.mult)
                vdum(e)
                e.reciprocal(q_rs[:, 0:nb], q_s[:, 0:nb])
                vdum(e)
                return e.tensor_scalar(dmy[:, 0:32], dmy[:, 128:160], 1.0,
                                       None, ALU.mult)
            vop(mk_scales, waits=waits)
            rs_b = q_rs[:, 0:nb].unsqueeze(2).broadcast_to([128, nb, 16])
            s_b = q_s[:, 0:nb].unsqueeze(2).broadcast_to([128, nb, 16])
            l_u = land("qu")
            uw = [q_state["zs"]] if q_state["zs"] else []
            vop(lambda e: e.tensor_tensor(q_u[:, 0:w], src, rs_b, ALU.mult),
                waits=uw, defs=[l_u])
            l_zs = land("qzs")
            sop(lambda e: e.activation(q_zs[:, 0:w], q_u[:, 0:w], AF.Sign),
                waits=[l_u])
            sop(lambda e: e.activation(q_u[:, 0:w], q_u[:, 0:w], AF.Abs),
                defs=[l_zs])
            q_state["zs"] = l_zs
            t0, w0 = THR[0]
            t1x, w1x = THR[1]
            vop(lambda e: e.tensor_scalar(
                q_t1[:, 0:w], q_u[:, 0:w], t0, w0, ALU.is_gt, ALU.mult),
                waits=[l_zs])
            vop(lambda e: e.tensor_scalar(
                q_t2[:, 0:w], q_u[:, 0:w], t1x, w1x, ALU.is_gt, ALU.mult))
            vop(lambda e: e.tensor_tensor(q_acc[:, 0:w], q_t1[:, 0:w],
                                          q_t2[:, 0:w], ALU.add))
            for tj, wj in THR[2:]:
                vop(lambda e, tj=tj, wj=wj: e.tensor_scalar(
                    q_t1[:, 0:w], q_u[:, 0:w], tj, wj, ALU.is_gt, ALU.mult))
                vop(lambda e: e.tensor_tensor(q_acc[:, 0:w], q_acc[:, 0:w],
                                              q_t1[:, 0:w], ALU.add))
            vop(lambda e: e.tensor_tensor(q_t2[:, 0:w], q_acc[:, 0:w],
                                          q_zs[:, 0:w], ALU.mult))
            l_out = land("qo")
            ow = [dst_free] if dst_free else []
            vop(lambda e: e.tensor_tensor(outd, q_t2[:, 0:w], s_b, ALU.mult),
                waits=ow, defs=[l_out])
            return l_out

        l_dmy = land("dmy0")
        vop(lambda e: e.memset(dmy[:], 0.0), defs=[l_dmy])
        first_eng = {"scalar": False, "gpsimd": False}
        l_km = land("km")
        dma(kmask, kmv[:, 0:NCH],
            defs=[l_km])
        l_iota = land("iota")
        gop(lambda e: e.iota(iocol[:], pattern=[[0, 1]], base=0,
                             channel_multiplier=1,
                             allow_small_or_imprecise_dtypes=True))
        gop(lambda e: e.iota(iof[:], pattern=[[1, 128]], base=0,
                             channel_multiplier=0,
                             allow_small_or_imprecise_dtypes=True),
            defs=[l_iota])
        l_ones = land("ones")
        vop(lambda e: e.memset(onescol[:], 1.0), defs=[l_ones])
        l_ltm = land("ltm")
        def mk_ltm(e):
            last = None
            for r in range(4):
                last = e.tensor_scalar(ltm4[:, r * 128:(r + 1) * 128], iof[:],
                                       iocol[:], None, ALU.is_ge)
            return last
        vop(mk_ltm, waits=[l_iota], defs=[l_ltm])

        x_done_lm = [None] * NCH
        last_mr = [None]
        last_xtp = [None]
        last_xq_tp = [None]
        for c in range(NCH):
            buf = xbuf[c % 2]
            wl = [] if c < 2 else [x_done_lm[c - 2]]
            l_ld = land("xa")
            dma(xs_tm[c * CH:(c + 1) * CH, :], buf[:], waits=wl, defs=[l_ld])
            def mk_mr(e, b=buf):
                e.bn_stats(sumc[:, 0:6], b[:, 0:512])
                e.bn_stats(sumc[:, 6:12], b[:, 512:1024])
                e.bn_aggr(mrc[:, 4:6], sumc[:, 0:6])
                e.bn_aggr(mrc[:, 6:8], sumc[:, 6:12])
                vdum(e)
                e.tensor_tensor(mrc[:, 0:1], mrc[:, 4:5], mrc[:, 6:7], ALU.add)
                e.tensor_tensor(mrc[:, 3:4], mrc[:, 4:5], mrc[:, 6:7], ALU.subtract)
                e.tensor_tensor(mrc[:, 1:2], mrc[:, 5:6], mrc[:, 7:8], ALU.add)
                e.tensor_scalar(mrc[:, 0:1], mrc[:, 0:1], -0.5, None, ALU.mult)
                e.tensor_tensor(mrc[:, 3:4], mrc[:, 3:4], mrc[:, 3:4], ALU.mult)
                e.tensor_scalar(mrc[:, 1:2], mrc[:, 1:2], 0.5, None, ALU.mult)
                e.tensor_scalar(mrc[:, 3:4], mrc[:, 3:4], 0.25, EPS_LN,
                                ALU.mult, ALU.add)
                vdum(e)
                e.tensor_tensor(mrc[:, 3:4], mrc[:, 1:2], mrc[:, 3:4], ALU.add)
                vdum(e)
                return e.reciprocal(mrc[:, 1:2], mrc[:, 3:4])
            l_mr = land("lm")
            vop(mk_mr, waits=[l_ld] + ([last_mr[0]] if last_mr[0] else []),
                defs=[l_mr])
            last_mr[0] = l_mr
            l_r = land("r")
            sop(lambda e: e.activation(mrc[:, 1:2], mrc[:, 1:2], AF.Sqrt),
                waits=[l_mr], defs=[l_r])
            l_xt = land("xt")

            def mk_xt(e, b=buf):
                e.tensor_tensor(mrc[:, 2:3], mrc[:, 0:1], mrc[:, 1:2], ALU.mult)
                vdum(e)
                return e.tensor_scalar(xt_tm[:], b[:], mrc[:, 1:2], mrc[:, 2:3],
                                       ALU.mult, ALU.add)
            xtw = [l_r] + ([last_xtp[0]] if last_xtp[0] else [])
            vop(mk_xt, waits=xtw, defs=[l_xt])
            l_tp = land("tp")
            dmatp(xt_tm[:], xt_fm[:].rearrange("p (dc t) -> p dc t", t=CH),
                  waits=[l_xt], defs=[l_tp])
            last_xtp[0] = l_tp
            l_xf = land("xf")
            dma(xt_fm[:].rearrange("p (dc t) -> p dc t", t=CH),
                sc_xfm.rearrange("(dc p) s -> p dc s", p=128)
                [:, :, c * CH:(c + 1) * CH],
                waits=[l_tp], defs=[l_xf])
            if c >= NCH // 2:
                tc = c - NCH // 2
                l_xn = land("xn")

                def mk_xn(e, b=buf):
                    return e.tensor_scalar(xn_tm[:], b[:], mrc[:, 1:2],
                                           mrc[:, 2:3], ALU.mult, ALU.add)
                vop(mk_xn, waits=[l_r], defs=[l_xn])
                l_q = emit_quant2(lambda: xn_tm[:], D,
                                  lambda: q_out[:, 0:D], waits=[l_xn],
                                  dst_free=last_xq_tp[0])
                l_tq = land("tq")
                dmatp(q_out[:, 0:D],
                      xqfm[:].rearrange("p (dc t) -> p dc t", t=TH)
                      [:, :, tc * CH:(tc + 1) * CH],
                      waits=[l_q], defs=[l_tq])
                last_xq_tp[0] = l_tq
                x_done_lm[c] = l_tq
            else:
                x_done_lm[c] = l_xf

        if 'B' not in phases:
            if dbg:
                dma(sc_xfm, dbg_out["d_xfm"], waits=[("ALL", "DM")])
            pl.op("sync", None, incs=[],
                  waits=[("ALL", "DM"), ("ALL", "TP")])
            pl.resolve({k: None for k in SEMS})
            return _emit_all(pl, block, sem, nc, dmy)
        wstate = {"tp": None, "prev": [None, None], "i": 0}

        def wprep(w_in, row0, col0, width, sc_out):
            i = wstate["i"]
            wstate["i"] += 1
            buf = wbuf[i % 2]
            wl = [] if wstate["prev"][i % 2] is None else [wstate["prev"][i % 2]]
            l_ld = land("wl")
            dma(w_in[row0:row0 + CH, col0:col0 + width], buf[:, 0:width],
                waits=wl, defs=[l_ld])
            l_q = emit_quant2(lambda b=buf: b[:, 0:width], width,
                              lambda: wq_tm[:, 0:width], waits=[l_ld],
                              dst_free=wstate["tp"])
            l_tp = land("wt")
            ncc = width // 128
            dmatp(wq_tm[:, 0:width],
                  wq_fm[:].rearrange("p (cc t) -> p cc t", t=CH)[:, 0:ncc, :],
                  waits=[l_q], defs=[l_tp])
            wstate["tp"] = l_tp
            l_st = land("ws")
            dma(wq_fm[:].rearrange("p (cc t) -> p cc t", t=CH)[:, 0:ncc, :],
                sc_out.rearrange("(cc p) n -> p cc n", p=128)
                [:, col0 // 128:col0 // 128 + ncc, row0:row0 + CH],
                waits=[l_tp], defs=[l_st])
            wstate["prev"][i % 2] = l_st
            return l_st

        for c in range(I // CH):
            wprep(wg, c * CH, 0, D, sc_wgt)
        for c in range(I // CH):
            wprep(wu, c * CH, 0, D, sc_wut)
        for c in range(D // CH):
            wprep(wd, c * CH, 0, 1024, sc_wdt)
            wprep(wd, c * CH, 1024, 1024, sc_wdt)
            l_wdone = wprep(wd, c * CH, 2048, 768, sc_wdt)

        if 'C' not in phases:
            if dbg:
                dma(sc_xfm, dbg_out["d_xfm"], waits=[("ALL", "DM")])
                dma(sc_wgt, dbg_out["d_wgt"])
            pl.op("sync", None, incs=[],
                  waits=[("ALL", "DM"), ("ALL", "TP")])
            pl.resolve({k: None for k in SEMS})
            return _emit_all(pl, block, sem, nc, dmy)
        hq_done = []
        hq_cp = [None]
        hq_st = [None]
        last_wg_lm = None
        _ctiles = int(_os.environ.get('CTILES', '999'))
        if dbg:
            l_dg0 = land("dg0")
            dma(sc_wgt.rearrange("(dc p) n -> p dc n", p=128)[:, :, 0:512],
                wgs[:].rearrange("p (dc n) -> p dc n", n=512),
                waits=[l_wdone], defs=[l_dg0])
            l_dgm = land("dgm")
            xsl0 = xqfm[:].rearrange("p (dc t) -> p dc t", t=TH)
            import os as _os
            _dsel = _os.environ.get('DIAGSEL', '1')
            _dml = []
            if _dsel == '1':
                _dml.append(lambda e: e.matmul(
                    p_w0[:, 0:512], xsl0[:, 0, 0:128],
                    wgs[:].rearrange("p (dc n) -> p dc n", n=512)[:, 0, :],
                    start=True, stop=True))
            else:
                for dc in range(8):
                    _dml.append(lambda e, dc=dc: e.matmul(
                        p_w0[:, 0:512], xsl0[:, dc, 0:128],
                        wgs[:].rearrange("p (dc n) -> p dc n", n=512)[:, dc, :],
                        start=(dc == 0), stop=(dc == 7)))
                if _dsel == '16':
                    for dc in range(8):
                        _dml.append(lambda e, dc=dc: e.matmul(
                            p_w1[:, 0:512], xsl0[:, dc, 0:128],
                            wus[:].rearrange("p (dc n) -> p dc n", n=512)[:, dc, :],
                            start=(dc == 0), stop=(dc == 7)))
            tmms(_dml, waits=[l_dg0], defs=[l_dgm])
            l_dgc = land("dgc")
            sop(lambda e: e.activation(sil[:, 0:512], p_w0[:, 0:512], AF.Copy),
                waits=[l_dgm], defs=[l_dgc])
            dma(sil[:, 0:512], dbg_out["d_diag"], waits=[l_dgc])
        for it in range(NITI):
            isz = min(512, I - it * 512)
            l_wg = land("fg")
            wl = [l_wdone] if it == 0 else [hq_done[-1]]
            dma(sc_wgt.rearrange("(dc p) n -> p dc n", p=128)
                [:, :, it * 512:it * 512 + isz],
                wgs[:].rearrange("p (dc n) -> p dc n", n=512)[:, :, 0:isz],
                waits=wl, defs=[l_wg])
            l_wu = land("fu")
            dma(sc_wut.rearrange("(dc p) n -> p dc n", p=128)
                [:, :, it * 512:it * 512 + isz],
                wus[:].rearrange("p (dc n) -> p dc n", n=512)[:, :, 0:isz],
                defs=[l_wu])
            last_wg_lm = l_wu
            for tc in range(TH // CH):
                if it * 16 + tc >= _ctiles:
                    continue
                xsl = xqfm[:].rearrange("p (dc t) -> p dc t", t=TH)
                l_mm = land("fm")
                gml = []
                for dc in range(8):
                    gml.append(lambda e, dc=dc, tc=tc, isz=isz, xsl=xsl: e.matmul(
                        p_w0[:, 0:isz],
                        xsl[:, dc, tc * CH:(tc + 1) * CH],
                        wgs[:].rearrange("p (dc n) -> p dc n", n=512)[:, dc, 0:isz],
                        start=(dc == 0), stop=(dc == 7)))
                for dc in range(8):
                    gml.append(lambda e, dc=dc, tc=tc, isz=isz, xsl=xsl: e.matmul(
                        p_w1[:, 0:isz],
                        xsl[:, dc, tc * CH:(tc + 1) * CH],
                        wus[:].rearrange("p (dc n) -> p dc n", n=512)[:, dc, 0:isz],
                        start=(dc == 0), stop=(dc == 7)))
                mmw = [l_wg, l_wu] if tc == 0 else [l_wg, l_wu, hq_done[-1]]
                tmms(gml, waits=mmw, defs=[l_mm])
                l_sil = land("fs")
                sop(lambda e, isz=isz: e.activation(sil[:, 0:isz], p_w0[:, 0:isz],
                                                    AF.Silu),
                    waits=[l_mm], defs=[l_sil])
                if csub < 2:
                    hq_done.append(l_sil)
                    continue
                l_h = land("fh")
                vop(lambda e, isz=isz: e.tensor_tensor(
                    hbb[:, 0:isz], sil[:, 0:isz], p_w1[:, 0:isz], ALU.mult),
                    waits=[l_sil], defs=[l_h])
                l_q = emit_quant2(lambda isz=isz: hbb[:, 0:isz], isz,
                                  lambda isz=isz: hq_tm[:, 0:isz], waits=[l_h],
                                  dst_free=hq_done[-1] if hq_done else None)
                _slots = {(0, 0): 0, (0, 15): 2, (1, 0): 3,
                          (2, 0): 4, (3, 0): 5, (4, 1): 6, (5, 0): 7}
                if dbg and (it, tc) in _slots:
                    dma(sil[:], dbg_out[f"d_s{_slots[(it, tc)]}"],
                        waits=[l_sil])
                if dbg and (it, tc) == (0, 0):
                    l_pd = land("pd")
                    vop(lambda e: e.tensor_copy(aout[:], p_w0[:, 0:512]),
                        waits=[l_sil], defs=[l_pd])
                    dma(aout[:], dbg_out["d_s1"], waits=[l_pd])
                if dbg and it == 4 and tc == 0:
                    dma(sil[:], dbg_out["d_sil"], waits=[l_sil])
                    dma(hbb[:], dbg_out["d_hbb"], waits=[l_h])
                    dma(hq_tm[:], dbg_out["d_hqt"], waits=[l_q])
                    dma(wgs[:], dbg_out["d_wgs"], waits=[l_mm])
                l_tp2 = land("ftp")
                ncc = isz // 128
                tpw = [l_q] + ([hq_cp[0]] if hq_cp[0] else [])
                dmatp(hq_tm[:, 0:isz],
                      hqp.rearrange("p (cc t) -> p cc t", t=CH)[:, 0:ncc, :],
                      waits=tpw, defs=[l_tp2])
                l_tq = land("ft")
                vop(lambda e, it=it, tc=tc, ncc=ncc: e.tensor_copy(
                    hqfm.rearrange("p (cc t) -> p cc t", t=TH)
                    [:, it * 4:it * 4 + ncc, tc * CH:(tc + 1) * CH],
                    hqp.rearrange("p (cc t) -> p cc t", t=CH)[:, 0:ncc, :]),
                    waits=[l_tp2], defs=[l_tq])
                hq_cp[0] = l_tq
                hq_done.append(l_tq)
        if csub < 2 or _ctiles < 96:
            if dbg:
                dma(xqfm[:], dbg_out["d_xq"], waits=[("ALL", "DM"), ("ALL", "TP")])
            pl.op("sync", None, incs=[], waits=[("ALL", "DM"), ("ALL", "TP")])
            pl.resolve({k: None for k in SEMS})
            return _emit_all(pl, block, sem, nc, dmy)
        axr_done = []
        ffn_fe = [None]
        l_hq_all = hq_done[-1]
        last_wd = [None]
        for oc in range(8):
            l_wdl = land("dwl")
            wlw = [last_wd[0]] if last_wd[0] else [l_hq_all]
            dma(sc_wdt.rearrange("(ic p) o -> p ic o", p=128)
                [:, :, oc * 128:(oc + 1) * 128],
                wds.rearrange("p (ic o) -> p ic o", o=128),
                waits=wlw, defs=[l_wdl])
            for ts4 in range(TH // 512):
                l_mm = land("dm")
                mml = []
                for ic in range(22):
                    mml.append(lambda e, ic=ic, oc=oc, ts4=ts4: e.matmul(
                        p_w0[:, 0:512],
                        wds.rearrange("p (ic o) -> p ic o", o=128)[:, ic, :],
                        hqfm.rearrange("p (cc t) -> p cc t", t=TH)
                        [:, ic, ts4 * 512:(ts4 + 1) * 512],
                        start=(ic == 0), stop=(ic == 21)))
                mmw = [l_wdl]
                if axr_done:
                    mmw.append(axr_done[-1])
                tmms(mml, waits=mmw, defs=[l_mm])
                l_x = land("dx")
                axw = [ffn_fe[0]] if ffn_fe[0] else []
                dma(xs_fmh[oc * 128:(oc + 1) * 128, ts4 * 512:(ts4 + 1) * 512],
                    axr[:], waits=axw, defs=[l_x])
                l_e = land("de")
                few = [l_mm, l_x] + ([axr_done[-1]] if axr_done else [])
                vop(lambda e: e.tensor_tensor(aout[:], p_w0[:, 0:512], axr[:],
                                              ALU.add),
                    waits=few, defs=[l_e])
                ffn_fe[0] = l_e
                l_s = land("ds")
                dma(aout[:], sc_axr[oc * 128:(oc + 1) * 128,
                                    ts4 * 512:(ts4 + 1) * 512],
                    waits=[l_e], defs=[l_s])
                axr_done.append(l_s)
            last_wd[0] = l_mm
        l_axr_all = axr_done[-1]

        if 'D' not in phases:
            if dbg:
                dma(sc_xfm, dbg_out["d_xfm"], waits=[("ALL", "DM")])
                dma(sc_axr, dbg_out["d_axr"])
                dma(sc_wgt, dbg_out["d_wgt"])
                dma(xqfm[:], dbg_out["d_xq"])
                dma(hqfm, dbg_out["d_hq2"])
            pl.op("sync", None, incs=[],
                  waits=[("ALL", "DM"), ("ALL", "TP")])
            pl.resolve({k: None for k in SEMS})
            return _emit_all(pl, block, sem, nc, dmy)
        wsrc = {"q": wqt, "k": wkt, "v": wvt, "o": wot}
        l_wcast = {}
        prev = [l_axr_all, l_axr_all]
        for wi, wname in enumerate(("q", "k", "v", "o")):
            for dc in range(8):
                idx = (wi * 8 + dc) % 2
                buf = wfb[idx]
                wl = [prev[idx]] if prev[idx] else []
                l_ld = land("awl")
                dma(wsrc[wname][dc * 128:(dc + 1) * 128, :], buf[:],
                    waits=wl, defs=[l_ld])
                l_c = land("awc")
                vop(lambda e, b=buf, wname=wname, dc=dc: e.tensor_copy(
                    wab[wname][:, dc * D:(dc + 1) * D], b[:]),
                    waits=[l_ld], defs=[l_c])
                prev[idx] = l_c
            l_wcast[wname] = prev[(wi * 8 + 7) % 2]

        if dsub < 1.5:
            pl.op("sync", None, incs=[], waits=[("ALL", "DM"), ("ALL", "TP")])
            pl.resolve({k: None for k in SEMS})
            return _emit_all(pl, block, sem, nc, dmy)
        l_ksa0 = land("ks0")
        vop(lambda e: e.memset(ksum_acc[:], 0.0), defs=[l_ksa0])
        last_ksa = [l_ksa0]
        st_lm = [None, None]
        out_done = []
        last_supd = [None]
        prev_om = [None]
        prev_kf = [None]
        prev_wo = [None]
        last_fe = [None]
        for st in range(NST):
            full = st >= NST // 2
            xb = xfmst[st % 2]
            wl = [st_lm[st % 2]] if st_lm[st % 2] else [l_axr_all]
            l_x = land("sx")
            dma(sc_xfm.rearrange("(dc p) s -> p dc s", p=128)
                [:, :, st * STW:(st + 1) * STW],
                xb[:].rearrange("p (dc t) -> p dc t", t=STW),
                waits=wl, defs=[l_x])
            xsl = xb[:].rearrange("p (dc t) -> p dc t", t=STW)
            kv_lms = []
            for tc in range(4):
                for half in range(2):
                    l_mm = land("kp")
                    mml = []
                    for dc in range(8):
                        mml.append(lambda e, dc=dc, tc=tc, half=half,
                                    xsl=xsl: e.matmul(
                            p_w0[:, 0:512],
                            xsl[:, dc, tc * CH:(tc + 1) * CH],
                            wab["k"][:, dc * D + half * 512: dc * D + (half + 1) * 512],
                            start=(dc == 0), stop=(dc == 7)))
                    for dc in range(8):
                        mml.append(lambda e, dc=dc, tc=tc, half=half,
                                    xsl=xsl: e.matmul(
                            p_w1[:, 0:512],
                            xsl[:, dc, tc * CH:(tc + 1) * CH],
                            wab["v"][:, dc * D + half * 512: dc * D + (half + 1) * 512],
                            start=(dc == 0), stop=(dc == 7)))
                    mmw = [l_x, l_wcast["k"], l_wcast["v"], l_km]
                    if kv_lms:
                        mmw += [kv_lms[-1][0], kv_lms[-1][1]]
                    else:
                        if prev_kf[0]:
                            mmw.append(prev_kf[0])
                        if last_supd[0]:
                            mmw.append(last_supd[0])
                    tmms(mml, waits=mmw, defs=[l_mm])
                    l_e0 = land("ke0")
                    l_e1 = land("ke")
                    vop(lambda e: e.tensor_scalar(elu1[:, 0:512], p_w0[:, 0:512],
                                                  0.0, None, ALU.min),
                        waits=[l_mm], defs=[l_e0])
                    sop(lambda e: e.activation(elu1[:, 0:512], elu1[:, 0:512],
                                               AF.Exp), waits=[l_e0])
                    sop(lambda e: e.activation(elu2[:, 0:512], p_w0[:, 0:512],
                                               AF.Relu), defs=[l_e1])
                    l_k = land("kk")

                    def mk_k(e, tc=tc, half=half, st=st):
                        e.tensor_tensor(elu1[:, 0:512], elu1[:, 0:512],
                                        elu2[:, 0:512], ALU.add)
                        return e.tensor_scalar(
                            kmtm[:, tc * D + half * 512: tc * D + (half + 1) * 512],
                            elu1[:, 0:512],
                            kmv[:, st * 4 + tc: st * 4 + tc + 1], None, ALU.mult)
                    vop(mk_k, waits=[l_e1], defs=[l_k])
                    l_v = land("kv")
                    sop(lambda e, tc=tc, half=half: e.activation(
                        vtm[:, tc * D + half * 512: tc * D + (half + 1) * 512],
                        p_w1[:, 0:512], AF.Copy),
                        waits=[l_mm], defs=[l_v])
                    kv_lms.append((l_v, l_k))
            q_lms = []
            kf_lms = []
            if full and dsub >= 3:
                for oc in range(8):
                    l_mm = land("qp")
                    mml = []
                    for dc in range(8):
                        mml.append(lambda e, dc=dc, oc=oc, xsl=xsl: e.matmul(
                            p_w0[:, 0:512] if oc % 2 == 0 else p_w1[:, 0:512],
                            wab["q"][:, dc * D + oc * 128: dc * D + (oc + 1) * 128],
                            xsl[:, dc, :],
                            start=(dc == 0), stop=(dc == 7)))
                    mmw = [l_x, l_wcast["q"]]
                    if q_lms:
                        mmw.append(q_lms[-1])
                    else:
                        mmw += [kv_lms[-1][0], kv_lms[-1][1]]
                    tmms(mml, waits=mmw, defs=[l_mm])
                    l_e0 = land("qe0")
                    l_e1 = land("qe")
                    psrc = p_w0 if oc % 2 == 0 else p_w1
                    vop(lambda e, p=psrc: e.tensor_scalar(
                        elu1[:, 0:512], p[:, 0:512], 0.0, None, ALU.min),
                        waits=[l_mm], defs=[l_e0])
                    sop(lambda e: e.activation(elu1[:, 0:512], elu1[:, 0:512],
                                               AF.Exp), waits=[l_e0])
                    sop(lambda e, p=psrc: e.activation(
                        elu2[:, 0:512], p[:, 0:512], AF.Relu), defs=[l_e1])
                    l_q = land("qq")
                    vop(lambda e, oc=oc: e.tensor_tensor(
                        qfm[:, oc * STW:(oc + 1) * STW], elu1[:, 0:512],
                        elu2[:, 0:512], ALU.add),
                        waits=[l_e1], defs=[l_q])
                    q_lms.append(l_q)
                for tc in range(4):
                    l_t = land("kt")
                    dmatp(kmtm[:, tc * D:(tc + 1) * D],
                          kfm[:].rearrange("p (oc t) -> p oc t", t=STW)
                          [:, :, tc * CH:(tc + 1) * CH],
                          waits=[kv_lms[tc * 2 + 1][1]], defs=[l_t])
                    kf_lms.append(l_t)
                prev_kf[0] = kf_lms[-1]
            om_lms = []
            for tc in range(4):
                c_glob = st * 4 + tc
                l_sa = land("sa")

                def mk_sa(e):
                    for pr in range(8):
                        e.copy(
                            saug[:].rearrange("p (pr x) -> p pr x", x=65)
                            [:, pr, 0:64],
                            p_S[:, pr * 64:(pr + 1) * 64])
                    return e.copy(
                        saug[:].rearrange("p (pr x) -> p pr x", x=65)
                        [:, :, 64:65], ksum_acc[:].unsqueeze(2))
                sa_w = ([last_supd[0]] if last_supd[0] else []) + [last_ksa[0]]
                if c_glob == 0:
                    sa_w = sa_w + [l_ones]
                sop(mk_sa, waits=sa_w, defs=[l_sa])
                if full and dsub >= 4:
                    omtm_lms = []
                    for hgrp in range(4):
                        l_at = land("at")
                        mml = []
                        for hh in range(4):
                            h = hgrp * 4 + hh
                            pb = 64 * (h % 2)
                            oc = h // 2
                            mml.append((pb, lambda e, h=h, pb=pb, oc=oc,
                                        hh=hh, tc=tc: e.matmul(
                                p_at[:, hh * 128:(hh + 1) * 128],
                                kfm[pb:pb + 64,
                                    oc * STW + tc * CH: oc * STW + (tc + 1) * CH],
                                qfm[pb:pb + 64,
                                    oc * STW + tc * CH: oc * STW + (tc + 1) * CH],
                                start=True, stop=True, tile_position=(pb, 0))))
                        atw = [q_lms[-1], kf_lms[tc], l_ltm]
                        if omtm_lms:
                            atw.append(omtm_lms[-1])
                        else:
                            if prev_om[0]:
                                atw.append(prev_om[0])
                            if prev_wo[0]:
                                atw.append(prev_wo[0])
                        tmms(mml, waits=atw, defs=[l_at])
                        l_am = land("am")
                        if dsub < 4.08:
                            vop(lambda e: e.tensor_copy(atm[:], p_at[:, 0:512]),
                                waits=[l_at], defs=[l_am])
                        else:
                            vop(lambda e: e.tensor_tensor(atm[:], p_at[:, 0:512],
                                                          ltm4[:], ALU.mult),
                                waits=[l_at], defs=[l_am])
                        if dsub < 4.2:
                            omtm_lms.append(l_am)
                            continue
                        l_nx = land("nx")
                        mml2 = []
                        for hh in range(4):
                            h = hgrp * 4 + hh
                            pb = 64 * (h % 2)
                            oc = h // 2
                            pn = p_n0 if h < 8 else p_n1
                            hs = (h % 8)
                            mml2.append(lambda e, h=h, hh=hh, tc=tc, pn=pn, hs=hs:
                                        e.matmul(
                                pn[:, hs * 64:(hs + 1) * 64],
                                atm[:, hh * 128:(hh + 1) * 128],
                                vtm[:, tc * D + h * 64: tc * D + (h + 1) * 64],
                                start=True, stop=False))
                            mml2.append((pb, lambda e, h=h, pb=pb, oc=oc,
                                         tc=tc, pn=pn, hs=hs: e.matmul(
                                pn[:, hs * 64:(hs + 1) * 64],
                                qfm[pb:pb + 64,
                                    oc * STW + tc * CH: oc * STW + (tc + 1) * CH],
                                saug[pb:pb + 64,
                                     (h // 2) * 65:(h // 2) * 65 + 64],
                                start=False, stop=True, tile_position=(pb, 0))))
                            if dsub < 4.5:
                                continue
                            mml2.append(lambda e, h=h, hh=hh: e.matmul(
                                p_dn[:, h:h + 1],
                                atm[:, hh * 128:(hh + 1) * 128],
                                onescol[:],
                                start=True, stop=False,
                                skip_group_check=True))
                            mml2.append((pb, lambda e, h=h, pb=pb, oc=oc,
                                         tc=tc: e.matmul(
                                p_dn[:, h:h + 1],
                                qfm[pb:pb + 64,
                                    oc * STW + tc * CH: oc * STW + (tc + 1) * CH],
                                saug[pb:pb + 64,
                                     (h // 2) * 65 + 64:(h // 2) * 65 + 65],
                                start=False, stop=True, skip_group_check=True,
                                tile_position=(pb, 0))))
                        tmms(mml2, waits=[l_am, l_sa, l_ones], defs=[l_nx])
                        omtm_lms.append(l_nx)
                    if dsub < 4.5:
                        prev_om[0] = omtm_lms[-1]
                        last_supd_skip = True
                    l_dr = land("dr")

                    def mk_dr(e):
                        e.tensor_copy(dencol[:], p_dn[:, 0:16])
                        vdum(e)
                        e.tensor_scalar(dencol[:], dencol[:], EPS_ATT, None,
                                        ALU.add)
                        vdum(e)
                        e.reciprocal(denr[:], dencol[:])
                        vdum(e)
                        return e.tensor_scalar(dmy[:, 0:32], dmy[:, 128:160],
                                               1.0, None, ALU.mult)
                    if dsub < 4.5:
                        l_dr = None
                    else:
                        vop(mk_dr, waits=[omtm_lms[-1]], defs=[l_dr])
                    l_om = land("om")

                    def mk_om(e, tc=tc):
                        pass

                    def mk_om(e, tc=tc):
                        last = None
                        for h in range(16):
                            pn = p_n0 if h < 8 else p_n1
                            hs = h % 8
                            last = e.tensor_scalar(
                                omtm[:, tc * D + h * 64: tc * D + (h + 1) * 64],
                                pn[:, hs * 64:(hs + 1) * 64],
                                denr[:, h:h + 1], None, ALU.mult)
                        return last
                    if dsub >= 4.5:
                        vop(mk_om, waits=[l_dr], defs=[l_om])
                        prev_om[0] = l_om
                        om_lms.append((tc, l_om))
                l_su = land("su")
                mml3 = []
                for h in range(16):
                    pb = 64 * (h % 2)
                    mml3.append(lambda e, h=h, pb=pb, tc=tc, c_glob=c_glob:
                                e.matmul(
                        p_S[pb:pb + 64, (h // 2) * 64:(h // 2) * 64 + 64],
                        kmtm[:, tc * D + h * 64: tc * D + (h + 1) * 64],
                        vtm[:, tc * D + h * 64: tc * D + (h + 1) * 64],
                        start=(c_glob == 0), stop=(c_glob == NCH - 1),
                        tile_position=(0, pb)))
                    mml3.append(lambda e, h=h, pb=pb, tc=tc, c_glob=c_glob:
                                e.matmul(
                        p_ks[pb:pb + 64, (h // 2):(h // 2) + 1],
                        kmtm[:, tc * D + h * 64: tc * D + (h + 1) * 64],
                        onescol[:],
                        start=True, stop=True,
                        skip_group_check=True, tile_position=(0, pb)))
                suw = [l_sa, kv_lms[tc * 2 + 1][1], kv_lms[tc * 2 + 1][0],
                       l_ones]
                tmms(mml3, waits=suw, defs=[l_su])
                last_supd[0] = l_su
                l_ksa = land("ksa")
                vop(lambda e: e.tensor_tensor(ksum_acc[:], ksum_acc[:], p_ks,
                                              ALU.add),
                    waits=[l_su], defs=[l_ksa])
                last_ksa[0] = l_ksa
            if full and dsub >= 5:
                ot_lms = []
                for tci, l_omx in om_lms:
                    l_ot = land("ot")
                    otw = [l_omx]
                    if prev_wo[0] and not ot_lms:
                        otw.append(prev_wo[0])
                    dmatp(omtm[:, tci * D:(tci + 1) * D],
                          omfm[:].rearrange("p (oc t) -> p oc t", t=STW)
                          [:, :, tci * CH:(tci + 1) * CH],
                          waits=otw, defs=[l_ot])
                    ot_lms.append(l_ot)
                to = st - NST // 2
                for oc in range(8):
                    l_wo = land("wo")
                    mml4 = []
                    pw = p_w0 if oc % 2 == 0 else p_w1
                    for mc in range(8):
                        mml4.append(lambda e, mc=mc, oc=oc, pw=pw: e.matmul(
                            pw[:, 0:512],
                            wab["o"][:, mc * D + oc * 128: mc * D + (oc + 1) * 128],
                            omfm[:].rearrange("p (k t) -> p k t", t=STW)[:, mc, :],
                            start=(mc == 0), stop=(mc == 7)))
                    wow = ot_lms + [l_wcast["o"], l_axr_all]
                    if out_done:
                        wow.append(out_done[-1])
                    tmms(mml4, waits=wow, defs=[l_wo])
                    prev_wo[0] = l_wo
                    l_ax = land("ax")
                    axw = [last_fe[0]] if last_fe[0] else [l_axr_all]
                    dma(sc_axr[oc * 128:(oc + 1) * 128, to * STW:(to + 1) * STW],
                        axr_d[:], waits=axw, defs=[l_ax])
                    l_fe = land("fe")
                    few = [l_wo, l_ax] + ([out_done[-1]] if out_done else [])
                    vop(lambda e, pw=pw: e.tensor_tensor(
                        aout_d[:], pw[:, 0:512], axr_d[:], ALU.add),
                        waits=few, defs=[l_fe])
                    last_fe[0] = l_fe
                    l_fo = land("fo")
                    dma(aout_d[:], out_fm[oc * 128:(oc + 1) * 128,
                                          to * STW:(to + 1) * STW],
                        waits=[l_fe], defs=[l_fo])
                    out_done.append(l_fo)
            st_lm[st % 2] = last_supd[0]

        if dbg:
            dma(sc_xfm, dbg_out["d_xfm"], waits=[("ALL", "DM")])
            dma(sc_axr, dbg_out["d_axr"])
            dma(sc_wgt, dbg_out["d_wgt"])
            pass
            dma(xqfm[:], dbg_out["d_xq"])
            dma(hqfm, dbg_out["d_hq2"])
            dma(sc_wut, dbg_out["d_wut"])
            dma(sc_wdt, dbg_out["d_wdt"])

        fw = ([out_done[-1]] if out_done else []) + [("ALL", "DM"), ("ALL", "TP")]
        pl.op("sync", None, incs=[], waits=fw)

        pl.resolve({k: None for k in SEMS})
        return _emit_all(pl, block, sem, nc, dmy)


_NC_CACHE = None


def kernel(L0_attn_x, L0_attn_Wq, L0_attn_Wk, L0_attn_Wv, L0_attn_Wo,
           L0_ffn_w_gate, L0_ffn_w_up, L0_ffn_w_down):
    global _NC_CACHE
    x = np.ascontiguousarray(np.asarray(L0_attn_x, dtype=np.float32))
    wq = np.ascontiguousarray(np.asarray(L0_attn_Wq, dtype=np.float32).T)
    wk = np.ascontiguousarray(np.asarray(L0_attn_Wk, dtype=np.float32).T)
    wv = np.ascontiguousarray(np.asarray(L0_attn_Wv, dtype=np.float32).T)
    wo = np.ascontiguousarray(np.asarray(L0_attn_Wo, dtype=np.float32).T)
    wg = np.ascontiguousarray(np.asarray(L0_ffn_w_gate, dtype=np.float32))
    wu = np.ascontiguousarray(np.asarray(L0_ffn_w_up, dtype=np.float32))
    wd = np.ascontiguousarray(np.asarray(L0_ffn_w_down, dtype=np.float32))

    in_maps = []
    for c in range(NC):
        b, hf = c // 2, c % 2
        if hf == 0:
            prefix = np.zeros((TH, D), dtype=np.float32)
            km = np.concatenate([np.zeros(TH, np.float32),
                                 np.ones(TH, np.float32)])
        else:
            prefix = x[b, 0:TH]
            km = np.ones(S, np.float32)
        km = np.ascontiguousarray(km.reshape(NCH, CH).T)
        own = x[b, hf * TH:(hf + 1) * TH]
        xs = np.ascontiguousarray(np.concatenate([prefix, own], axis=0))
        xfh = np.ascontiguousarray(own.T)
        in_maps.append({
            "xs_tm": xs, "xs_fmh": xfh, "kmask": km,
            "wqt": wq, "wkt": wk, "wvt": wv, "wot": wo,
            "wg": wg, "wu": wu, "wd": wd,
        })

    if _NC_CACHE is None:
        _NC_CACHE = build_program()
    res = run_bass_kernel_spmd(_NC_CACHE, in_maps, core_ids=list(range(NC)))
    out = np.empty((B, S, D), dtype=np.float32)
    for c in range(NC):
        b, hf = c // 2, c % 2
        out[b, hf * TH:(hf + 1) * TH] = res.results[c]["out_fm"].T
    return out


# version 05
# speedup vs baseline: 1.6278x; 1.1034x over previous
import sys
sys.path.insert(0, '/opt/trn_rl_repo')
import contextlib
import numpy as np
import ml_dtypes
import concourse.bass as bass
import concourse.mybir as mybir
from concourse.bass_utils import run_bass_kernel_spmd

F32 = mybir.dt.float32
BF16 = mybir.dt.bfloat16
AF = mybir.ActivationFunctionType
ALU = mybir.AluOpType

B, S, D, H, DH, I = 4, 4096, 1024, 16, 64, 2816
NC = 8
TH = S // 2
CH = 128
NCH = S // CH
STW = 512
NST = S // STW
NITI = (I + 511) // 512
EPS_ATT = 1e-8
EPS_LN = 1e-5
THR = [(0.25, 0.5), (0.75, 0.5), (1.25, 0.5), (1.75, 0.5),
       (2.5, 1.0), (3.5, 1.0), (5.0, 2.0)]


class Plan:

    def __init__(self, barrier=False):
        self.ops = []
        self.lands = {}
        self.barrier = barrier

    def op(self, eng, fn=None, incs=(), waits=(), defs=()):
        if self.barrier:
            waits = tuple(waits) + tuple(
                ("ALL", s) for s in ("DM", "TP", "V", "SC", "G", "T"))
        self.ops.append((eng, fn, tuple(incs), tuple(waits), tuple(defs)))

    def resolve(self, sems):
        counts = {k: 0 for k in sems}
        pre = []
        for eng, fn, incs, waits, defs in self.ops:
            pre.append(dict(counts))
            for sname, n in incs:
                counts[sname] += n
            for dname in defs:
                assert incs, f"landmark {dname} on op with no incs"
                self.lands[dname] = (incs[-1][0], counts[incs[-1][0]])
        self.resolved = []
        for i, (eng, fn, incs, waits, defs) in enumerate(self.ops):
            out = []
            for w in waits:
                if isinstance(w, tuple) and w[0] == "ALL":
                    if pre[i][w[1]] > 0:
                        out.append((w[1], pre[i][w[1]]))
                else:
                    assert w in self.lands, f"wait on undefined landmark {w}"
                    out.append(self.lands[w])
            self.resolved.append(out)

    def emit(self, eng_name, handle, sems, flush=None):
        for i, (eng, fn, incs, waits, defs) in enumerate(self.ops):
            if eng != eng_name:
                continue
            byname = {}
            for sname, val in self.resolved[i]:
                byname[sname] = max(byname.get(sname, 0), val)
            for sname, val in byname.items():
                handle.wait_ge(sems[sname], val)
            inst = fn(handle) if fn is not None else None
            if incs:
                if flush is not None and fn is not None:
                    inst = flush(handle)
                elif inst is None:
                    inst = handle.engine_nop()
                for sname, n in incs:
                    inst = inst.then_inc(sems[sname], n)


def ceil_div(a, b):
    return (a + b - 1) // b


def _emit_all(pl, block, sem, nc, dmy):
    ALU_ = mybir.AluOpType

    @block.sync
    def _(sync):
        pl.emit("sync", sync, sem)

    @block.vector
    def _(vector):
        pl.emit("vector", vector, sem,
                flush=lambda e: e.tensor_scalar(
                    dmy[:, 224:256], dmy[:, 128:160], 1.0, None, ALU_.mult))

    @block.scalar
    def _(scalar):
        pl.emit("scalar", scalar, sem,
                flush=lambda e: e.copy(dmy[:, 160:176], dmy[:, 128:144]))

    @block.gpsimd
    def _(gpsimd):
        pl.emit("gpsimd", gpsimd, sem,
                flush=lambda e: e.tensor_copy(dmy[:, 192:208], dmy[:, 144:160]))

    @block.tensor
    def _(tensor):
        pl.emit("tensor", tensor, sem, flush=(pl.tensor_flush if pl.use_tflush else None))
    return nc


def build_program(phases='ABCD', dsub=9, dbg=False, csub=9):
    nc = bass.Bass("TRN2", target_bir_lowering=False,
                   detect_race_conditions=False)

    xs_tm = nc.dram_tensor("xs_tm", [S, D], F32, kind="ExternalInput").ap()
    xs_fmh = nc.dram_tensor("xs_fmh", [D, TH], F32, kind="ExternalInput").ap()
    kmask = nc.dram_tensor("kmask", [CH, NCH], F32, kind="ExternalInput").ap()
    wqt = nc.dram_tensor("wqt", [D, D], F32, kind="ExternalInput").ap()
    wkt = nc.dram_tensor("wkt", [D, D], F32, kind="ExternalInput").ap()
    wvt = nc.dram_tensor("wvt", [D, D], F32, kind="ExternalInput").ap()
    wot = nc.dram_tensor("wot", [D, D], F32, kind="ExternalInput").ap()
    wg = nc.dram_tensor("wg", [I, D], F32, kind="ExternalInput").ap()
    wu = nc.dram_tensor("wu", [I, D], F32, kind="ExternalInput").ap()
    wd = nc.dram_tensor("wd", [D, I], F32, kind="ExternalInput").ap()
    out_fm = nc.dram_tensor("out_fm", [D, TH], F32, kind="ExternalOutput").ap()
    sc_xfm = nc.dram_tensor("sc_xfm", [D, S], BF16, kind="Internal").ap()
    sc_wgt = nc.dram_tensor("sc_wgt", [D, I], BF16, kind="Internal").ap()
    sc_wut = nc.dram_tensor("sc_wut", [D, I], BF16, kind="Internal").ap()
    sc_wdt = nc.dram_tensor("sc_wdt", [I, D], BF16, kind="Internal").ap()
    sc_axr = nc.dram_tensor("sc_axr", [D, TH], F32, kind="Internal").ap()
    sc_hqf = nc.dram_tensor("sc_hqf", [I, TH], BF16, kind="Internal").ap()
    dbg_out = {}
    if dbg:
        dbg_out["d_xfm"] = nc.dram_tensor("d_xfm", [D, S], BF16, kind="ExternalOutput").ap()
        dbg_out["d_axr"] = nc.dram_tensor("d_axr", [D, TH], F32, kind="ExternalOutput").ap()
        dbg_out["d_wgt"] = nc.dram_tensor("d_wgt", [D, I], BF16, kind="ExternalOutput").ap()
        dbg_out["d_hqf"] = nc.dram_tensor("d_hqf", [I, TH], BF16, kind="ExternalOutput").ap()
        dbg_out["d_xq"] = nc.dram_tensor("d_xq", [128, 8 * TH], BF16, kind="ExternalOutput").ap()
        dbg_out["d_diag"] = nc.dram_tensor("d_diag", [128, 512], F32, kind="ExternalOutput").ap()
        for _i in range(8):
            dbg_out[f"d_s{_i}"] = nc.dram_tensor(f"d_s{_i}", [128, 512], F32, kind="ExternalOutput").ap()
        dbg_out["d_wut"] = nc.dram_tensor("d_wut", [D, I], BF16, kind="ExternalOutput").ap()
        dbg_out["d_wdt"] = nc.dram_tensor("d_wdt", [I, D], BF16, kind="ExternalOutput").ap()
        dbg_out["d_hq2"] = nc.dram_tensor("d_hq2", [128, 22 * TH], BF16, kind="ExternalOutput").ap()
        dbg_out["d_sil"] = nc.dram_tensor("d_sil", [128, 512], F32, kind="ExternalOutput").ap()
        dbg_out["d_hbb"] = nc.dram_tensor("d_hbb", [128, 512], F32, kind="ExternalOutput").ap()
        dbg_out["d_hqt"] = nc.dram_tensor("d_hqt", [128, 512], BF16, kind="ExternalOutput").ap()
        dbg_out["d_wgs"] = nc.dram_tensor("d_wgs", [128, 8 * 512], BF16, kind="ExternalOutput").ap()

    import os as _os
    pl = Plan(barrier=_os.environ.get("BARRIER", "") != "")
    SEMS = ["DM", "TP", "V", "SC", "G", "T"]
    uid = [0]

    def land(pfx):
        uid[0] += 1
        return f"{pfx}{uid[0]}"

    with contextlib.ExitStack() as stk:
        E = stk.enter_context
        sem = {name: E(nc.semaphore(name)) for name in SEMS}

        xqfm = E(nc.sbuf_tensor("xqfm", [128, 8 * TH], BF16,
                                is_transpose_dest=True))
        QW = 1024
        q_u = E(nc.sbuf_tensor("q_u", [CH, QW], F32))
        q_zs = E(nc.sbuf_tensor("q_zs", [CH, QW], BF16))
        q_acc = E(nc.sbuf_tensor("q_acc", [CH, QW], BF16))
        q_t1 = E(nc.sbuf_tensor("q_t1", [CH, QW], BF16))
        q_t2 = E(nc.sbuf_tensor("q_t2", [CH, QW], BF16))
        q_out = E(nc.sbuf_tensor("q_out", [CH, QW], BF16))
        q_bm = E(nc.sbuf_tensor("q_bm", [CH, QW // 16], F32))
        q_s = E(nc.sbuf_tensor("q_s", [CH, QW // 16], F32))
        q_rs = E(nc.sbuf_tensor("q_rs", [CH, QW // 16], F32))
        sumc = E(nc.sbuf_tensor("sumc", [CH, 12], F32))
        mrc = E(nc.sbuf_tensor("mrc", [CH, 8], F32))
        kmv = E(nc.sbuf_tensor("kmv", [CH, NCH], F32))
        onescol = E(nc.sbuf_tensor("onescol", [128, 1], BF16))
        iocol = E(nc.sbuf_tensor("iocol", [128, 1], F32))
        iof = E(nc.sbuf_tensor("iof", [128, 128], F32))
        ltm4 = E(nc.sbuf_tensor("ltm4", [128, 512], BF16))
        dencol = E(nc.sbuf_tensor("dencol", [128, 16], F32))
        dmy = E(nc.sbuf_tensor("dmy", [128, 256], F32))
        ksum_acc = E(nc.sbuf_tensor("ksum_acc", [128, 8], F32))
        denr = E(nc.sbuf_tensor("denr", [128, 16], F32))
        g_f32 = E(nc.sbuf_tensor("g_f32", [128, 7104], F32))
        g_bf = E(nc.sbuf_tensor("g_bf", [128, 66624], BF16,
                                is_transpose_dest=True))

        class Carver:
            def __init__(self, base_ap):
                self.ap = base_ap
                self.off = 0

            def take(self, cols):
                cols_al = (cols + 63) & ~63
                a = self.ap[:, self.off:self.off + cols]
                self.off += cols_al
                assert self.off <= self.ap.shape[1], "arena overflow"
                return a

        cf = Carver(g_f32[:])
        cb = Carver(g_bf[:])
        xbuf = [cf.take(D) for _ in range(2)]
        trash = cf.take(D)
        xn_tm = cf.take(D)
        wbuf = [cf.take(1024) for _ in range(2)]
        xt_tm = cb.take(D)
        xt_fm = cb.take(D)
        wq_tm = cb.take(1024)
        wq_fm = cb.take(1024)
        cf = Carver(g_f32[:])
        cb = Carver(g_bf[:])
        sil = cf.take(512)
        hbb = cf.take(512)
        axr = cf.take(512)
        aout = cf.take(512)
        dbg_p = cf.take(512)
        hqp = cb.take(4 * 128)
        wgs = cb.take(8 * 512)
        wus = cb.take(8 * 512)
        hq_tm = cb.take(512)
        hqfm = cb.take(22 * TH)
        wds = cb.take(22 * 128)
        cf = Carver(g_f32[:])
        cb = Carver(g_bf[:])
        wfb = [cf.take(D) for _ in range(2)]
        elu1 = cf.take(512)
        elu2 = cf.take(512)
        axr_d = cf.take(512)
        aout_d = cf.take(512)
        kfm = cb.take(8 * STW)
        omfm = cb.take(8 * STW)
        xfmst = [cb.take(8 * STW) for _ in range(2)]
        wab = {w: cb.take(8 * D) for w in ("q", "k", "v", "o")}
        kmtm = cb.take(4 * D)
        vtm = cb.take(4 * D)
        qfm = cb.take(8 * STW)
        saug = cb.take(8 * 65)
        atm = cb.take(512)
        omtm = cb.take(4 * D)
        p_S = E(nc.psum_tensor("p_S", [128, 512], F32))
        p_misc = E(nc.psum_tensor("p_misc", [128, 512], F32))
        p_ks = p_misc[:, 0:8]
        p_dn = p_misc[:, 16:32]
        p_fl = p_misc[0:1, 256:257]
        p_at = E(nc.psum_tensor("p_at", [128, 512], F32))
        p_n0 = E(nc.psum_tensor("p_n0", [128, 512], F32))
        p_n1 = E(nc.psum_tensor("p_n1", [128, 512], F32))
        p_w0 = E(nc.psum_tensor("p_w0", [128, 512], F32))
        p_w1 = E(nc.psum_tensor("p_w1", [128, 512], F32))

        block = E(nc.Block())

        dmy_i = [0]

        def vdum(e):
            i = dmy_i[0] % 4
            dmy_i[0] += 1
            e.tensor_scalar(dmy[:, 32 * i:32 * i + 32], dmy[:, 128:160],
                            1.0, None, ALU.mult)

        def _tflush(e):
            return e.matmul(p_fl, onescol[:], onescol[:], start=True,
                            stop=True, skip_group_check=True)
        pl.tensor_flush = _tflush
        import os as _os
        pl.use_tflush = _os.environ.get("NO_TFLUSH", "") == ""

        def dma(src_ap, dst_ap, waits=(), defs=()):
            pl.op("sync", lambda e, s=src_ap, d=dst_ap: e.dma_start(out=d, in_=s),
                  incs=[("DM", 16)], waits=tuple(waits) + (("ALL", "TP"),),
                  defs=defs)

        def dmatp(src_ap, dst_ap, waits=(), defs=()):
            pl.op("sync", lambda e, s=src_ap, d=dst_ap: e.dma_start_transpose(d, s),
                  incs=[("TP", 16)],
                  waits=tuple(waits) + (("ALL", "DM"), ("ALL", "TP")), defs=defs)

        def vop(fn, waits=(), defs=()):
            pl.op("vector", fn, incs=[("V", 1)], waits=waits, defs=defs)

        def sop(fn, waits=(), defs=()):
            if not first_eng["scalar"]:
                first_eng["scalar"] = True
                waits = tuple(waits) + (l_dmy,)
            pl.op("scalar", fn, incs=[("SC", 1)], waits=waits, defs=defs)

        def gop(fn, waits=(), defs=()):
            if not first_eng["gpsimd"]:
                first_eng["gpsimd"] = True
                waits = tuple(waits) + (l_dmy,)
            pl.op("gpsimd", fn, incs=[("G", 1)], waits=waits, defs=defs)

        def top(fn, waits=(), defs=()):
            pl.op("tensor", fn, incs=[("T", 1)], waits=waits, defs=defs)

        te_state = {"base": None}

        def tmms(mmlist, waits=(), defs=()):
            def go(e):
                last = None
                for item in mmlist:
                    if isinstance(item, tuple):
                        base, f = item
                        if te_state["base"] is not None and \
                                te_state["base"] != base:
                            e.ldweights(ltm4[:, 0:128])
                        te_state["base"] = base
                        last = f(e)
                    else:
                        te_state["base"] = None
                        last = item(e)
                return last
            pl.op("tensor", go, incs=[("T", 1)], waits=waits, defs=defs)

        q_state = {"zs": None, "dst": None}

        def emit_quant2(src_fn, w, out_dst_fn, waits=(), dst_free=None):
            nb = w // 16
            src = src_fn()
            outd = out_dst_fn()
            def mk_scales(e):
                e.tensor_reduce(
                    q_bm[:, 0:nb], src.rearrange("p (b k) -> p b k", k=16),
                    axis=mybir.AxisListType.X, op=ALU.max,
                    apply_absolute_value=True)
                vdum(e)
                e.tensor_scalar(q_s[:, 0:nb], q_bm[:, 0:nb],
                                6e-12, 1.0 / 6.0, ALU.max, ALU.mult)
                vdum(e)
                e.reciprocal(q_rs[:, 0:nb], q_s[:, 0:nb])
                vdum(e)
                return e.tensor_scalar(dmy[:, 0:32], dmy[:, 128:160], 1.0,
                                       None, ALU.mult)
            vop(mk_scales, waits=waits)
            rs_b = q_rs[:, 0:nb].unsqueeze(2).broadcast_to([128, nb, 16])
            s_b = q_s[:, 0:nb].unsqueeze(2).broadcast_to([128, nb, 16])
            l_u = land("qu")
            uw = [q_state["zs"]] if q_state["zs"] else []
            vop(lambda e: e.tensor_tensor(q_u[:, 0:w], src, rs_b, ALU.mult),
                waits=uw, defs=[l_u])
            l_zs = land("qzs")
            sop(lambda e: e.activation(q_zs[:, 0:w], q_u[:, 0:w], AF.Sign),
                waits=[l_u])
            sop(lambda e: e.activation(q_u[:, 0:w], q_u[:, 0:w], AF.Abs),
                defs=[l_zs])
            q_state["zs"] = l_zs
            t0, w0 = THR[0]
            t1x, w1x = THR[1]
            vop(lambda e: e.tensor_scalar(
                q_t1[:, 0:w], q_u[:, 0:w], t0, w0, ALU.is_gt, ALU.mult),
                waits=[l_zs])
            vop(lambda e: e.tensor_scalar(
                q_t2[:, 0:w], q_u[:, 0:w], t1x, w1x, ALU.is_gt, ALU.mult))
            vop(lambda e: e.tensor_tensor(q_acc[:, 0:w], q_t1[:, 0:w],
                                          q_t2[:, 0:w], ALU.add))
            for tj, wj in THR[2:]:
                vop(lambda e, tj=tj, wj=wj: e.tensor_scalar(
                    q_t1[:, 0:w], q_u[:, 0:w], tj, wj, ALU.is_gt, ALU.mult))
                vop(lambda e: e.tensor_tensor(q_acc[:, 0:w], q_acc[:, 0:w],
                                              q_t1[:, 0:w], ALU.add))
            vop(lambda e: e.tensor_tensor(q_t2[:, 0:w], q_acc[:, 0:w],
                                          q_zs[:, 0:w], ALU.mult))
            l_out = land("qo")
            ow = [dst_free] if dst_free else []
            vop(lambda e: e.tensor_tensor(outd, q_t2[:, 0:w], s_b, ALU.mult),
                waits=ow, defs=[l_out])
            return l_out

        l_dmy = land("dmy0")
        vop(lambda e: e.memset(dmy[:], 0.0), defs=[l_dmy])
        first_eng = {"scalar": False, "gpsimd": False}
        l_km = land("km")
        dma(kmask, kmv[:, 0:NCH],
            defs=[l_km])
        l_iota = land("iota")
        gop(lambda e: e.iota(iocol[:], pattern=[[0, 1]], base=0,
                             channel_multiplier=1,
                             allow_small_or_imprecise_dtypes=True))
        gop(lambda e: e.iota(iof[:], pattern=[[1, 128]], base=0,
                             channel_multiplier=0,
                             allow_small_or_imprecise_dtypes=True),
            defs=[l_iota])
        l_ones = land("ones")
        vop(lambda e: e.memset(onescol[:], 1.0), defs=[l_ones])
        l_ltm = land("ltm")
        def mk_ltm(e):
            last = None
            for r in range(4):
                last = e.tensor_scalar(ltm4[:, r * 128:(r + 1) * 128], iof[:],
                                       iocol[:], None, ALU.is_ge)
            return last
        vop(mk_ltm, waits=[l_iota], defs=[l_ltm])

        x_done_lm = [None] * NCH
        last_mr = [None]
        last_xtp = [None]
        last_xq_tp = [None]
        for c in range(NCH):
            buf = xbuf[c % 2]
            wl = [] if c < 2 else [x_done_lm[c - 2]]
            l_ld = land("xa")
            dma(xs_tm[c * CH:(c + 1) * CH, :], buf[:], waits=wl, defs=[l_ld])
            def mk_mr(e, b=buf):
                e.bn_stats(sumc[:, 0:6], b[:, 0:512])
                e.bn_stats(sumc[:, 6:12], b[:, 512:1024])
                e.bn_aggr(mrc[:, 4:6], sumc[:, 0:6])
                e.bn_aggr(mrc[:, 6:8], sumc[:, 6:12])
                vdum(e)
                e.tensor_tensor(mrc[:, 0:1], mrc[:, 4:5], mrc[:, 6:7], ALU.add)
                e.tensor_tensor(mrc[:, 3:4], mrc[:, 4:5], mrc[:, 6:7], ALU.subtract)
                e.tensor_tensor(mrc[:, 1:2], mrc[:, 5:6], mrc[:, 7:8], ALU.add)
                e.tensor_scalar(mrc[:, 0:1], mrc[:, 0:1], -0.5, None, ALU.mult)
                e.tensor_tensor(mrc[:, 3:4], mrc[:, 3:4], mrc[:, 3:4], ALU.mult)
                e.tensor_scalar(mrc[:, 1:2], mrc[:, 1:2], 0.5, None, ALU.mult)
                e.tensor_scalar(mrc[:, 3:4], mrc[:, 3:4], 0.25, EPS_LN,
                                ALU.mult, ALU.add)
                vdum(e)
                e.tensor_tensor(mrc[:, 3:4], mrc[:, 1:2], mrc[:, 3:4], ALU.add)
                vdum(e)
                return e.reciprocal(mrc[:, 1:2], mrc[:, 3:4])
            l_mr = land("lm")
            vop(mk_mr, waits=[l_ld] + ([last_mr[0]] if last_mr[0] else []),
                defs=[l_mr])
            last_mr[0] = l_mr
            l_r = land("r")
            sop(lambda e: e.activation(mrc[:, 1:2], mrc[:, 1:2], AF.Sqrt),
                waits=[l_mr], defs=[l_r])
            l_xt = land("xt")

            def mk_xt(e, b=buf):
                e.tensor_tensor(mrc[:, 2:3], mrc[:, 0:1], mrc[:, 1:2], ALU.mult)
                vdum(e)
                return e.tensor_scalar(xt_tm[:], b[:], mrc[:, 1:2], mrc[:, 2:3],
                                       ALU.mult, ALU.add)
            xtw = [l_r] + ([last_xtp[0]] if last_xtp[0] else [])
            vop(mk_xt, waits=xtw, defs=[l_xt])
            l_tp = land("tp")
            dmatp(xt_tm[:], xt_fm[:].rearrange("p (dc t) -> p dc t", t=CH),
                  waits=[l_xt], defs=[l_tp])
            last_xtp[0] = l_tp
            l_xf = land("xf")
            dma(xt_fm[:].rearrange("p (dc t) -> p dc t", t=CH),
                sc_xfm.rearrange("(dc p) s -> p dc s", p=128)
                [:, :, c * CH:(c + 1) * CH],
                waits=[l_tp], defs=[l_xf])
            if c >= NCH // 2:
                tc = c - NCH // 2
                l_xn = land("xn")

                def mk_xn(e, b=buf):
                    return e.tensor_scalar(xn_tm[:], b[:], mrc[:, 1:2],
                                           mrc[:, 2:3], ALU.mult, ALU.add)
                vop(mk_xn, waits=[l_r], defs=[l_xn])
                l_q = emit_quant2(lambda: xn_tm[:], D,
                                  lambda: q_out[:, 0:D], waits=[l_xn],
                                  dst_free=last_xq_tp[0])
                l_tq = land("tq")
                dmatp(q_out[:, 0:D],
                      xqfm[:].rearrange("p (dc t) -> p dc t", t=TH)
                      [:, :, tc * CH:(tc + 1) * CH],
                      waits=[l_q], defs=[l_tq])
                last_xq_tp[0] = l_tq
                x_done_lm[c] = l_tq
            else:
                x_done_lm[c] = l_xf

        if 'B' not in phases:
            if dbg:
                dma(sc_xfm, dbg_out["d_xfm"], waits=[("ALL", "DM")])
            pl.op("sync", None, incs=[],
                  waits=[("ALL", "DM"), ("ALL", "TP")])
            pl.resolve({k: None for k in SEMS})
            return _emit_all(pl, block, sem, nc, dmy)
        wstate = {"tp": None, "prev": [None, None], "i": 0}

        def wprep(w_in, row0, col0, width, sc_out):
            i = wstate["i"]
            wstate["i"] += 1
            buf = wbuf[i % 2]
            wl = [] if wstate["prev"][i % 2] is None else [wstate["prev"][i % 2]]
            l_ld = land("wl")
            dma(w_in[row0:row0 + CH, col0:col0 + width], buf[:, 0:width],
                waits=wl, defs=[l_ld])
            l_q = emit_quant2(lambda b=buf: b[:, 0:width], width,
                              lambda: wq_tm[:, 0:width], waits=[l_ld],
                              dst_free=wstate["tp"])
            l_tp = land("wt")
            ncc = width // 128
            dmatp(wq_tm[:, 0:width],
                  wq_fm[:].rearrange("p (cc t) -> p cc t", t=CH)[:, 0:ncc, :],
                  waits=[l_q], defs=[l_tp])
            wstate["tp"] = l_tp
            l_st = land("ws")
            dma(wq_fm[:].rearrange("p (cc t) -> p cc t", t=CH)[:, 0:ncc, :],
                sc_out.rearrange("(cc p) n -> p cc n", p=128)
                [:, col0 // 128:col0 // 128 + ncc, row0:row0 + CH],
                waits=[l_tp], defs=[l_st])
            wstate["prev"][i % 2] = l_st
            return l_st

        for c in range(I // CH):
            wprep(wg, c * CH, 0, D, sc_wgt)
        for c in range(I // CH):
            wprep(wu, c * CH, 0, D, sc_wut)
        for c in range(D // CH):
            wprep(wd, c * CH, 0, 1024, sc_wdt)
            wprep(wd, c * CH, 1024, 1024, sc_wdt)
            l_wdone = wprep(wd, c * CH, 2048, 768, sc_wdt)

        if 'C' not in phases:
            if dbg:
                dma(sc_xfm, dbg_out["d_xfm"], waits=[("ALL", "DM")])
                dma(sc_wgt, dbg_out["d_wgt"])
            pl.op("sync", None, incs=[],
                  waits=[("ALL", "DM"), ("ALL", "TP")])
            pl.resolve({k: None for k in SEMS})
            return _emit_all(pl, block, sem, nc, dmy)
        hq_done = []
        hq_cp = [None]
        hq_st = [None]
        pipec = [None]
        last_wg_lm = None
        _ctiles = int(_os.environ.get('CTILES', '999'))
        if dbg:
            l_dg0 = land("dg0")
            dma(sc_wgt.rearrange("(dc p) n -> p dc n", p=128)[:, :, 0:512],
                wgs[:].rearrange("p (dc n) -> p dc n", n=512),
                waits=[l_wdone], defs=[l_dg0])
            l_dgm = land("dgm")
            xsl0 = xqfm[:].rearrange("p (dc t) -> p dc t", t=TH)
            import os as _os
            _dsel = _os.environ.get('DIAGSEL', '1')
            _dml = []
            if _dsel == '1':
                _dml.append(lambda e: e.matmul(
                    p_w0[:, 0:512], xsl0[:, 0, 0:128],
                    wgs[:].rearrange("p (dc n) -> p dc n", n=512)[:, 0, :],
                    start=True, stop=True))
            else:
                for dc in range(8):
                    _dml.append(lambda e, dc=dc: e.matmul(
                        p_w0[:, 0:512], xsl0[:, dc, 0:128],
                        wgs[:].rearrange("p (dc n) -> p dc n", n=512)[:, dc, :],
                        start=(dc == 0), stop=(dc == 7)))
                if _dsel == '16':
                    for dc in range(8):
                        _dml.append(lambda e, dc=dc: e.matmul(
                            p_w1[:, 0:512], xsl0[:, dc, 0:128],
                            wus[:].rearrange("p (dc n) -> p dc n", n=512)[:, dc, :],
                            start=(dc == 0), stop=(dc == 7)))
            tmms(_dml, waits=[l_dg0], defs=[l_dgm])
            l_dgc = land("dgc")
            sop(lambda e: e.activation(sil[:, 0:512], p_w0[:, 0:512], AF.Copy),
                waits=[l_dgm], defs=[l_dgc])
            dma(sil[:, 0:512], dbg_out["d_diag"], waits=[l_dgc])
        for it in range(NITI):
            isz = min(512, I - it * 512)
            l_wg = land("fg")
            wl = [l_wdone] if it == 0 else [hq_done[-1]]
            dma(sc_wgt.rearrange("(dc p) n -> p dc n", p=128)
                [:, :, it * 512:it * 512 + isz],
                wgs[:].rearrange("p (dc n) -> p dc n", n=512)[:, :, 0:isz],
                waits=wl, defs=[l_wg])
            l_wu = land("fu")
            dma(sc_wut.rearrange("(dc p) n -> p dc n", p=128)
                [:, :, it * 512:it * 512 + isz],
                wus[:].rearrange("p (dc n) -> p dc n", n=512)[:, :, 0:isz],
                defs=[l_wu])
            last_wg_lm = l_wu
            for tc in range(TH // CH):
                if it * 16 + tc >= _ctiles:
                    continue
                xsl = xqfm[:].rearrange("p (dc t) -> p dc t", t=TH)
                l_mm = land("fm")
                gml = []
                for dc in range(8):
                    gml.append(lambda e, dc=dc, tc=tc, isz=isz, xsl=xsl: e.matmul(
                        p_w0[:, 0:isz],
                        xsl[:, dc, tc * CH:(tc + 1) * CH],
                        wgs[:].rearrange("p (dc n) -> p dc n", n=512)[:, dc, 0:isz],
                        start=(dc == 0), stop=(dc == 7)))
                for dc in range(8):
                    gml.append(lambda e, dc=dc, tc=tc, isz=isz, xsl=xsl: e.matmul(
                        p_w1[:, 0:isz],
                        xsl[:, dc, tc * CH:(tc + 1) * CH],
                        wus[:].rearrange("p (dc n) -> p dc n", n=512)[:, dc, 0:isz],
                        start=(dc == 0), stop=(dc == 7)))
                mmw = [l_wg, l_wu]
                if pipec[0] is not None:
                    mmw += list(pipec[0])
                tmms(gml, waits=mmw, defs=[l_mm])
                l_sil = land("fs")
                sop(lambda e, isz=isz: e.activation(sil[:, 0:isz], p_w0[:, 0:isz],
                                                    AF.Silu),
                    waits=[l_mm], defs=[l_sil])
                if csub < 2:
                    hq_done.append(l_sil)
                    continue
                l_h = land("fh")
                vop(lambda e, isz=isz: e.tensor_tensor(
                    hbb[:, 0:isz], sil[:, 0:isz], p_w1[:, 0:isz], ALU.mult),
                    waits=[l_sil], defs=[l_h])
                pipec[0] = (l_sil, l_h)
                l_q = emit_quant2(lambda isz=isz: hbb[:, 0:isz], isz,
                                  lambda isz=isz: hq_tm[:, 0:isz], waits=[l_h],
                                  dst_free=hq_done[-1] if hq_done else None)
                _slots = {(0, 0): 0, (0, 15): 2, (1, 0): 3,
                          (2, 0): 4, (3, 0): 5, (4, 1): 6, (5, 0): 7}
                if dbg and (it, tc) in _slots:
                    dma(sil[:], dbg_out[f"d_s{_slots[(it, tc)]}"],
                        waits=[l_sil])
                if dbg and (it, tc) == (0, 0):
                    l_pd = land("pd")
                    vop(lambda e: e.tensor_copy(aout[:], p_w0[:, 0:512]),
                        waits=[l_sil], defs=[l_pd])
                    dma(aout[:], dbg_out["d_s1"], waits=[l_pd])
                if dbg and it == 4 and tc == 0:
                    dma(sil[:], dbg_out["d_sil"], waits=[l_sil])
                    dma(hbb[:], dbg_out["d_hbb"], waits=[l_h])
                    dma(hq_tm[:], dbg_out["d_hqt"], waits=[l_q])
                    dma(wgs[:], dbg_out["d_wgs"], waits=[l_mm])
                l_tp2 = land("ftp")
                ncc = isz // 128
                tpw = [l_q] + ([hq_cp[0]] if hq_cp[0] else [])
                dmatp(hq_tm[:, 0:isz],
                      hqp.rearrange("p (cc t) -> p cc t", t=CH)[:, 0:ncc, :],
                      waits=tpw, defs=[l_tp2])
                l_tq = land("ft")
                vop(lambda e, it=it, tc=tc, ncc=ncc: e.tensor_copy(
                    hqfm.rearrange("p (cc t) -> p cc t", t=TH)
                    [:, it * 4:it * 4 + ncc, tc * CH:(tc + 1) * CH],
                    hqp.rearrange("p (cc t) -> p cc t", t=CH)[:, 0:ncc, :]),
                    waits=[l_tp2], defs=[l_tq])
                hq_cp[0] = l_tq
                hq_done.append(l_tq)
        if csub < 2 or _ctiles < 96:
            if dbg:
                dma(xqfm[:], dbg_out["d_xq"], waits=[("ALL", "DM"), ("ALL", "TP")])
            pl.op("sync", None, incs=[], waits=[("ALL", "DM"), ("ALL", "TP")])
            pl.resolve({k: None for k in SEMS})
            return _emit_all(pl, block, sem, nc, dmy)
        axr_done = []
        ffn_fe = [None]
        l_hq_all = hq_done[-1]
        last_wd = [None]
        for oc in range(8):
            l_wdl = land("dwl")
            wlw = [last_wd[0]] if last_wd[0] else [l_hq_all]
            dma(sc_wdt.rearrange("(ic p) o -> p ic o", p=128)
                [:, :, oc * 128:(oc + 1) * 128],
                wds.rearrange("p (ic o) -> p ic o", o=128),
                waits=wlw, defs=[l_wdl])
            for ts4 in range(TH // 512):
                l_mm = land("dm")
                mml = []
                for ic in range(22):
                    mml.append(lambda e, ic=ic, oc=oc, ts4=ts4: e.matmul(
                        p_w0[:, 0:512],
                        wds.rearrange("p (ic o) -> p ic o", o=128)[:, ic, :],
                        hqfm.rearrange("p (cc t) -> p cc t", t=TH)
                        [:, ic, ts4 * 512:(ts4 + 1) * 512],
                        start=(ic == 0), stop=(ic == 21)))
                mmw = [l_wdl]
                if axr_done:
                    mmw.append(axr_done[-1])
                tmms(mml, waits=mmw, defs=[l_mm])
                l_x = land("dx")
                axw = [ffn_fe[0]] if ffn_fe[0] else []
                dma(xs_fmh[oc * 128:(oc + 1) * 128, ts4 * 512:(ts4 + 1) * 512],
                    axr[:], waits=axw, defs=[l_x])
                l_e = land("de")
                few = [l_mm, l_x] + ([axr_done[-1]] if axr_done else [])
                vop(lambda e: e.tensor_tensor(aout[:], p_w0[:, 0:512], axr[:],
                                              ALU.add),
                    waits=few, defs=[l_e])
                ffn_fe[0] = l_e
                l_s = land("ds")
                dma(aout[:], sc_axr[oc * 128:(oc + 1) * 128,
                                    ts4 * 512:(ts4 + 1) * 512],
                    waits=[l_e], defs=[l_s])
                axr_done.append(l_s)
            last_wd[0] = l_mm
        l_axr_all = axr_done[-1]

        if 'D' not in phases:
            if dbg:
                dma(sc_xfm, dbg_out["d_xfm"], waits=[("ALL", "DM")])
                dma(sc_axr, dbg_out["d_axr"])
                dma(sc_wgt, dbg_out["d_wgt"])
                dma(xqfm[:], dbg_out["d_xq"])
                dma(hqfm, dbg_out["d_hq2"])
            pl.op("sync", None, incs=[],
                  waits=[("ALL", "DM"), ("ALL", "TP")])
            pl.resolve({k: None for k in SEMS})
            return _emit_all(pl, block, sem, nc, dmy)
        wsrc = {"q": wqt, "k": wkt, "v": wvt, "o": wot}
        l_wcast = {}
        prev = [l_axr_all, l_axr_all]
        for wi, wname in enumerate(("q", "k", "v", "o")):
            for dc in range(8):
                idx = (wi * 8 + dc) % 2
                buf = wfb[idx]
                wl = [prev[idx]] if prev[idx] else []
                l_ld = land("awl")
                dma(wsrc[wname][dc * 128:(dc + 1) * 128, :], buf[:],
                    waits=wl, defs=[l_ld])
                l_c = land("awc")
                vop(lambda e, b=buf, wname=wname, dc=dc: e.tensor_copy(
                    wab[wname][:, dc * D:(dc + 1) * D], b[:]),
                    waits=[l_ld], defs=[l_c])
                prev[idx] = l_c
            l_wcast[wname] = prev[(wi * 8 + 7) % 2]

        if dsub < 1.5:
            pl.op("sync", None, incs=[], waits=[("ALL", "DM"), ("ALL", "TP")])
            pl.resolve({k: None for k in SEMS})
            return _emit_all(pl, block, sem, nc, dmy)
        l_ksa0 = land("ks0")
        vop(lambda e: e.memset(ksum_acc[:], 0.0), defs=[l_ksa0])
        last_ksa = [l_ksa0]
        st_lm = [None, None]
        out_done = []
        last_supd = [None]
        prev_om = [None]
        prev_kf = [None]
        prev_wo = [None]
        last_fe = [None]
        for st in range(NST):
            full = st >= NST // 2
            xb = xfmst[st % 2]
            wl = [st_lm[st % 2]] if st_lm[st % 2] else [l_axr_all]
            l_x = land("sx")
            dma(sc_xfm.rearrange("(dc p) s -> p dc s", p=128)
                [:, :, st * STW:(st + 1) * STW],
                xb[:].rearrange("p (dc t) -> p dc t", t=STW),
                waits=wl, defs=[l_x])
            xsl = xb[:].rearrange("p (dc t) -> p dc t", t=STW)
            kv_lms = []
            for tc in range(4):
                for half in range(2):
                    l_mm = land("kp")
                    mml = []
                    for dc in range(8):
                        mml.append(lambda e, dc=dc, tc=tc, half=half,
                                    xsl=xsl: e.matmul(
                            p_w0[:, 0:512],
                            xsl[:, dc, tc * CH:(tc + 1) * CH],
                            wab["k"][:, dc * D + half * 512: dc * D + (half + 1) * 512],
                            start=(dc == 0), stop=(dc == 7)))
                    for dc in range(8):
                        mml.append(lambda e, dc=dc, tc=tc, half=half,
                                    xsl=xsl: e.matmul(
                            p_w1[:, 0:512],
                            xsl[:, dc, tc * CH:(tc + 1) * CH],
                            wab["v"][:, dc * D + half * 512: dc * D + (half + 1) * 512],
                            start=(dc == 0), stop=(dc == 7)))
                    mmw = [l_x, l_wcast["k"], l_wcast["v"], l_km]
                    if kv_lms:
                        mmw += [kv_lms[-1][0], kv_lms[-1][1]]
                    else:
                        if prev_kf[0]:
                            mmw.append(prev_kf[0])
                        if last_supd[0]:
                            mmw.append(last_supd[0])
                    tmms(mml, waits=mmw, defs=[l_mm])
                    l_e0 = land("ke0")
                    l_e1 = land("ke")
                    vop(lambda e: e.tensor_scalar(elu1[:, 0:512], p_w0[:, 0:512],
                                                  0.0, None, ALU.min),
                        waits=[l_mm], defs=[l_e0])
                    sop(lambda e: e.activation(elu1[:, 0:512], elu1[:, 0:512],
                                               AF.Exp), waits=[l_e0])
                    sop(lambda e: e.activation(elu2[:, 0:512], p_w0[:, 0:512],
                                               AF.Relu), defs=[l_e1])
                    l_k = land("kk")

                    def mk_k(e, tc=tc, half=half, st=st):
                        e.tensor_tensor(elu1[:, 0:512], elu1[:, 0:512],
                                        elu2[:, 0:512], ALU.add)
                        return e.tensor_scalar(
                            kmtm[:, tc * D + half * 512: tc * D + (half + 1) * 512],
                            elu1[:, 0:512],
                            kmv[:, st * 4 + tc: st * 4 + tc + 1], None, ALU.mult)
                    vop(mk_k, waits=[l_e1], defs=[l_k])
                    l_v = land("kv")
                    sop(lambda e, tc=tc, half=half: e.activation(
                        vtm[:, tc * D + half * 512: tc * D + (half + 1) * 512],
                        p_w1[:, 0:512], AF.Copy),
                        waits=[l_mm], defs=[l_v])
                    kv_lms.append((l_v, l_k))
            q_lms = []
            kf_lms = []
            if full and dsub >= 3:
                for oc in range(8):
                    l_mm = land("qp")
                    mml = []
                    for dc in range(8):
                        mml.append(lambda e, dc=dc, oc=oc, xsl=xsl: e.matmul(
                            p_w0[:, 0:512] if oc % 2 == 0 else p_w1[:, 0:512],
                            wab["q"][:, dc * D + oc * 128: dc * D + (oc + 1) * 128],
                            xsl[:, dc, :],
                            start=(dc == 0), stop=(dc == 7)))
                    mmw = [l_x, l_wcast["q"]]
                    if q_lms:
                        mmw.append(q_lms[-1])
                    else:
                        mmw += [kv_lms[-1][0], kv_lms[-1][1]]
                    tmms(mml, waits=mmw, defs=[l_mm])
                    l_e0 = land("qe0")
                    l_e1 = land("qe")
                    psrc = p_w0 if oc % 2 == 0 else p_w1
                    vop(lambda e, p=psrc: e.tensor_scalar(
                        elu1[:, 0:512], p[:, 0:512], 0.0, None, ALU.min),
                        waits=[l_mm], defs=[l_e0])
                    sop(lambda e: e.activation(elu1[:, 0:512], elu1[:, 0:512],
                                               AF.Exp), waits=[l_e0])
                    sop(lambda e, p=psrc: e.activation(
                        elu2[:, 0:512], p[:, 0:512], AF.Relu), defs=[l_e1])
                    l_q = land("qq")
                    vop(lambda e, oc=oc: e.tensor_tensor(
                        qfm[:, oc * STW:(oc + 1) * STW], elu1[:, 0:512],
                        elu2[:, 0:512], ALU.add),
                        waits=[l_e1], defs=[l_q])
                    q_lms.append(l_q)
                for tc in range(4):
                    l_t = land("kt")
                    dmatp(kmtm[:, tc * D:(tc + 1) * D],
                          kfm[:].rearrange("p (oc t) -> p oc t", t=STW)
                          [:, :, tc * CH:(tc + 1) * CH],
                          waits=[kv_lms[tc * 2 + 1][1]], defs=[l_t])
                    kf_lms.append(l_t)
                prev_kf[0] = kf_lms[-1]
            om_lms = []
            for tc in range(4):
                c_glob = st * 4 + tc
                l_sa = land("sa")

                def mk_sa(e):
                    for pr in range(8):
                        e.copy(
                            saug[:].rearrange("p (pr x) -> p pr x", x=65)
                            [:, pr, 0:64],
                            p_S[:, pr * 64:(pr + 1) * 64])
                    return e.copy(
                        saug[:].rearrange("p (pr x) -> p pr x", x=65)
                        [:, :, 64:65], ksum_acc[:].unsqueeze(2))
                sa_w = ([last_supd[0]] if last_supd[0] else []) + [last_ksa[0]]
                if c_glob == 0:
                    sa_w = sa_w + [l_ones]
                sop(mk_sa, waits=sa_w, defs=[l_sa])
                if full and dsub >= 4:
                    omtm_lms = []
                    for hgrp in range(4):
                        l_at = land("at")
                        mml = []
                        for hh in range(4):
                            h = hgrp * 4 + hh
                            pb = 64 * (h % 2)
                            oc = h // 2
                            mml.append((pb, lambda e, h=h, pb=pb, oc=oc,
                                        hh=hh, tc=tc: e.matmul(
                                p_at[:, hh * 128:(hh + 1) * 128],
                                kfm[pb:pb + 64,
                                    oc * STW + tc * CH: oc * STW + (tc + 1) * CH],
                                qfm[pb:pb + 64,
                                    oc * STW + tc * CH: oc * STW + (tc + 1) * CH],
                                start=True, stop=True, tile_position=(pb, 0))))
                        atw = [q_lms[-1], kf_lms[tc], l_ltm]
                        if omtm_lms:
                            atw.append(omtm_lms[-1])
                        else:
                            if prev_om[0]:
                                atw.append(prev_om[0])
                            if prev_wo[0]:
                                atw.append(prev_wo[0])
                        tmms(mml, waits=atw, defs=[l_at])
                        l_am = land("am")
                        if dsub < 4.08:
                            vop(lambda e: e.tensor_copy(atm[:], p_at[:, 0:512]),
                                waits=[l_at], defs=[l_am])
                        else:
                            vop(lambda e: e.tensor_tensor(atm[:], p_at[:, 0:512],
                                                          ltm4[:], ALU.mult),
                                waits=[l_at], defs=[l_am])
                        if dsub < 4.2:
                            omtm_lms.append(l_am)
                            continue
                        l_nx = land("nx")
                        mml2 = []
                        for hh in range(4):
                            h = hgrp * 4 + hh
                            pb = 64 * (h % 2)
                            oc = h // 2
                            pn = p_n0 if h < 8 else p_n1
                            hs = (h % 8)
                            mml2.append(lambda e, h=h, hh=hh, tc=tc, pn=pn, hs=hs:
                                        e.matmul(
                                pn[:, hs * 64:(hs + 1) * 64],
                                atm[:, hh * 128:(hh + 1) * 128],
                                vtm[:, tc * D + h * 64: tc * D + (h + 1) * 64],
                                start=True, stop=False))
                            mml2.append((pb, lambda e, h=h, pb=pb, oc=oc,
                                         tc=tc, pn=pn, hs=hs: e.matmul(
                                pn[:, hs * 64:(hs + 1) * 64],
                                qfm[pb:pb + 64,
                                    oc * STW + tc * CH: oc * STW + (tc + 1) * CH],
                                saug[pb:pb + 64,
                                     (h // 2) * 65:(h // 2) * 65 + 64],
                                start=False, stop=True, tile_position=(pb, 0))))
                            if dsub < 4.5:
                                continue
                            mml2.append(lambda e, h=h, hh=hh: e.matmul(
                                p_dn[:, h:h + 1],
                                atm[:, hh * 128:(hh + 1) * 128],
                                onescol[:],
                                start=True, stop=False,
                                skip_group_check=True))
                            mml2.append((pb, lambda e, h=h, pb=pb, oc=oc,
                                         tc=tc: e.matmul(
                                p_dn[:, h:h + 1],
                                qfm[pb:pb + 64,
                                    oc * STW + tc * CH: oc * STW + (tc + 1) * CH],
                                saug[pb:pb + 64,
                                     (h // 2) * 65 + 64:(h // 2) * 65 + 65],
                                start=False, stop=True, skip_group_check=True,
                                tile_position=(pb, 0))))
                        tmms(mml2, waits=[l_am, l_sa, l_ones], defs=[l_nx])
                        omtm_lms.append(l_nx)
                    if dsub < 4.5:
                        prev_om[0] = omtm_lms[-1]
                        last_supd_skip = True
                    l_dr = land("dr")

                    def mk_dr(e):
                        e.tensor_copy(dencol[:], p_dn[:, 0:16])
                        vdum(e)
                        e.tensor_scalar(dencol[:], dencol[:], EPS_ATT, None,
                                        ALU.add)
                        vdum(e)
                        e.reciprocal(denr[:], dencol[:])
                        vdum(e)
                        return e.tensor_scalar(dmy[:, 0:32], dmy[:, 128:160],
                                               1.0, None, ALU.mult)
                    if dsub < 4.5:
                        l_dr = None
                    else:
                        vop(mk_dr, waits=[omtm_lms[-1]], defs=[l_dr])
                    l_om = land("om")

                    def mk_om(e, tc=tc):
                        pass

                    def mk_om(e, tc=tc):
                        last = None
                        for h in range(16):
                            pn = p_n0 if h < 8 else p_n1
                            hs = h % 8
                            last = e.tensor_scalar(
                                omtm[:, tc * D + h * 64: tc * D + (h + 1) * 64],
                                pn[:, hs * 64:(hs + 1) * 64],
                                denr[:, h:h + 1], None, ALU.mult)
                        return last
                    if dsub >= 4.5:
                        vop(mk_om, waits=[l_dr], defs=[l_om])
                        prev_om[0] = l_om
                        om_lms.append((tc, l_om))
                l_su = land("su")
                mml3 = []
                for h in range(16):
                    pb = 64 * (h % 2)
                    mml3.append(lambda e, h=h, pb=pb, tc=tc, c_glob=c_glob:
                                e.matmul(
                        p_S[pb:pb + 64, (h // 2) * 64:(h // 2) * 64 + 64],
                        kmtm[:, tc * D + h * 64: tc * D + (h + 1) * 64],
                        vtm[:, tc * D + h * 64: tc * D + (h + 1) * 64],
                        start=(c_glob == 0), stop=(c_glob == NCH - 1),
                        tile_position=(0, pb)))
                    mml3.append(lambda e, h=h, pb=pb, tc=tc, c_glob=c_glob:
                                e.matmul(
                        p_ks[pb:pb + 64, (h // 2):(h // 2) + 1],
                        kmtm[:, tc * D + h * 64: tc * D + (h + 1) * 64],
                        onescol[:],
                        start=True, stop=True,
                        skip_group_check=True, tile_position=(0, pb)))
                suw = [l_sa, kv_lms[tc * 2 + 1][1], kv_lms[tc * 2 + 1][0],
                       l_ones]
                tmms(mml3, waits=suw, defs=[l_su])
                last_supd[0] = l_su
                l_ksa = land("ksa")
                vop(lambda e: e.tensor_tensor(ksum_acc[:], ksum_acc[:], p_ks,
                                              ALU.add),
                    waits=[l_su], defs=[l_ksa])
                last_ksa[0] = l_ksa
            if full and dsub >= 5:
                ot_lms = []
                for tci, l_omx in om_lms:
                    l_ot = land("ot")
                    otw = [l_omx]
                    if prev_wo[0] and not ot_lms:
                        otw.append(prev_wo[0])
                    dmatp(omtm[:, tci * D:(tci + 1) * D],
                          omfm[:].rearrange("p (oc t) -> p oc t", t=STW)
                          [:, :, tci * CH:(tci + 1) * CH],
                          waits=otw, defs=[l_ot])
                    ot_lms.append(l_ot)
                to = st - NST // 2
                for oc in range(8):
                    l_wo = land("wo")
                    mml4 = []
                    pw = p_w0 if oc % 2 == 0 else p_w1
                    for mc in range(8):
                        mml4.append(lambda e, mc=mc, oc=oc, pw=pw: e.matmul(
                            pw[:, 0:512],
                            wab["o"][:, mc * D + oc * 128: mc * D + (oc + 1) * 128],
                            omfm[:].rearrange("p (k t) -> p k t", t=STW)[:, mc, :],
                            start=(mc == 0), stop=(mc == 7)))
                    wow = ot_lms + [l_wcast["o"], l_axr_all]
                    if out_done:
                        wow.append(out_done[-1])
                    tmms(mml4, waits=wow, defs=[l_wo])
                    prev_wo[0] = l_wo
                    l_ax = land("ax")
                    axw = [last_fe[0]] if last_fe[0] else [l_axr_all]
                    dma(sc_axr[oc * 128:(oc + 1) * 128, to * STW:(to + 1) * STW],
                        axr_d[:], waits=axw, defs=[l_ax])
                    l_fe = land("fe")
                    few = [l_wo, l_ax] + ([out_done[-1]] if out_done else [])
                    vop(lambda e, pw=pw: e.tensor_tensor(
                        aout_d[:], pw[:, 0:512], axr_d[:], ALU.add),
                        waits=few, defs=[l_fe])
                    last_fe[0] = l_fe
                    l_fo = land("fo")
                    dma(aout_d[:], out_fm[oc * 128:(oc + 1) * 128,
                                          to * STW:(to + 1) * STW],
                        waits=[l_fe], defs=[l_fo])
                    out_done.append(l_fo)
            st_lm[st % 2] = last_supd[0]

        if dbg:
            dma(sc_xfm, dbg_out["d_xfm"], waits=[("ALL", "DM")])
            dma(sc_axr, dbg_out["d_axr"])
            dma(sc_wgt, dbg_out["d_wgt"])
            pass
            dma(xqfm[:], dbg_out["d_xq"])
            dma(hqfm, dbg_out["d_hq2"])
            dma(sc_wut, dbg_out["d_wut"])
            dma(sc_wdt, dbg_out["d_wdt"])

        fw = ([out_done[-1]] if out_done else []) + [("ALL", "DM"), ("ALL", "TP")]
        pl.op("sync", None, incs=[], waits=fw)

        pl.resolve({k: None for k in SEMS})
        return _emit_all(pl, block, sem, nc, dmy)


_NC_CACHE = None


def kernel(L0_attn_x, L0_attn_Wq, L0_attn_Wk, L0_attn_Wv, L0_attn_Wo,
           L0_ffn_w_gate, L0_ffn_w_up, L0_ffn_w_down):
    global _NC_CACHE
    x = np.ascontiguousarray(np.asarray(L0_attn_x, dtype=np.float32))
    wq = np.ascontiguousarray(np.asarray(L0_attn_Wq, dtype=np.float32).T)
    wk = np.ascontiguousarray(np.asarray(L0_attn_Wk, dtype=np.float32).T)
    wv = np.ascontiguousarray(np.asarray(L0_attn_Wv, dtype=np.float32).T)
    wo = np.ascontiguousarray(np.asarray(L0_attn_Wo, dtype=np.float32).T)
    wg = np.ascontiguousarray(np.asarray(L0_ffn_w_gate, dtype=np.float32))
    wu = np.ascontiguousarray(np.asarray(L0_ffn_w_up, dtype=np.float32))
    wd = np.ascontiguousarray(np.asarray(L0_ffn_w_down, dtype=np.float32))

    in_maps = []
    for c in range(NC):
        b, hf = c // 2, c % 2
        if hf == 0:
            prefix = np.zeros((TH, D), dtype=np.float32)
            km = np.concatenate([np.zeros(TH, np.float32),
                                 np.ones(TH, np.float32)])
        else:
            prefix = x[b, 0:TH]
            km = np.ones(S, np.float32)
        km = np.ascontiguousarray(km.reshape(NCH, CH).T)
        own = x[b, hf * TH:(hf + 1) * TH]
        xs = np.ascontiguousarray(np.concatenate([prefix, own], axis=0))
        xfh = np.ascontiguousarray(own.T)
        in_maps.append({
            "xs_tm": xs, "xs_fmh": xfh, "kmask": km,
            "wqt": wq, "wkt": wk, "wvt": wv, "wot": wo,
            "wg": wg, "wu": wu, "wd": wd,
        })

    if _NC_CACHE is None:
        _NC_CACHE = build_program()
    res = run_bass_kernel_spmd(_NC_CACHE, in_maps, core_ids=list(range(NC)))
    out = np.empty((B, S, D), dtype=np.float32)
    for c in range(NC):
        b, hf = c // 2, c % 2
        out[b, hf * TH:(hf + 1) * TH] = res.results[c]["out_fm"].T
    return out


# version 06
# speedup vs baseline: 1.6292x; 1.0009x over previous
import sys
sys.path.insert(0, '/opt/trn_rl_repo')
import contextlib
import numpy as np
import ml_dtypes
import concourse.bass as bass
import concourse.mybir as mybir
from concourse.bass_utils import run_bass_kernel_spmd

F32 = mybir.dt.float32
BF16 = mybir.dt.bfloat16
AF = mybir.ActivationFunctionType
ALU = mybir.AluOpType

B, S, D, H, DH, I = 4, 4096, 1024, 16, 64, 2816
NC = 8
TH = S // 2
CH = 128
NCH = S // CH
STW = 512
NST = S // STW
NITI = (I + 511) // 512
EPS_ATT = 1e-8
EPS_LN = 1e-5
THR = [(0.25, 0.5), (0.75, 0.5), (1.25, 0.5), (1.75, 0.5),
       (2.5, 1.0), (3.5, 1.0), (5.0, 2.0)]


class Plan:

    def __init__(self, barrier=False):
        self.ops = []
        self.lands = {}
        self.barrier = barrier

    def op(self, eng, fn=None, incs=(), waits=(), defs=()):
        if self.barrier:
            waits = tuple(waits) + tuple(
                ("ALL", s) for s in ("DM", "TP", "V", "SC", "G", "T"))
        self.ops.append((eng, fn, tuple(incs), tuple(waits), tuple(defs)))

    def resolve(self, sems):
        counts = {k: 0 for k in sems}
        pre = []
        for eng, fn, incs, waits, defs in self.ops:
            pre.append(dict(counts))
            for sname, n in incs:
                counts[sname] += n
            for dname in defs:
                assert incs, f"landmark {dname} on op with no incs"
                self.lands[dname] = (incs[-1][0], counts[incs[-1][0]])
        self.resolved = []
        for i, (eng, fn, incs, waits, defs) in enumerate(self.ops):
            out = []
            for w in waits:
                if isinstance(w, tuple) and w[0] == "ALL":
                    if pre[i][w[1]] > 0:
                        out.append((w[1], pre[i][w[1]]))
                else:
                    assert w in self.lands, f"wait on undefined landmark {w}"
                    out.append(self.lands[w])
            self.resolved.append(out)

    def emit(self, eng_name, handle, sems, flush=None):
        for i, (eng, fn, incs, waits, defs) in enumerate(self.ops):
            if eng != eng_name:
                continue
            byname = {}
            for sname, val in self.resolved[i]:
                byname[sname] = max(byname.get(sname, 0), val)
            for sname, val in byname.items():
                handle.wait_ge(sems[sname], val)
            inst = fn(handle) if fn is not None else None
            if incs:
                if flush is not None and fn is not None:
                    inst = flush(handle)
                elif inst is None:
                    inst = handle.engine_nop()
                for sname, n in incs:
                    inst = inst.then_inc(sems[sname], n)


def ceil_div(a, b):
    return (a + b - 1) // b


def _emit_all(pl, block, sem, nc, dmy):
    ALU_ = mybir.AluOpType

    @block.sync
    def _(sync):
        pl.emit("sync", sync, sem)

    @block.vector
    def _(vector):
        pl.emit("vector", vector, sem,
                flush=lambda e: e.tensor_scalar(
                    dmy[:, 224:256], dmy[:, 128:160], 1.0, None, ALU_.mult))

    @block.scalar
    def _(scalar):
        pl.emit("scalar", scalar, sem,
                flush=lambda e: e.copy(dmy[:, 160:176], dmy[:, 128:144]))

    @block.gpsimd
    def _(gpsimd):
        pl.emit("gpsimd", gpsimd, sem,
                flush=lambda e: e.tensor_copy(dmy[:, 192:208], dmy[:, 144:160]))

    @block.tensor
    def _(tensor):
        pl.emit("tensor", tensor, sem, flush=(pl.tensor_flush if pl.use_tflush else None))
    return nc


def build_program(phases='ABCD', dsub=9, dbg=False, csub=9):
    nc = bass.Bass("TRN2", target_bir_lowering=False,
                   detect_race_conditions=False)

    xs_tm = nc.dram_tensor("xs_tm", [S, D], F32, kind="ExternalInput").ap()
    xs_fmh = nc.dram_tensor("xs_fmh", [D, TH], F32, kind="ExternalInput").ap()
    kmask = nc.dram_tensor("kmask", [CH, NCH], F32, kind="ExternalInput").ap()
    wqt = nc.dram_tensor("wqt", [D, D], F32, kind="ExternalInput").ap()
    wkt = nc.dram_tensor("wkt", [D, D], F32, kind="ExternalInput").ap()
    wvt = nc.dram_tensor("wvt", [D, D], F32, kind="ExternalInput").ap()
    wot = nc.dram_tensor("wot", [D, D], F32, kind="ExternalInput").ap()
    wg = nc.dram_tensor("wg", [I, D], F32, kind="ExternalInput").ap()
    wu = nc.dram_tensor("wu", [I, D], F32, kind="ExternalInput").ap()
    wd = nc.dram_tensor("wd", [D, I], F32, kind="ExternalInput").ap()
    out_fm = nc.dram_tensor("out_fm", [D, TH], F32, kind="ExternalOutput").ap()
    sc_xfm = nc.dram_tensor("sc_xfm", [D, S], BF16, kind="Internal").ap()
    sc_wgt = nc.dram_tensor("sc_wgt", [D, I], BF16, kind="Internal").ap()
    sc_wut = nc.dram_tensor("sc_wut", [D, I], BF16, kind="Internal").ap()
    sc_wdt = nc.dram_tensor("sc_wdt", [I, D], BF16, kind="Internal").ap()
    sc_axr = nc.dram_tensor("sc_axr", [D, TH], F32, kind="Internal").ap()
    sc_hqf = nc.dram_tensor("sc_hqf", [I, TH], BF16, kind="Internal").ap()
    dbg_out = {}
    if dbg:
        dbg_out["d_xfm"] = nc.dram_tensor("d_xfm", [D, S], BF16, kind="ExternalOutput").ap()
        dbg_out["d_axr"] = nc.dram_tensor("d_axr", [D, TH], F32, kind="ExternalOutput").ap()
        dbg_out["d_wgt"] = nc.dram_tensor("d_wgt", [D, I], BF16, kind="ExternalOutput").ap()
        dbg_out["d_hqf"] = nc.dram_tensor("d_hqf", [I, TH], BF16, kind="ExternalOutput").ap()
        dbg_out["d_xq"] = nc.dram_tensor("d_xq", [128, 8 * TH], BF16, kind="ExternalOutput").ap()
        dbg_out["d_diag"] = nc.dram_tensor("d_diag", [128, 512], F32, kind="ExternalOutput").ap()
        for _i in range(8):
            dbg_out[f"d_s{_i}"] = nc.dram_tensor(f"d_s{_i}", [128, 512], F32, kind="ExternalOutput").ap()
        dbg_out["d_wut"] = nc.dram_tensor("d_wut", [D, I], BF16, kind="ExternalOutput").ap()
        dbg_out["d_wdt"] = nc.dram_tensor("d_wdt", [I, D], BF16, kind="ExternalOutput").ap()
        dbg_out["d_hq2"] = nc.dram_tensor("d_hq2", [128, 22 * TH], BF16, kind="ExternalOutput").ap()
        dbg_out["d_sil"] = nc.dram_tensor("d_sil", [128, 512], F32, kind="ExternalOutput").ap()
        dbg_out["d_hbb"] = nc.dram_tensor("d_hbb", [128, 512], F32, kind="ExternalOutput").ap()
        dbg_out["d_hqt"] = nc.dram_tensor("d_hqt", [128, 512], BF16, kind="ExternalOutput").ap()
        dbg_out["d_wgs"] = nc.dram_tensor("d_wgs", [128, 8 * 512], BF16, kind="ExternalOutput").ap()

    import os as _os
    pl = Plan(barrier=_os.environ.get("BARRIER", "") != "")
    SEMS = ["DM", "TP", "V", "SC", "G", "T"]
    uid = [0]

    def land(pfx):
        uid[0] += 1
        return f"{pfx}{uid[0]}"

    with contextlib.ExitStack() as stk:
        E = stk.enter_context
        sem = {name: E(nc.semaphore(name)) for name in SEMS}

        xqfm = E(nc.sbuf_tensor("xqfm", [128, 8 * TH], BF16,
                                is_transpose_dest=True))
        QW = 1024
        q_u = E(nc.sbuf_tensor("q_u", [CH, QW], F32))
        q_zs = E(nc.sbuf_tensor("q_zs", [CH, QW], BF16))
        q_acc = E(nc.sbuf_tensor("q_acc", [CH, QW], BF16))
        q_t1 = E(nc.sbuf_tensor("q_t1", [CH, QW], BF16))
        q_t2 = E(nc.sbuf_tensor("q_t2", [CH, QW], BF16))
        q_out = E(nc.sbuf_tensor("q_out", [CH, QW], BF16))
        q_bm = E(nc.sbuf_tensor("q_bm", [CH, QW // 16], F32))
        q_s = E(nc.sbuf_tensor("q_s", [CH, QW // 16], F32))
        q_rs = E(nc.sbuf_tensor("q_rs", [CH, QW // 16], F32))
        sumc = E(nc.sbuf_tensor("sumc", [CH, 12], F32))
        mrc = E(nc.sbuf_tensor("mrc", [CH, 8], F32))
        kmv = E(nc.sbuf_tensor("kmv", [CH, NCH], F32))
        onescol = E(nc.sbuf_tensor("onescol", [128, 1], BF16))
        iocol = E(nc.sbuf_tensor("iocol", [128, 1], F32))
        iof = E(nc.sbuf_tensor("iof", [128, 128], F32))
        ltm4 = E(nc.sbuf_tensor("ltm4", [128, 512], BF16))
        dencol = E(nc.sbuf_tensor("dencol", [128, 16], F32))
        dmy = E(nc.sbuf_tensor("dmy", [128, 256], F32))
        ksum_acc = E(nc.sbuf_tensor("ksum_acc", [128, 8], F32))
        denr = E(nc.sbuf_tensor("denr", [128, 16], F32))
        g_f32 = E(nc.sbuf_tensor("g_f32", [128, 7104], F32))
        g_bf = E(nc.sbuf_tensor("g_bf", [128, 66624], BF16,
                                is_transpose_dest=True))

        class Carver:
            def __init__(self, base_ap):
                self.ap = base_ap
                self.off = 0

            def take(self, cols):
                cols_al = (cols + 63) & ~63
                a = self.ap[:, self.off:self.off + cols]
                self.off += cols_al
                assert self.off <= self.ap.shape[1], "arena overflow"
                return a

        cf = Carver(g_f32[:])
        cb = Carver(g_bf[:])
        xbuf = [cf.take(D) for _ in range(2)]
        trash = cf.take(D)
        xn_tm = cf.take(D)
        wbuf = [cf.take(1024) for _ in range(2)]
        xt_tm = cb.take(D)
        xt_fm = cb.take(D)
        wq_tm = cb.take(1024)
        wq_fm = cb.take(1024)
        cf = Carver(g_f32[:])
        cb = Carver(g_bf[:])
        sil = cf.take(512)
        hbb = cf.take(512)
        axr = cf.take(512)
        aout = cf.take(512)
        dbg_p = cf.take(512)
        hqp = cb.take(4 * 128)
        wgs = cb.take(8 * 512)
        wus = cb.take(8 * 512)
        hq_tm = cb.take(512)
        hqfm = cb.take(22 * TH)
        wds = cb.take(22 * 128)
        cf = Carver(g_f32[:])
        cb = Carver(g_bf[:])
        wfb = [cf.take(D) for _ in range(2)]
        elu1 = cf.take(512)
        elu2 = cf.take(512)
        axr_d = cf.take(512)
        aout_d = cf.take(512)
        kfm = cb.take(8 * STW)
        omfm = cb.take(8 * STW)
        xfmst = [cb.take(8 * STW) for _ in range(2)]
        wab = {w: cb.take(8 * D) for w in ("q", "k", "v", "o")}
        kmtm = cb.take(4 * D)
        vtm = cb.take(4 * D)
        qfm = cb.take(8 * STW)
        saug = cb.take(8 * 65)
        atm = cb.take(512)
        omtm = cb.take(4 * D)
        p_S = E(nc.psum_tensor("p_S", [128, 512], F32))
        p_misc = E(nc.psum_tensor("p_misc", [128, 512], F32))
        p_ks = p_misc[:, 0:8]
        p_dn = p_misc[:, 16:32]
        p_fl = p_misc[0:1, 256:257]
        p_at = E(nc.psum_tensor("p_at", [128, 512], F32))
        p_n0 = E(nc.psum_tensor("p_n0", [128, 512], F32))
        p_n1 = E(nc.psum_tensor("p_n1", [128, 512], F32))
        p_w0 = E(nc.psum_tensor("p_w0", [128, 512], F32))
        p_w1 = E(nc.psum_tensor("p_w1", [128, 512], F32))

        block = E(nc.Block())

        dmy_i = [0]

        def vdum(e):
            i = dmy_i[0] % 4
            dmy_i[0] += 1
            e.tensor_scalar(dmy[:, 32 * i:32 * i + 32], dmy[:, 128:160],
                            1.0, None, ALU.mult)

        def _tflush(e):
            return e.matmul(p_fl, onescol[:], onescol[:], start=True,
                            stop=True, skip_group_check=True)
        pl.tensor_flush = _tflush
        import os as _os
        pl.use_tflush = _os.environ.get("NO_TFLUSH", "") == ""

        def dma(src_ap, dst_ap, waits=(), defs=()):
            pl.op("sync", lambda e, s=src_ap, d=dst_ap: e.dma_start(out=d, in_=s),
                  incs=[("DM", 16)], waits=tuple(waits), defs=defs)

        def dmatp(src_ap, dst_ap, waits=(), defs=()):
            pl.op("sync", lambda e, s=src_ap, d=dst_ap: e.dma_start_transpose(d, s),
                  incs=[("TP", 16)],
                  waits=tuple(waits) + (("ALL", "TP"),), defs=defs)

        def vop(fn, waits=(), defs=()):
            pl.op("vector", fn, incs=[("V", 1)], waits=waits, defs=defs)

        def sop(fn, waits=(), defs=()):
            if not first_eng["scalar"]:
                first_eng["scalar"] = True
                waits = tuple(waits) + (l_dmy,)
            pl.op("scalar", fn, incs=[("SC", 1)], waits=waits, defs=defs)

        def gop(fn, waits=(), defs=()):
            if not first_eng["gpsimd"]:
                first_eng["gpsimd"] = True
                waits = tuple(waits) + (l_dmy,)
            pl.op("gpsimd", fn, incs=[("G", 1)], waits=waits, defs=defs)

        def top(fn, waits=(), defs=()):
            pl.op("tensor", fn, incs=[("T", 1)], waits=waits, defs=defs)

        te_state = {"base": None}

        def tmms(mmlist, waits=(), defs=()):
            def go(e):
                last = None
                for item in mmlist:
                    if isinstance(item, tuple):
                        base, f = item
                        if te_state["base"] is not None and \
                                te_state["base"] != base:
                            e.ldweights(ltm4[:, 0:128])
                        te_state["base"] = base
                        last = f(e)
                    else:
                        te_state["base"] = None
                        last = item(e)
                return last
            pl.op("tensor", go, incs=[("T", 1)], waits=waits, defs=defs)

        q_state = {"zs": None, "dst": None}

        def emit_quant2(src_fn, w, out_dst_fn, waits=(), dst_free=None):
            nb = w // 16
            src = src_fn()
            outd = out_dst_fn()
            def mk_scales(e):
                e.tensor_reduce(
                    q_bm[:, 0:nb], src.rearrange("p (b k) -> p b k", k=16),
                    axis=mybir.AxisListType.X, op=ALU.max,
                    apply_absolute_value=True)
                vdum(e)
                e.tensor_scalar(q_s[:, 0:nb], q_bm[:, 0:nb],
                                6e-12, 1.0 / 6.0, ALU.max, ALU.mult)
                vdum(e)
                e.reciprocal(q_rs[:, 0:nb], q_s[:, 0:nb])
                vdum(e)
                return e.tensor_scalar(dmy[:, 0:32], dmy[:, 128:160], 1.0,
                                       None, ALU.mult)
            vop(mk_scales, waits=waits)
            rs_b = q_rs[:, 0:nb].unsqueeze(2).broadcast_to([128, nb, 16])
            s_b = q_s[:, 0:nb].unsqueeze(2).broadcast_to([128, nb, 16])
            l_u = land("qu")
            uw = [q_state["zs"]] if q_state["zs"] else []
            vop(lambda e: e.tensor_tensor(q_u[:, 0:w], src, rs_b, ALU.mult),
                waits=uw, defs=[l_u])
            l_zs = land("qzs")
            sop(lambda e: e.activation(q_zs[:, 0:w], q_u[:, 0:w], AF.Sign),
                waits=[l_u])
            sop(lambda e: e.activation(q_u[:, 0:w], q_u[:, 0:w], AF.Abs),
                defs=[l_zs])
            q_state["zs"] = l_zs
            t0, w0 = THR[0]
            t1x, w1x = THR[1]
            vop(lambda e: e.tensor_scalar(
                q_t1[:, 0:w], q_u[:, 0:w], t0, w0, ALU.is_gt, ALU.mult),
                waits=[l_zs])
            vop(lambda e: e.tensor_scalar(
                q_t2[:, 0:w], q_u[:, 0:w], t1x, w1x, ALU.is_gt, ALU.mult))
            vop(lambda e: e.tensor_tensor(q_acc[:, 0:w], q_t1[:, 0:w],
                                          q_t2[:, 0:w], ALU.add))
            for tj, wj in THR[2:]:
                vop(lambda e, tj=tj, wj=wj: e.tensor_scalar(
                    q_t1[:, 0:w], q_u[:, 0:w], tj, wj, ALU.is_gt, ALU.mult))
                vop(lambda e: e.tensor_tensor(q_acc[:, 0:w], q_acc[:, 0:w],
                                              q_t1[:, 0:w], ALU.add))
            vop(lambda e: e.tensor_tensor(q_t2[:, 0:w], q_acc[:, 0:w],
                                          q_zs[:, 0:w], ALU.mult))
            l_out = land("qo")
            ow = [dst_free] if dst_free else []
            vop(lambda e: e.tensor_tensor(outd, q_t2[:, 0:w], s_b, ALU.mult),
                waits=ow, defs=[l_out])
            return l_out

        l_dmy = land("dmy0")
        vop(lambda e: e.memset(dmy[:], 0.0), defs=[l_dmy])
        first_eng = {"scalar": False, "gpsimd": False}
        l_km = land("km")
        dma(kmask, kmv[:, 0:NCH],
            defs=[l_km])
        l_iota = land("iota")
        gop(lambda e: e.iota(iocol[:], pattern=[[0, 1]], base=0,
                             channel_multiplier=1,
                             allow_small_or_imprecise_dtypes=True))
        gop(lambda e: e.iota(iof[:], pattern=[[1, 128]], base=0,
                             channel_multiplier=0,
                             allow_small_or_imprecise_dtypes=True),
            defs=[l_iota])
        l_ones = land("ones")
        vop(lambda e: e.memset(onescol[:], 1.0), defs=[l_ones])
        l_ltm = land("ltm")
        def mk_ltm(e):
            last = None
            for r in range(4):
                last = e.tensor_scalar(ltm4[:, r * 128:(r + 1) * 128], iof[:],
                                       iocol[:], None, ALU.is_ge)
            return last
        vop(mk_ltm, waits=[l_iota], defs=[l_ltm])

        x_done_lm = [None] * NCH
        last_mr = [None]
        last_xtp = [None]
        last_xq_tp = [None]
        for c in range(NCH):
            buf = xbuf[c % 2]
            wl = [] if c < 2 else [x_done_lm[c - 2]]
            l_ld = land("xa")
            dma(xs_tm[c * CH:(c + 1) * CH, :], buf[:], waits=wl, defs=[l_ld])
            def mk_mr(e, b=buf):
                e.bn_stats(sumc[:, 0:6], b[:, 0:512])
                e.bn_stats(sumc[:, 6:12], b[:, 512:1024])
                e.bn_aggr(mrc[:, 4:6], sumc[:, 0:6])
                e.bn_aggr(mrc[:, 6:8], sumc[:, 6:12])
                vdum(e)
                e.tensor_tensor(mrc[:, 0:1], mrc[:, 4:5], mrc[:, 6:7], ALU.add)
                e.tensor_tensor(mrc[:, 3:4], mrc[:, 4:5], mrc[:, 6:7], ALU.subtract)
                e.tensor_tensor(mrc[:, 1:2], mrc[:, 5:6], mrc[:, 7:8], ALU.add)
                e.tensor_scalar(mrc[:, 0:1], mrc[:, 0:1], -0.5, None, ALU.mult)
                e.tensor_tensor(mrc[:, 3:4], mrc[:, 3:4], mrc[:, 3:4], ALU.mult)
                e.tensor_scalar(mrc[:, 1:2], mrc[:, 1:2], 0.5, None, ALU.mult)
                e.tensor_scalar(mrc[:, 3:4], mrc[:, 3:4], 0.25, EPS_LN,
                                ALU.mult, ALU.add)
                vdum(e)
                e.tensor_tensor(mrc[:, 3:4], mrc[:, 1:2], mrc[:, 3:4], ALU.add)
                vdum(e)
                return e.reciprocal(mrc[:, 1:2], mrc[:, 3:4])
            l_mr = land("lm")
            vop(mk_mr, waits=[l_ld] + ([last_mr[0]] if last_mr[0] else []),
                defs=[l_mr])
            last_mr[0] = l_mr
            l_r = land("r")
            sop(lambda e: e.activation(mrc[:, 1:2], mrc[:, 1:2], AF.Sqrt),
                waits=[l_mr], defs=[l_r])
            l_xt = land("xt")

            def mk_xt(e, b=buf):
                e.tensor_tensor(mrc[:, 2:3], mrc[:, 0:1], mrc[:, 1:2], ALU.mult)
                vdum(e)
                return e.tensor_scalar(xt_tm[:], b[:], mrc[:, 1:2], mrc[:, 2:3],
                                       ALU.mult, ALU.add)
            xtw = [l_r] + ([last_xtp[0]] if last_xtp[0] else [])
            vop(mk_xt, waits=xtw, defs=[l_xt])
            l_tp = land("tp")
            dmatp(xt_tm[:], xt_fm[:].rearrange("p (dc t) -> p dc t", t=CH),
                  waits=[l_xt], defs=[l_tp])
            last_xtp[0] = l_tp
            l_xf = land("xf")
            dma(xt_fm[:].rearrange("p (dc t) -> p dc t", t=CH),
                sc_xfm.rearrange("(dc p) s -> p dc s", p=128)
                [:, :, c * CH:(c + 1) * CH],
                waits=[l_tp], defs=[l_xf])
            if c >= NCH // 2:
                tc = c - NCH // 2
                l_xn = land("xn")

                def mk_xn(e, b=buf):
                    return e.tensor_scalar(xn_tm[:], b[:], mrc[:, 1:2],
                                           mrc[:, 2:3], ALU.mult, ALU.add)
                vop(mk_xn, waits=[l_r], defs=[l_xn])
                l_q = emit_quant2(lambda: xn_tm[:], D,
                                  lambda: q_out[:, 0:D], waits=[l_xn],
                                  dst_free=last_xq_tp[0])
                l_tq = land("tq")
                dmatp(q_out[:, 0:D],
                      xqfm[:].rearrange("p (dc t) -> p dc t", t=TH)
                      [:, :, tc * CH:(tc + 1) * CH],
                      waits=[l_q], defs=[l_tq])
                last_xq_tp[0] = l_tq
                x_done_lm[c] = l_tq
            else:
                x_done_lm[c] = l_xf

        if 'B' not in phases:
            if dbg:
                dma(sc_xfm, dbg_out["d_xfm"], waits=[("ALL", "DM")])
            pl.op("sync", None, incs=[],
                  waits=[("ALL", "DM"), ("ALL", "TP")])
            pl.resolve({k: None for k in SEMS})
            return _emit_all(pl, block, sem, nc, dmy)
        wstate = {"tp": None, "prev": [None, None], "i": 0}

        def wprep(w_in, row0, col0, width, sc_out):
            i = wstate["i"]
            wstate["i"] += 1
            buf = wbuf[i % 2]
            wl = [] if wstate["prev"][i % 2] is None else [wstate["prev"][i % 2]]
            l_ld = land("wl")
            dma(w_in[row0:row0 + CH, col0:col0 + width], buf[:, 0:width],
                waits=wl, defs=[l_ld])
            l_q = emit_quant2(lambda b=buf: b[:, 0:width], width,
                              lambda: wq_tm[:, 0:width], waits=[l_ld],
                              dst_free=wstate["tp"])
            l_tp = land("wt")
            ncc = width // 128
            dmatp(wq_tm[:, 0:width],
                  wq_fm[:].rearrange("p (cc t) -> p cc t", t=CH)[:, 0:ncc, :],
                  waits=[l_q], defs=[l_tp])
            wstate["tp"] = l_tp
            l_st = land("ws")
            dma(wq_fm[:].rearrange("p (cc t) -> p cc t", t=CH)[:, 0:ncc, :],
                sc_out.rearrange("(cc p) n -> p cc n", p=128)
                [:, col0 // 128:col0 // 128 + ncc, row0:row0 + CH],
                waits=[l_tp], defs=[l_st])
            wstate["prev"][i % 2] = l_st
            return l_st

        for c in range(I // CH):
            wprep(wg, c * CH, 0, D, sc_wgt)
        for c in range(I // CH):
            wprep(wu, c * CH, 0, D, sc_wut)
        for c in range(D // CH):
            wprep(wd, c * CH, 0, 1024, sc_wdt)
            wprep(wd, c * CH, 1024, 1024, sc_wdt)
            l_wdone = wprep(wd, c * CH, 2048, 768, sc_wdt)

        if 'C' not in phases:
            if dbg:
                dma(sc_xfm, dbg_out["d_xfm"], waits=[("ALL", "DM")])
                dma(sc_wgt, dbg_out["d_wgt"])
            pl.op("sync", None, incs=[],
                  waits=[("ALL", "DM"), ("ALL", "TP")])
            pl.resolve({k: None for k in SEMS})
            return _emit_all(pl, block, sem, nc, dmy)
        hq_done = []
        hq_cp = [None]
        hq_st = [None]
        pipec = [None]
        last_wg_lm = None
        _ctiles = int(_os.environ.get('CTILES', '999'))
        if dbg:
            l_dg0 = land("dg0")
            dma(sc_wgt.rearrange("(dc p) n -> p dc n", p=128)[:, :, 0:512],
                wgs[:].rearrange("p (dc n) -> p dc n", n=512),
                waits=[l_wdone], defs=[l_dg0])
            l_dgm = land("dgm")
            xsl0 = xqfm[:].rearrange("p (dc t) -> p dc t", t=TH)
            import os as _os
            _dsel = _os.environ.get('DIAGSEL', '1')
            _dml = []
            if _dsel == '1':
                _dml.append(lambda e: e.matmul(
                    p_w0[:, 0:512], xsl0[:, 0, 0:128],
                    wgs[:].rearrange("p (dc n) -> p dc n", n=512)[:, 0, :],
                    start=True, stop=True))
            else:
                for dc in range(8):
                    _dml.append(lambda e, dc=dc: e.matmul(
                        p_w0[:, 0:512], xsl0[:, dc, 0:128],
                        wgs[:].rearrange("p (dc n) -> p dc n", n=512)[:, dc, :],
                        start=(dc == 0), stop=(dc == 7)))
                if _dsel == '16':
                    for dc in range(8):
                        _dml.append(lambda e, dc=dc: e.matmul(
                            p_w1[:, 0:512], xsl0[:, dc, 0:128],
                            wus[:].rearrange("p (dc n) -> p dc n", n=512)[:, dc, :],
                            start=(dc == 0), stop=(dc == 7)))
            tmms(_dml, waits=[l_dg0], defs=[l_dgm])
            l_dgc = land("dgc")
            sop(lambda e: e.activation(sil[:, 0:512], p_w0[:, 0:512], AF.Copy),
                waits=[l_dgm], defs=[l_dgc])
            dma(sil[:, 0:512], dbg_out["d_diag"], waits=[l_dgc])
        for it in range(NITI):
            isz = min(512, I - it * 512)
            l_wg = land("fg")
            wl = [l_wdone] if it == 0 else [hq_done[-1]]
            dma(sc_wgt.rearrange("(dc p) n -> p dc n", p=128)
                [:, :, it * 512:it * 512 + isz],
                wgs[:].rearrange("p (dc n) -> p dc n", n=512)[:, :, 0:isz],
                waits=wl, defs=[l_wg])
            l_wu = land("fu")
            dma(sc_wut.rearrange("(dc p) n -> p dc n", p=128)
                [:, :, it * 512:it * 512 + isz],
                wus[:].rearrange("p (dc n) -> p dc n", n=512)[:, :, 0:isz],
                defs=[l_wu])
            last_wg_lm = l_wu
            for tc in range(TH // CH):
                if it * 16 + tc >= _ctiles:
                    continue
                xsl = xqfm[:].rearrange("p (dc t) -> p dc t", t=TH)
                l_mm = land("fm")
                gml = []
                for dc in range(8):
                    gml.append(lambda e, dc=dc, tc=tc, isz=isz, xsl=xsl: e.matmul(
                        p_w0[:, 0:isz],
                        xsl[:, dc, tc * CH:(tc + 1) * CH],
                        wgs[:].rearrange("p (dc n) -> p dc n", n=512)[:, dc, 0:isz],
                        start=(dc == 0), stop=(dc == 7)))
                for dc in range(8):
                    gml.append(lambda e, dc=dc, tc=tc, isz=isz, xsl=xsl: e.matmul(
                        p_w1[:, 0:isz],
                        xsl[:, dc, tc * CH:(tc + 1) * CH],
                        wus[:].rearrange("p (dc n) -> p dc n", n=512)[:, dc, 0:isz],
                        start=(dc == 0), stop=(dc == 7)))
                mmw = [l_wg, l_wu]
                if pipec[0] is not None:
                    mmw += list(pipec[0])
                tmms(gml, waits=mmw, defs=[l_mm])
                l_sil = land("fs")
                sop(lambda e, isz=isz: e.activation(sil[:, 0:isz], p_w0[:, 0:isz],
                                                    AF.Silu),
                    waits=[l_mm], defs=[l_sil])
                if csub < 2:
                    hq_done.append(l_sil)
                    continue
                l_h = land("fh")
                vop(lambda e, isz=isz: e.tensor_tensor(
                    hbb[:, 0:isz], sil[:, 0:isz], p_w1[:, 0:isz], ALU.mult),
                    waits=[l_sil], defs=[l_h])
                pipec[0] = (l_sil, l_h)
                l_q = emit_quant2(lambda isz=isz: hbb[:, 0:isz], isz,
                                  lambda isz=isz: hq_tm[:, 0:isz], waits=[l_h],
                                  dst_free=hq_done[-1] if hq_done else None)
                _slots = {(0, 0): 0, (0, 15): 2, (1, 0): 3,
                          (2, 0): 4, (3, 0): 5, (4, 1): 6, (5, 0): 7}
                if dbg and (it, tc) in _slots:
                    dma(sil[:], dbg_out[f"d_s{_slots[(it, tc)]}"],
                        waits=[l_sil])
                if dbg and (it, tc) == (0, 0):
                    l_pd = land("pd")
                    vop(lambda e: e.tensor_copy(aout[:], p_w0[:, 0:512]),
                        waits=[l_sil], defs=[l_pd])
                    dma(aout[:], dbg_out["d_s1"], waits=[l_pd])
                if dbg and it == 4 and tc == 0:
                    dma(sil[:], dbg_out["d_sil"], waits=[l_sil])
                    dma(hbb[:], dbg_out["d_hbb"], waits=[l_h])
                    dma(hq_tm[:], dbg_out["d_hqt"], waits=[l_q])
                    dma(wgs[:], dbg_out["d_wgs"], waits=[l_mm])
                l_tp2 = land("ftp")
                ncc = isz // 128
                tpw = [l_q] + ([hq_cp[0]] if hq_cp[0] else [])
                dmatp(hq_tm[:, 0:isz],
                      hqp.rearrange("p (cc t) -> p cc t", t=CH)[:, 0:ncc, :],
                      waits=tpw, defs=[l_tp2])
                l_tq = land("ft")
                vop(lambda e, it=it, tc=tc, ncc=ncc: e.tensor_copy(
                    hqfm.rearrange("p (cc t) -> p cc t", t=TH)
                    [:, it * 4:it * 4 + ncc, tc * CH:(tc + 1) * CH],
                    hqp.rearrange("p (cc t) -> p cc t", t=CH)[:, 0:ncc, :]),
                    waits=[l_tp2], defs=[l_tq])
                hq_cp[0] = l_tq
                hq_done.append(l_tq)
        if csub < 2 or _ctiles < 96:
            if dbg:
                dma(xqfm[:], dbg_out["d_xq"], waits=[("ALL", "DM"), ("ALL", "TP")])
            pl.op("sync", None, incs=[], waits=[("ALL", "DM"), ("ALL", "TP")])
            pl.resolve({k: None for k in SEMS})
            return _emit_all(pl, block, sem, nc, dmy)
        axr_done = []
        ffn_fe = [None]
        l_hq_all = hq_done[-1]
        last_wd = [None]
        for oc in range(8):
            l_wdl = land("dwl")
            wlw = [last_wd[0]] if last_wd[0] else [l_hq_all]
            dma(sc_wdt.rearrange("(ic p) o -> p ic o", p=128)
                [:, :, oc * 128:(oc + 1) * 128],
                wds.rearrange("p (ic o) -> p ic o", o=128),
                waits=wlw, defs=[l_wdl])
            for ts4 in range(TH // 512):
                l_mm = land("dm")
                mml = []
                for ic in range(22):
                    mml.append(lambda e, ic=ic, oc=oc, ts4=ts4: e.matmul(
                        p_w0[:, 0:512],
                        wds.rearrange("p (ic o) -> p ic o", o=128)[:, ic, :],
                        hqfm.rearrange("p (cc t) -> p cc t", t=TH)
                        [:, ic, ts4 * 512:(ts4 + 1) * 512],
                        start=(ic == 0), stop=(ic == 21)))
                mmw = [l_wdl]
                if axr_done:
                    mmw.append(axr_done[-1])
                tmms(mml, waits=mmw, defs=[l_mm])
                l_x = land("dx")
                axw = [ffn_fe[0]] if ffn_fe[0] else []
                dma(xs_fmh[oc * 128:(oc + 1) * 128, ts4 * 512:(ts4 + 1) * 512],
                    axr[:], waits=axw, defs=[l_x])
                l_e = land("de")
                few = [l_mm, l_x] + ([axr_done[-1]] if axr_done else [])
                vop(lambda e: e.tensor_tensor(aout[:], p_w0[:, 0:512], axr[:],
                                              ALU.add),
                    waits=few, defs=[l_e])
                ffn_fe[0] = l_e
                l_s = land("ds")
                dma(aout[:], sc_axr[oc * 128:(oc + 1) * 128,
                                    ts4 * 512:(ts4 + 1) * 512],
                    waits=[l_e], defs=[l_s])
                axr_done.append(l_s)
            last_wd[0] = l_mm
        l_axr_all = axr_done[-1]

        if 'D' not in phases:
            if dbg:
                dma(sc_xfm, dbg_out["d_xfm"], waits=[("ALL", "DM")])
                dma(sc_axr, dbg_out["d_axr"])
                dma(sc_wgt, dbg_out["d_wgt"])
                dma(xqfm[:], dbg_out["d_xq"])
                dma(hqfm, dbg_out["d_hq2"])
            pl.op("sync", None, incs=[],
                  waits=[("ALL", "DM"), ("ALL", "TP")])
            pl.resolve({k: None for k in SEMS})
            return _emit_all(pl, block, sem, nc, dmy)
        wsrc = {"q": wqt, "k": wkt, "v": wvt, "o": wot}
        l_wcast = {}
        prev = [l_axr_all, l_axr_all]
        for wi, wname in enumerate(("q", "k", "v", "o")):
            for dc in range(8):
                idx = (wi * 8 + dc) % 2
                buf = wfb[idx]
                wl = [prev[idx]] if prev[idx] else []
                l_ld = land("awl")
                dma(wsrc[wname][dc * 128:(dc + 1) * 128, :], buf[:],
                    waits=wl, defs=[l_ld])
                l_c = land("awc")
                vop(lambda e, b=buf, wname=wname, dc=dc: e.tensor_copy(
                    wab[wname][:, dc * D:(dc + 1) * D], b[:]),
                    waits=[l_ld], defs=[l_c])
                prev[idx] = l_c
            l_wcast[wname] = prev[(wi * 8 + 7) % 2]

        if dsub < 1.5:
            pl.op("sync", None, incs=[], waits=[("ALL", "DM"), ("ALL", "TP")])
            pl.resolve({k: None for k in SEMS})
            return _emit_all(pl, block, sem, nc, dmy)
        l_ksa0 = land("ks0")
        vop(lambda e: e.memset(ksum_acc[:], 0.0), defs=[l_ksa0])
        last_ksa = [l_ksa0]
        st_lm = [None, None]
        out_done = []
        last_supd = [None]
        prev_om = [None]
        prev_kf = [None]
        prev_wo = [None]
        last_fe = [None]
        for st in range(NST):
            full = st >= NST // 2
            xb = xfmst[st % 2]
            wl = [st_lm[st % 2]] if st_lm[st % 2] else [l_axr_all]
            l_x = land("sx")
            dma(sc_xfm.rearrange("(dc p) s -> p dc s", p=128)
                [:, :, st * STW:(st + 1) * STW],
                xb[:].rearrange("p (dc t) -> p dc t", t=STW),
                waits=wl, defs=[l_x])
            xsl = xb[:].rearrange("p (dc t) -> p dc t", t=STW)
            kv_lms = []
            for tc in range(4):
                for half in range(2):
                    l_mm = land("kp")
                    mml = []
                    for dc in range(8):
                        mml.append(lambda e, dc=dc, tc=tc, half=half,
                                    xsl=xsl: e.matmul(
                            p_w0[:, 0:512],
                            xsl[:, dc, tc * CH:(tc + 1) * CH],
                            wab["k"][:, dc * D + half * 512: dc * D + (half + 1) * 512],
                            start=(dc == 0), stop=(dc == 7)))
                    for dc in range(8):
                        mml.append(lambda e, dc=dc, tc=tc, half=half,
                                    xsl=xsl: e.matmul(
                            p_w1[:, 0:512],
                            xsl[:, dc, tc * CH:(tc + 1) * CH],
                            wab["v"][:, dc * D + half * 512: dc * D + (half + 1) * 512],
                            start=(dc == 0), stop=(dc == 7)))
                    mmw = [l_x, l_wcast["k"], l_wcast["v"], l_km]
                    if kv_lms:
                        mmw += [kv_lms[-1][0], kv_lms[-1][1]]
                    else:
                        if prev_kf[0]:
                            mmw.append(prev_kf[0])
                        if last_supd[0]:
                            mmw.append(last_supd[0])
                    tmms(mml, waits=mmw, defs=[l_mm])
                    l_e0 = land("ke0")
                    l_e1 = land("ke")
                    vop(lambda e: e.tensor_scalar(elu1[:, 0:512], p_w0[:, 0:512],
                                                  0.0, None, ALU.min),
                        waits=[l_mm], defs=[l_e0])
                    sop(lambda e: e.activation(elu1[:, 0:512], elu1[:, 0:512],
                                               AF.Exp), waits=[l_e0])
                    sop(lambda e: e.activation(elu2[:, 0:512], p_w0[:, 0:512],
                                               AF.Relu), defs=[l_e1])
                    l_k = land("kk")

                    def mk_k(e, tc=tc, half=half, st=st):
                        e.tensor_tensor(elu1[:, 0:512], elu1[:, 0:512],
                                        elu2[:, 0:512], ALU.add)
                        return e.tensor_scalar(
                            kmtm[:, tc * D + half * 512: tc * D + (half + 1) * 512],
                            elu1[:, 0:512],
                            kmv[:, st * 4 + tc: st * 4 + tc + 1], None, ALU.mult)
                    vop(mk_k, waits=[l_e1], defs=[l_k])
                    l_v = land("kv")
                    sop(lambda e, tc=tc, half=half: e.activation(
                        vtm[:, tc * D + half * 512: tc * D + (half + 1) * 512],
                        p_w1[:, 0:512], AF.Copy),
                        waits=[l_mm], defs=[l_v])
                    kv_lms.append((l_v, l_k))
            q_lms = []
            kf_lms = []
            if full and dsub >= 3:
                for oc in range(8):
                    l_mm = land("qp")
                    mml = []
                    for dc in range(8):
                        mml.append(lambda e, dc=dc, oc=oc, xsl=xsl: e.matmul(
                            p_w0[:, 0:512] if oc % 2 == 0 else p_w1[:, 0:512],
                            wab["q"][:, dc * D + oc * 128: dc * D + (oc + 1) * 128],
                            xsl[:, dc, :],
                            start=(dc == 0), stop=(dc == 7)))
                    mmw = [l_x, l_wcast["q"]]
                    if q_lms:
                        mmw.append(q_lms[-1])
                    else:
                        mmw += [kv_lms[-1][0], kv_lms[-1][1]]
                    tmms(mml, waits=mmw, defs=[l_mm])
                    l_e0 = land("qe0")
                    l_e1 = land("qe")
                    psrc = p_w0 if oc % 2 == 0 else p_w1
                    vop(lambda e, p=psrc: e.tensor_scalar(
                        elu1[:, 0:512], p[:, 0:512], 0.0, None, ALU.min),
                        waits=[l_mm], defs=[l_e0])
                    sop(lambda e: e.activation(elu1[:, 0:512], elu1[:, 0:512],
                                               AF.Exp), waits=[l_e0])
                    sop(lambda e, p=psrc: e.activation(
                        elu2[:, 0:512], p[:, 0:512], AF.Relu), defs=[l_e1])
                    l_q = land("qq")
                    vop(lambda e, oc=oc: e.tensor_tensor(
                        qfm[:, oc * STW:(oc + 1) * STW], elu1[:, 0:512],
                        elu2[:, 0:512], ALU.add),
                        waits=[l_e1], defs=[l_q])
                    q_lms.append(l_q)
                for tc in range(4):
                    l_t = land("kt")
                    dmatp(kmtm[:, tc * D:(tc + 1) * D],
                          kfm[:].rearrange("p (oc t) -> p oc t", t=STW)
                          [:, :, tc * CH:(tc + 1) * CH],
                          waits=[kv_lms[tc * 2 + 1][1]], defs=[l_t])
                    kf_lms.append(l_t)
                prev_kf[0] = kf_lms[-1]
            om_lms = []
            for tc in range(4):
                c_glob = st * 4 + tc
                l_sa = land("sa")

                def mk_sa(e):
                    for pr in range(8):
                        e.copy(
                            saug[:].rearrange("p (pr x) -> p pr x", x=65)
                            [:, pr, 0:64],
                            p_S[:, pr * 64:(pr + 1) * 64])
                    return e.copy(
                        saug[:].rearrange("p (pr x) -> p pr x", x=65)
                        [:, :, 64:65], ksum_acc[:].unsqueeze(2))
                sa_w = ([last_supd[0]] if last_supd[0] else []) + [last_ksa[0]]
                if c_glob == 0:
                    sa_w = sa_w + [l_ones]
                sop(mk_sa, waits=sa_w, defs=[l_sa])
                if full and dsub >= 4:
                    omtm_lms = []
                    for hgrp in range(4):
                        l_at = land("at")
                        mml = []
                        for hh in range(4):
                            h = hgrp * 4 + hh
                            pb = 64 * (h % 2)
                            oc = h // 2
                            mml.append((pb, lambda e, h=h, pb=pb, oc=oc,
                                        hh=hh, tc=tc: e.matmul(
                                p_at[:, hh * 128:(hh + 1) * 128],
                                kfm[pb:pb + 64,
                                    oc * STW + tc * CH: oc * STW + (tc + 1) * CH],
                                qfm[pb:pb + 64,
                                    oc * STW + tc * CH: oc * STW + (tc + 1) * CH],
                                start=True, stop=True, tile_position=(pb, 0))))
                        atw = [q_lms[-1], kf_lms[tc], l_ltm]
                        if omtm_lms:
                            atw.append(omtm_lms[-1])
                        else:
                            if prev_om[0]:
                                atw.append(prev_om[0])
                            if prev_wo[0]:
                                atw.append(prev_wo[0])
                        tmms(mml, waits=atw, defs=[l_at])
                        l_am = land("am")
                        if dsub < 4.08:
                            vop(lambda e: e.tensor_copy(atm[:], p_at[:, 0:512]),
                                waits=[l_at], defs=[l_am])
                        else:
                            vop(lambda e: e.tensor_tensor(atm[:], p_at[:, 0:512],
                                                          ltm4[:], ALU.mult),
                                waits=[l_at], defs=[l_am])
                        if dsub < 4.2:
                            omtm_lms.append(l_am)
                            continue
                        l_nx = land("nx")
                        mml2 = []
                        for hh in range(4):
                            h = hgrp * 4 + hh
                            pb = 64 * (h % 2)
                            oc = h // 2
                            pn = p_n0 if h < 8 else p_n1
                            hs = (h % 8)
                            mml2.append(lambda e, h=h, hh=hh, tc=tc, pn=pn, hs=hs:
                                        e.matmul(
                                pn[:, hs * 64:(hs + 1) * 64],
                                atm[:, hh * 128:(hh + 1) * 128],
                                vtm[:, tc * D + h * 64: tc * D + (h + 1) * 64],
                                start=True, stop=False))
                            mml2.append((pb, lambda e, h=h, pb=pb, oc=oc,
                                         tc=tc, pn=pn, hs=hs: e.matmul(
                                pn[:, hs * 64:(hs + 1) * 64],
                                qfm[pb:pb + 64,
                                    oc * STW + tc * CH: oc * STW + (tc + 1) * CH],
                                saug[pb:pb + 64,
                                     (h // 2) * 65:(h // 2) * 65 + 64],
                                start=False, stop=True, tile_position=(pb, 0))))
                            if dsub < 4.5:
                                continue
                            mml2.append(lambda e, h=h, hh=hh: e.matmul(
                                p_dn[:, h:h + 1],
                                atm[:, hh * 128:(hh + 1) * 128],
                                onescol[:],
                                start=True, stop=False,
                                skip_group_check=True))
                            mml2.append((pb, lambda e, h=h, pb=pb, oc=oc,
                                         tc=tc: e.matmul(
                                p_dn[:, h:h + 1],
                                qfm[pb:pb + 64,
                                    oc * STW + tc * CH: oc * STW + (tc + 1) * CH],
                                saug[pb:pb + 64,
                                     (h // 2) * 65 + 64:(h // 2) * 65 + 65],
                                start=False, stop=True, skip_group_check=True,
                                tile_position=(pb, 0))))
                        tmms(mml2, waits=[l_am, l_sa, l_ones], defs=[l_nx])
                        omtm_lms.append(l_nx)
                    if dsub < 4.5:
                        prev_om[0] = omtm_lms[-1]
                        last_supd_skip = True
                    l_dr = land("dr")

                    def mk_dr(e):
                        e.tensor_copy(dencol[:], p_dn[:, 0:16])
                        vdum(e)
                        e.tensor_scalar(dencol[:], dencol[:], EPS_ATT, None,
                                        ALU.add)
                        vdum(e)
                        e.reciprocal(denr[:], dencol[:])
                        vdum(e)
                        return e.tensor_scalar(dmy[:, 0:32], dmy[:, 128:160],
                                               1.0, None, ALU.mult)
                    if dsub < 4.5:
                        l_dr = None
                    else:
                        vop(mk_dr, waits=[omtm_lms[-1]], defs=[l_dr])
                    l_om = land("om")

                    def mk_om(e, tc=tc):
                        pass

                    def mk_om(e, tc=tc):
                        last = None
                        for h in range(16):
                            pn = p_n0 if h < 8 else p_n1
                            hs = h % 8
                            last = e.tensor_scalar(
                                omtm[:, tc * D + h * 64: tc * D + (h + 1) * 64],
                                pn[:, hs * 64:(hs + 1) * 64],
                                denr[:, h:h + 1], None, ALU.mult)
                        return last
                    if dsub >= 4.5:
                        vop(mk_om, waits=[l_dr], defs=[l_om])
                        prev_om[0] = l_om
                        om_lms.append((tc, l_om))
                l_su = land("su")
                mml3 = []
                for h in range(16):
                    pb = 64 * (h % 2)
                    mml3.append(lambda e, h=h, pb=pb, tc=tc, c_glob=c_glob:
                                e.matmul(
                        p_S[pb:pb + 64, (h // 2) * 64:(h // 2) * 64 + 64],
                        kmtm[:, tc * D + h * 64: tc * D + (h + 1) * 64],
                        vtm[:, tc * D + h * 64: tc * D + (h + 1) * 64],
                        start=(c_glob == 0), stop=(c_glob == NCH - 1),
                        tile_position=(0, pb)))
                    mml3.append(lambda e, h=h, pb=pb, tc=tc, c_glob=c_glob:
                                e.matmul(
                        p_ks[pb:pb + 64, (h // 2):(h // 2) + 1],
                        kmtm[:, tc * D + h * 64: tc * D + (h + 1) * 64],
                        onescol[:],
                        start=True, stop=True,
                        skip_group_check=True, tile_position=(0, pb)))
                suw = [l_sa, kv_lms[tc * 2 + 1][1], kv_lms[tc * 2 + 1][0],
                       l_ones]
                tmms(mml3, waits=suw, defs=[l_su])
                last_supd[0] = l_su
                l_ksa = land("ksa")
                vop(lambda e: e.tensor_tensor(ksum_acc[:], ksum_acc[:], p_ks,
                                              ALU.add),
                    waits=[l_su], defs=[l_ksa])
                last_ksa[0] = l_ksa
            if full and dsub >= 5:
                ot_lms = []
                for tci, l_omx in om_lms:
                    l_ot = land("ot")
                    otw = [l_omx]
                    if prev_wo[0] and not ot_lms:
                        otw.append(prev_wo[0])
                    dmatp(omtm[:, tci * D:(tci + 1) * D],
                          omfm[:].rearrange("p (oc t) -> p oc t", t=STW)
                          [:, :, tci * CH:(tci + 1) * CH],
                          waits=otw, defs=[l_ot])
                    ot_lms.append(l_ot)
                to = st - NST // 2
                for oc in range(8):
                    l_wo = land("wo")
                    mml4 = []
                    pw = p_w0 if oc % 2 == 0 else p_w1
                    for mc in range(8):
                        mml4.append(lambda e, mc=mc, oc=oc, pw=pw: e.matmul(
                            pw[:, 0:512],
                            wab["o"][:, mc * D + oc * 128: mc * D + (oc + 1) * 128],
                            omfm[:].rearrange("p (k t) -> p k t", t=STW)[:, mc, :],
                            start=(mc == 0), stop=(mc == 7)))
                    wow = ot_lms + [l_wcast["o"], l_axr_all]
                    if out_done:
                        wow.append(out_done[-1])
                    tmms(mml4, waits=wow, defs=[l_wo])
                    prev_wo[0] = l_wo
                    l_ax = land("ax")
                    axw = [last_fe[0]] if last_fe[0] else [l_axr_all]
                    dma(sc_axr[oc * 128:(oc + 1) * 128, to * STW:(to + 1) * STW],
                        axr_d[:], waits=axw, defs=[l_ax])
                    l_fe = land("fe")
                    few = [l_wo, l_ax] + ([out_done[-1]] if out_done else [])
                    vop(lambda e, pw=pw: e.tensor_tensor(
                        aout_d[:], pw[:, 0:512], axr_d[:], ALU.add),
                        waits=few, defs=[l_fe])
                    last_fe[0] = l_fe
                    l_fo = land("fo")
                    dma(aout_d[:], out_fm[oc * 128:(oc + 1) * 128,
                                          to * STW:(to + 1) * STW],
                        waits=[l_fe], defs=[l_fo])
                    out_done.append(l_fo)
            st_lm[st % 2] = last_supd[0]

        if dbg:
            dma(sc_xfm, dbg_out["d_xfm"], waits=[("ALL", "DM")])
            dma(sc_axr, dbg_out["d_axr"])
            dma(sc_wgt, dbg_out["d_wgt"])
            pass
            dma(xqfm[:], dbg_out["d_xq"])
            dma(hqfm, dbg_out["d_hq2"])
            dma(sc_wut, dbg_out["d_wut"])
            dma(sc_wdt, dbg_out["d_wdt"])

        fw = ([out_done[-1]] if out_done else []) + [("ALL", "DM"), ("ALL", "TP")]
        pl.op("sync", None, incs=[], waits=fw)

        pl.resolve({k: None for k in SEMS})
        return _emit_all(pl, block, sem, nc, dmy)


_NC_CACHE = None


def kernel(L0_attn_x, L0_attn_Wq, L0_attn_Wk, L0_attn_Wv, L0_attn_Wo,
           L0_ffn_w_gate, L0_ffn_w_up, L0_ffn_w_down):
    global _NC_CACHE
    x = np.ascontiguousarray(np.asarray(L0_attn_x, dtype=np.float32))
    wq = np.ascontiguousarray(np.asarray(L0_attn_Wq, dtype=np.float32).T)
    wk = np.ascontiguousarray(np.asarray(L0_attn_Wk, dtype=np.float32).T)
    wv = np.ascontiguousarray(np.asarray(L0_attn_Wv, dtype=np.float32).T)
    wo = np.ascontiguousarray(np.asarray(L0_attn_Wo, dtype=np.float32).T)
    wg = np.ascontiguousarray(np.asarray(L0_ffn_w_gate, dtype=np.float32))
    wu = np.ascontiguousarray(np.asarray(L0_ffn_w_up, dtype=np.float32))
    wd = np.ascontiguousarray(np.asarray(L0_ffn_w_down, dtype=np.float32))

    in_maps = []
    for c in range(NC):
        b, hf = c // 2, c % 2
        if hf == 0:
            prefix = np.zeros((TH, D), dtype=np.float32)
            km = np.concatenate([np.zeros(TH, np.float32),
                                 np.ones(TH, np.float32)])
        else:
            prefix = x[b, 0:TH]
            km = np.ones(S, np.float32)
        km = np.ascontiguousarray(km.reshape(NCH, CH).T)
        own = x[b, hf * TH:(hf + 1) * TH]
        xs = np.ascontiguousarray(np.concatenate([prefix, own], axis=0))
        xfh = np.ascontiguousarray(own.T)
        in_maps.append({
            "xs_tm": xs, "xs_fmh": xfh, "kmask": km,
            "wqt": wq, "wkt": wk, "wvt": wv, "wot": wo,
            "wg": wg, "wu": wu, "wd": wd,
        })

    if _NC_CACHE is None:
        _NC_CACHE = build_program()
    res = run_bass_kernel_spmd(_NC_CACHE, in_maps, core_ids=list(range(NC)))
    out = np.empty((B, S, D), dtype=np.float32)
    for c in range(NC):
        b, hf = c // 2, c % 2
        out[b, hf * TH:(hf + 1) * TH] = res.results[c]["out_fm"].T
    return out
